# version 01
# baseline (speedup 1.0000x reference)
import numpy as np
import ml_dtypes
from contextlib import ExitStack

import concourse.bass as bass
import concourse.mybir as mybir
from concourse.bass_utils import run_bass_kernel_spmd

F32 = mybir.dt.float32
BF16 = mybir.dt.bfloat16
AF = mybir.ActivationFunctionType
OP = mybir.AluOpType
BF = ml_dtypes.bfloat16

NCORES = 8
T = 2048
D = 1024
NT = 16
NEG = -1.0e30
X = mybir.AxisListType.X


def host_constants():
    c = {}
    c["c_ident_bf"] = np.eye(128, dtype=np.float32).astype(BF)
    c["c_ones_row"] = np.ones((1, 128), dtype=BF)
    half = 32
    ang = (np.arange(T, dtype=np.float32)[:, None] *
           (1.0 / (10000.0 ** (np.arange(half, dtype=np.float32) / np.float32(half)))
            ).astype(np.float32)[None, :])
    c["c_rope_sin"] = np.sin(ang).astype(np.float32)
    c["c_rope_cos"] = np.cos(ang).astype(np.float32)
    a = np.arange(128)
    c["c_causal128"] = np.where(a[:, None] >= a[None, :], 0.0, NEG).astype(np.float32).astype(BF)
    return c


class Sem:
    def __init__(self, prog, name):
        self.h = prog.ctx.enter_context(prog.nc.semaphore(name))
        self.n = 0

    def inc(self, inst, k=1):
        inst.then_inc(self.h, k)
        self.n += k
        return self.n

    def wait(self, eng, val=None):
        eng.wait_ge(self.h, self.n if val is None else val)


class Prog:

    def __init__(self):
        self.nc = bass.Bass("TRN2", target_bir_lowering=False)
        self.ctx = ExitStack()
        self.inputs = {}
        self.outputs = {}
        self._sb = {}

    def din(self, name, shape, dtype=F32):
        if name not in self.inputs:
            self.inputs[name] = self.nc.dram_tensor(name, list(shape), dtype,
                                                    kind="ExternalInput").ap()
        return self.inputs[name]

    def dout(self, name, shape, dtype=F32):
        if name not in self.outputs:
            self.outputs[name] = self.nc.dram_tensor(name, list(shape), dtype,
                                                     kind="ExternalOutput").ap()
        return self.outputs[name]

    def dint(self, name, shape, dtype=F32, addr_space="Local"):
        key = "int_" + name
        if key not in self._sb:
            self._sb[key] = self.nc.dram_tensor(name, list(shape), dtype,
                                                kind="Internal", addr_space=addr_space).ap()
        return self._sb[key]

    def sb(self, name, shape, dtype):
        if name not in self._sb:
            self._sb[name] = self.ctx.enter_context(
                self.nc.sbuf_tensor(name, list(shape), dtype))
        return self._sb[name]

    def ps(self, name, shape, dtype=F32):
        if name not in self._sb:
            self._sb[name] = self.ctx.enter_context(
                self.nc.psum_tensor(name, list(shape), dtype))
        return self._sb[name]

    def setup(self):
        self.block = self.ctx.enter_context(self.nc.Block())
        self.T = Sem(self, "Tl")
        self.dW = Sem(self, "dW")
        self.sMM = Sem(self, "sMM")
        self.sEV = Sem(self, "sEV")
        self.sAC = Sem(self, "sAC")
        self.sGP = Sem(self, "sGP")
        self.sCC = Sem(self, "sCC")
        self.xres = self.sb("xres", [128, NT * D], F32)
        self.A = self.sb("arenaA", [128, 32768], BF16)
        self.B = self.sb("arenaB", [128, 16384], BF16)
        self.E = self.sb("arenaE", [128, 4096], F32)
        self.ST = self.sb("stats", [128, 512], F32)
        self.W = self.sb("wstream", [128, 8192], BF16)
        self.Rb = self.sb("rowbuf", [1, 2048], BF16)
        self.ident = self.sb("identbf", [128, 128], BF16)
        self.onesb = self.sb("onesrow", [1, 128], BF16)
        self.CS = self.sb("cossin", [128, 1024], F32)
        self.psS = self.ps("psS", [128, 2048], F32)
        self.psA = self.ps("psA", [128, 512], F32)
        self.psB = self.ps("psB", [128, 512], F32)
        self.psC = self.ps("psC", [128, 512], F32)
        self.psT = self.ps("psT", [128, 1024], BF16)

    def wait_all(self, eng):
        eng.wait_ge(self.T.h, self.T.n)

    def dinc(self, eng, sem, k=1):
        sem.inc(eng.drain(), k)

    def section(self, engname):
        def deco(fn):
            @getattr(self.block, engname)
            def _(eng):
                self.wait_all(eng)
                last = fn(eng)
                if last is not None:
                    if engname in ("sync", "gpsimd"):
                        self.T.inc(last, 16)
                    elif engname == "tensor":
                        self.T.inc(last, 1)
                    else:
                        self.dinc(eng, self.T)
        return deco

    def raw_section(self, engname):
        def deco(fn):
            @getattr(self.block, engname)
            def _(eng):
                fn(eng)
        return deco


def emit_load_consts(p):
    ib = p.din("c_ident_bf", [128, 128], BF16)
    orow = p.din("c_ones_row", [1, 128], BF16)

    @p.section("sync")
    def _(s):
        i = s.dma_start(out=p.ident[:], in_=ib[:])
        p.T.inc(i, 16)
        return s.dma_start(out=p.onesb[:], in_=orow[:])


def emit_load_x(p, src_name="xin"):
    xin = p.din(src_name, [T, D], F32)

    @p.section("sync")
    def _(s):
        return s.dma_start(out=p.xres[:].rearrange("p (t d) -> p t d", d=D),
                           in_=xin.rearrange("(t p) d -> p t d", p=128))


def emit_store_x(p, dst_name="xout"):
    xout = p.dout(dst_name, [T, D], F32)

    @p.raw_section("gpsimd")
    def _(g):
        p.wait_all(g)
        i = g.dma_start(out=xout.rearrange("(t p) d -> p t d", p=128),
                        in_=p.xres[:].rearrange("p (t d) -> p t d", d=D))
        p.T.inc(i, 16)
        g.wait_ge(p.T.h, p.T.n)


def emit_ln(p, dst, dst_off, width=D, src=None, src_off=0, eps=1e-5,
            ntiles=NT, src_stride=None, dst_stride=None, stat_col=0,
            scratch=None, affine=None):
    src = src if src is not None else p.xres
    src_stride = src_stride if src_stride is not None else width
    dst_stride = dst_stride if dst_stride is not None else width
    scratch = scratch if scratch is not None else p.E[:, 0:width]
    mean = p.ST[:, stat_col:stat_col + ntiles]
    rstd = p.ST[:, stat_col + ntiles:stat_col + 2 * ntiles]
    sumsq = p.ST[:, stat_col + 2 * ntiles:stat_col + 3 * ntiles]

    @p.section("vector")
    def _(v):
        for ti in range(ntiles):
            st = src[:, src_off + ti * src_stride: src_off + ti * src_stride + width]
            v.tensor_reduce(out=mean[:, ti:ti + 1], in_=st, axis=X, op=OP.add)
            v.tensor_mul(scratch, st, st)
            v.tensor_reduce(out=sumsq[:, ti:ti + 1], in_=scratch, axis=X, op=OP.add)
        v.tensor_scalar_mul(mean[:], mean[:], 1.0 / width)
        v.tensor_scalar_mul(sumsq[:], sumsq[:], 1.0 / width)
        v.drain()
        v.tensor_tensor(out=rstd[:], in0=mean[:], in1=mean[:], op=OP.mult)
        v.drain()
        v.tensor_sub(rstd[:], sumsq[:], rstd[:])
        v.drain()
        return v.tensor_scalar_add(rstd[:], rstd[:], float(eps))

    @p.section("scalar")
    def _(s):
        return s.sqrt(rstd[:], rstd[:])

    @p.section("vector")
    def _(v):
        v.reciprocal(rstd[:], rstd[:])
        v.drain()
        last = None
        for ti in range(ntiles):
            st = src[:, src_off + ti * src_stride: src_off + ti * src_stride + width]
            dt_ = dst[:, dst_off + ti * dst_stride: dst_off + ti * dst_stride + width]
            if affine is None:
                last = v.tensor_scalar(out=dt_, in0=st, scalar1=mean[:, ti:ti + 1],
                                       scalar2=rstd[:, ti:ti + 1],
                                       op0=OP.subtract, op1=OP.mult)
            else:
                g_ap, b_ap = affine
                v.tensor_scalar(out=scratch, in0=st, scalar1=mean[:, ti:ti + 1],
                                scalar2=rstd[:, ti:ti + 1],
                                op0=OP.subtract, op1=OP.mult)
                v.tensor_mul(scratch, scratch, g_ap)
                last = v.tensor_tensor(out=dt_, in0=scratch, in1=b_ap, op=OP.add)
        return last


def emit_transpose(p, dst, dst_off, src, src_off, ntiles=NT, width=D,
                   src_stride=None, dst_stride=None):
    src_stride = src_stride if src_stride is not None else width
    dst_stride = dst_stride if dst_stride is not None else ntiles * 128
    nch = width // 128
    groups = []
    for c in range(nch):
        for g0 in range(0, ntiles, 8):
            groups.append((c, g0, min(8, ntiles - g0)))

    @p.raw_section("tensor")
    def _(t):
        p.wait_all(t)
        ev0 = p.sEV.n
        for gi, (c, g0, gl) in enumerate(groups):
            if gi >= 1:
                t.wait_ge(p.sEV.h, ev0 + gi)
            last = None
            for k in range(gl):
                ti = g0 + k
                sc = src_off + ti * src_stride + c * 128
                last = t.transpose(p.psT[:, k * 128:(k + 1) * 128],
                                   src[:, sc:sc + 128], p.ident[:])
            p.sMM.inc(last)

    @p.raw_section("vector")
    def _(v):
        base_mm = p.sMM.n - len(groups)
        last = None
        for gi, (c, g0, gl) in enumerate(groups):
            v.wait_ge(p.sMM.h, base_mm + gi + 1)
            dc = dst_off + c * dst_stride + g0 * 128
            last = v.tensor_copy(dst[:, dc:dc + gl * 128], p.psT[:, 0:gl * 128])
            p.sEV.inc(last)
        p.T.inc(v.drain())


def emit_weight_load(p, w_dram, dst, dst_off, n_in, n_out, in_off=0, out_off=0):
    @p.raw_section("gpsimd")
    def _(g):
        g.wait_ge(p.T.h, p.T.n)
        src = w_dram[in_off:in_off + n_in, out_off:out_off + n_out]
        i = g.dma_start(out=dst[:, dst_off:dst_off + (n_in // 128) * n_out]
                        .rearrange("p (c o) -> p c o", o=n_out),
                        in_=src.rearrange("(c p) o -> p c o", p=128))
        p.dW.inc(i, 16)


def emit_row_load(p, dram_vec, dst_off, n, src_off=0):
    @p.raw_section("gpsimd")
    def _(g):
        g.wait_ge(p.T.h, p.T.n)
        i = g.dma_start(out=p.Rb[0:1, dst_off:dst_off + n],
                        in_=dram_vec[src_off:src_off + n].unsqueeze(0))
        p.dW.inc(i, 16)


def emit_proj_tm(p, dst, dst_off, src_fm, src_off, w_sb, w_off, n_in, n_out,
                 bias_off=None, act=None, scale=1.0, ntiles=NT,
                 src_stride=None, dst_stride=None, wait_dw=None, evac="scalar"):
    src_stride = src_stride if src_stride is not None else ntiles * 128
    dst_stride = dst_stride if dst_stride is not None else n_out
    nkc = n_in // 128
    nog = (n_out + 511) // 512
    banks = [p.psA, p.psB]
    njobs = ntiles * nog
    sem = p.sEV if evac == "vector" else p.sAC

    @p.raw_section("tensor")
    def _(t):
        p.wait_all(t)
        if wait_dw is not None:
            t.wait_ge(p.dW.h, wait_dw)
        ev0 = sem.n
        job = 0
        for ti in range(ntiles):
            for og in range(nog):
                ow = min(512, n_out - og * 512)
                ps = banks[job % 2]
                if job >= 2:
                    t.wait_ge(sem.h, ev0 + job - 1)
                last = None
                for kc in range(nkc):
                    last = t.matmul(ps[:, 0:ow],
                                    src_fm[:, src_off + kc * src_stride + ti * 128:
                                           src_off + kc * src_stride + ti * 128 + 128],
                                    w_sb[:, w_off + kc * n_out + og * 512:
                                         w_off + kc * n_out + og * 512 + ow],
                                    start=(kc == 0),
                                    stop=(kc == nkc - 1 and bias_off is None))
                if bias_off is not None:
                    last = t.matmul(ps[:, 0:ow], p.onesb[:],
                                    p.Rb[0:1, bias_off + og * 512:bias_off + og * 512 + ow],
                                    start=False, stop=True)
                p.sMM.inc(last)
                job += 1

    engname = "vector" if evac == "vector" else "scalar"

    @p.raw_section(engname)
    def _(e):
        base_mm = p.sMM.n - njobs
        job = 0
        last = None
        for ti in range(ntiles):
            for og in range(nog):
                ow = min(512, n_out - og * 512)
                ps = banks[job % 2]
                e.wait_ge(p.sMM.h, base_mm + job + 1)
                dcol = dst_off + ti * dst_stride + og * 512
                if evac == "vector":
                    if scale == 1.0:
                        last = e.tensor_copy(dst[:, dcol:dcol + ow], ps[:, 0:ow])
                    else:
                        last = e.tensor_scalar_mul(dst[:, dcol:dcol + ow], ps[:, 0:ow], scale)
                else:
                    last = e.activation(out=dst[:, dcol:dcol + ow], in_=ps[:, 0:ow],
                                        func=(act or AF.Copy), scale=scale)
                sem.inc(last)
                job += 1
        p.T.inc(e.drain())


def emit_proj_residual(p, src_fm, w, n_in, n_out, wait_dw, bias_off=None, ntiles=NT,
                       dst=None):
    sbuf, soff = src_fm
    wbuf, woff = w
    if dst is None:
        dbuf, doff, dstride = p.xres, 0, D
    else:
        dbuf, doff, dstride = dst
    nkc = n_in // 128
    banks = [p.psA, p.psB]
    nog = n_out // 512
    njobs = ntiles * nog

    @p.raw_section("tensor")
    def _(t):
        p.wait_all(t)
        t.wait_ge(p.dW.h, wait_dw)
        ev0 = p.sEV.n
        job = 0
        for ti in range(ntiles):
            for og in range(nog):
                ps = banks[job % 2]
                if job >= 2:
                    t.wait_ge(p.sEV.h, ev0 + job - 1)
                last = None
                for kc in range(nkc):
                    last = t.matmul(ps[:], sbuf[:, soff + kc * (ntiles * 128) + ti * 128:
                                                soff + kc * (ntiles * 128) + ti * 128 + 128],
                                    wbuf[:, woff + kc * n_out + og * 512:
                                         woff + kc * n_out + og * 512 + 512],
                                    start=(kc == 0),
                                    stop=(kc == nkc - 1 and bias_off is None))
                if bias_off is not None:
                    last = t.matmul(ps[:], p.onesb[:],
                                    p.Rb[0:1, bias_off + og * 512:bias_off + og * 512 + 512],
                                    start=False, stop=True)
                p.sMM.inc(last)
                job += 1

    @p.raw_section("vector")
    def _(v):
        base = p.sMM.n - njobs
        job = 0
        last = None
        for ti in range(ntiles):
            for og in range(nog):
                ps = banks[job % 2]
                v.wait_ge(p.sMM.h, base + job + 1)
                xs = dbuf[:, doff + ti * dstride + og * 512: doff + ti * dstride + og * 512 + 512]
                last = v.tensor_add(xs, xs, ps[:])
                p.sEV.inc(last)
                job += 1
        p.T.inc(v.drain())


def emit_rope_quarter(p, src, src_off, dst, dst_off, width=256):
    scr = p.ST[:, 112:368]
    nh = width // 64

    @p.section("vector")
    def _(v):
        last = None
        for ti in range(NT):
            q = src[:, src_off + ti * width: src_off + (ti + 1) * width]
            qh = q.rearrange("p (h d) -> p h d", d=64)
            q1 = qh[:, :, 0:32]
            q2 = qh[:, :, 32:64]
            o = dst[:, dst_off + ti * width: dst_off + (ti + 1) * width]
            oh = o.rearrange("p (h d) -> p h d", d=64)
            o1 = oh[:, :, 0:32]
            o2 = oh[:, :, 32:64]
            cos = p.CS[:, ti * 32:(ti + 1) * 32].unsqueeze(1).broadcast_to([128, nh, 32])
            sin = p.CS[:, 512 + ti * 32:512 + (ti + 1) * 32].unsqueeze(1).broadcast_to([128, nh, 32])
            t1 = scr[:, 0:nh * 32].rearrange("p (h d) -> p h d", d=32)
            t2 = scr[:, 128:128 + nh * 32].rearrange("p (h d) -> p h d", d=32)
            v.tensor_tensor(out=t1, in0=q1, in1=cos, op=OP.mult)
            v.tensor_tensor(out=t2, in0=q2, in1=sin, op=OP.mult)
            v.tensor_tensor(out=o1, in0=t1, in1=t2, op=OP.subtract)
            v.tensor_tensor(out=t1, in0=q1, in1=sin, op=OP.mult)
            v.tensor_tensor(out=t2, in0=q2, in1=cos, op=OP.mult)
            last = v.tensor_tensor(out=o2, in0=t1, in1=t2, op=OP.add)
        return last


def emit_dump(p, name, buf, off, cols, dtype=BF16):
    d = p.dout(name, [128, cols], dtype)

    @p.raw_section("gpsimd")
    def _(g):
        p.wait_all(g)
        i = g.dma_start(out=d[:], in_=buf[:, off:off + cols])
        p.T.inc(i, 16)

def emit_l0_attn(p):
    Wq = p.din("L0_attn_W_q", [D, D])
    Wdkv = p.din("L0_attn_W_dkv", [D, 256])
    Wuk = p.din("L0_attn_W_uk", [256, D])
    Wuv = p.din("L0_attn_W_uv", [256, D])
    Wo = p.din("L0_attn_W_o", [D, D])
    lng = p.din("L0_attn_ln_gamma", [256])
    lnb = p.din("L0_attn_ln_beta", [256])
    cosd = p.din("c_rope_cos", [T, 32])
    sind = p.din("c_rope_sin", [T, 32])
    causd = p.din("c_causal128", [128, 128], BF16)

    XN = 16384
    COS, SIN, CAUS = 3072, 3584, 4096
    GAM = 4352
    emit_ln(p, dst=p.A, dst_off=0)
    emit_transpose(p, dst=p.A, dst_off=XN, src=p.A, src_off=0)

    emit_weight_load(p, Wdkv, p.W, 0, D, 256)
    emit_proj_tm(p, dst=p.E, dst_off=0, src_fm=p.A, src_off=XN, w_sb=p.W, w_off=0,
                 n_in=D, n_out=256, wait_dw=p.dW.n, evac="vector", dst_stride=256)

    mean = p.ST[:, 64:80]
    rstd = p.ST[:, 80:96]
    sumsq = p.ST[:, 96:112]
    scr = p.ST[:, 112:368]

    @p.section("vector")
    def _(v):
        for ti in range(NT):
            st = p.E[:, ti * 256:(ti + 1) * 256]
            v.tensor_reduce(out=mean[:, ti:ti + 1], in_=st, axis=X, op=OP.add)
            v.tensor_mul(scr, st, st)
            v.tensor_reduce(out=sumsq[:, ti:ti + 1], in_=scr, axis=X, op=OP.add)
        v.tensor_scalar_mul(mean[:], mean[:], 1.0 / 256)
        v.tensor_scalar_mul(sumsq[:], sumsq[:], 1.0 / 256)
        v.drain()
        v.tensor_tensor(out=rstd[:], in0=mean[:], in1=mean[:], op=OP.mult)
        v.drain()
        v.tensor_sub(rstd[:], sumsq[:], rstd[:])
        v.drain()
        return v.tensor_scalar_add(rstd[:], rstd[:], 1e-5)

    @p.section("scalar")
    def _(s):
        return s.sqrt(rstd[:], rstd[:])

    @p.section("vector")
    def _(v):
        v.reciprocal(rstd[:], rstd[:])
        v.drain()
        last = None
        for ti in range(NT):
            last = v.tensor_scalar(out=p.B[:, ti * 256:(ti + 1) * 256],
                                   in0=p.E[:, ti * 256:(ti + 1) * 256],
                                   scalar1=mean[:, ti:ti + 1], scalar2=rstd[:, ti:ti + 1],
                                   op0=OP.subtract, op1=OP.mult)
        return last

    emit_transpose(p, dst=p.B, dst_off=4096, src=p.B, src_off=0, width=256)
    LATF = 4096

    gam = p.ST[:, 374:376]

    BET = 4224

    @p.raw_section("gpsimd")
    def _(g):
        g.wait_ge(p.T.h, p.T.n)
        i = g.dma_start(out=gam, in_=lng.rearrange("(c p) -> p c", p=128),
                        allow_slow_non_contiguous=True)
        p.dW.inc(i, 16)
        i = g.dma_start(out=p.W[:, BET:BET + 2], in_=lnb.rearrange("(c p) -> p c", p=128),
                        allow_slow_non_contiguous=True)
        p.dW.inc(i, 16)
        i = g.dma_start(out=p.CS[:, 0:512].rearrange("p (t f) -> p t f", f=32),
                        in_=cosd.rearrange("(t p) f -> p t f", p=128))
        p.dW.inc(i, 16)
        i = g.dma_start(out=p.CS[:, 512:1024].rearrange("p (t f) -> p t f", f=32),
                        in_=sind.rearrange("(t p) f -> p t f", p=128))
        p.dW.inc(i, 16)
        i = g.dma_start(out=p.W[:, CAUS:CAUS + 128], in_=causd[:])
        p.dW.inc(i, 16)

    bet = p.W[:, BET:BET + 2]

    for qq in range(4):
        hsl = qq * 256
        emit_weight_load(p, Wq, p.W, 0, D, 256, out_off=hsl)
        emit_weight_load(p, Wuk, p.W, 2048, 256, 256, out_off=hsl)
        emit_weight_load(p, Wuv, p.W, 2560, 256, 256, out_off=hsl)
        dw_after = p.dW.n

        @p.raw_section("tensor")
        def _(t, dw_after=dw_after):
            t.wait_ge(p.dW.h, dw_after)
            for j, woff in enumerate((2048, 2560)):
                last = None
                for c in range(2):
                    last = t.matmul(p.psC[0:1, j * 256:(j + 1) * 256],
                                    bet[:, c:c + 1],
                                    p.W[:, woff + c * 256:woff + (c + 1) * 256],
                                    start=(c == 0), stop=(c == 1))
                p.sMM.inc(last)

        @p.raw_section("vector")
        def _(v):
            v.wait_ge(p.sMM.h, p.sMM.n)
            v.tensor_copy(p.Rb[0:1, 0:512], p.psC[0:1, 0:512])
            for woff in (2048, 2560):
                for c in range(2):
                    v.tensor_scalar_mul(p.W[:, woff + c * 256:woff + (c + 1) * 256],
                                        p.W[:, woff + c * 256:woff + (c + 1) * 256],
                                        gam[:, c:c + 1])
            p.dinc(v, p.T)

        emit_proj_tm(p, dst=p.E, dst_off=0, src_fm=p.A, src_off=XN, w_sb=p.W, w_off=0,
                     n_in=D, n_out=256, scale=0.125, wait_dw=dw_after,
                     dst_stride=256, evac="vector")
        emit_rope_quarter(p, src=p.E, src_off=0, dst=p.A, dst_off=0)
        emit_transpose(p, dst=p.A, dst_off=4096, src=p.A, src_off=0, width=256)
        emit_proj_tm(p, dst=p.E, dst_off=0, src_fm=p.B, src_off=LATF, w_sb=p.W,
                     w_off=2048, n_in=256, n_out=256, bias_off=0, dst_stride=256,
                     evac="vector")
        emit_rope_quarter(p, src=p.E, src_off=0, dst=p.A, dst_off=8192)
        emit_transpose(p, dst=p.A, dst_off=12288, src=p.A, src_off=8192, width=256)
        emit_proj_tm(p, dst=p.B, dst_off=8192, src_fm=p.B, src_off=LATF, w_sb=p.W,
                     w_off=2560, n_in=256, n_out=256, bias_off=256, dst_stride=256)
        emit_causal_attn_quarter(p, qfm=(p.A, 4096), kfm=(p.A, 12288),
                                 vtm=(p.B, 8192), dst=(p.B, 0),
                                 caus=(p.W, CAUS))
        emit_transpose(p, dst=p.B, dst_off=12288, src=p.B, src_off=0, width=256)
        emit_weight_load(p, Wo, p.W, 4608, 256, D, in_off=256 * qq)
        emit_proj_residual(p, src_fm=(p.B, 12288), w=(p.W, 4608), n_in=256, n_out=D,
                           wait_dw=p.dW.n)


def emit_causal_attn_quarter(p, qfm, kfm, vtm, dst, caus):
    qbuf, qoff = qfm
    kbuf, koff = kfm
    vbuf, voff = vtm
    dbuf, doff = dst
    cbuf, coff = caus
    P_off = 12288
    PT_off = 14336
    nmax = p.ST[:, 368:369]
    sums = p.ST[:, 369:373]
    rinv = p.ST[:, 373:374]

    for h in range(4):
        hd = h * 64
        ch = hd // 128
        hin = hd % 128
        for qi in range(NT):
            nk = (qi + 1) * 128
            nb = (nk + 511) // 512

            @p.raw_section("tensor")
            def _(t, qi=qi, nk=nk, nb=nb, ch=ch, hin=hin):
                p.wait_all(t)
                for b in range(nb):
                    kw = min(512, nk - b * 512)
                    t.matmul(p.psS[:, b * 512:b * 512 + kw],
                             qbuf[hin:hin + 64, qoff + ch * 2048 + qi * 128:
                                  qoff + ch * 2048 + qi * 128 + 128],
                             kbuf[hin:hin + 64, koff + ch * 2048 + b * 512:
                                  koff + ch * 2048 + b * 512 + kw],
                             start=True, stop=(b != nb - 1))
                last = t.matmul(p.psS[:, nk - 128:nk], p.ident[:],
                                cbuf[:, coff:coff + 128], start=False, stop=True)
                p.sMM.inc(last)

            @p.raw_section("vector")
            def _(v, nk=nk):
                v.wait_ge(p.sMM.h, p.sMM.n)
                v.reduce_max(out=nmax, in_=p.psS[:, 0:nk], axis=X, negate=True)
                p.dinc(v, p.sEV)

            @p.raw_section("scalar")
            def _(s, nk=nk, nb=nb):
                s.wait_ge(p.sEV.h, p.sEV.n)
                last = None
                for b in range(nb):
                    kw = min(512, nk - b * 512)
                    last = s.activation(out=p.B[:, P_off + b * 512:P_off + b * 512 + kw],
                                        in_=p.psS[:, b * 512:b * 512 + kw], func=AF.Exp,
                                        bias=nmax, accum_out=sums[:, b:b + 1])
                p.dinc(s, p.sAC)

            @p.raw_section("vector")
            def _(v, nb=nb):
                v.wait_ge(p.sAC.h, p.sAC.n)
                v.tensor_reduce(out=rinv, in_=sums[:, 0:nb], axis=X, op=OP.add)
                v.drain()
                v.reciprocal(rinv, rinv)
                p.dinc(v, p.sEV)

            nkb = qi + 1
            ngr = (nkb + 7) // 8

            @p.raw_section("tensor")
            def _(t, nkb=nkb, ngr=ngr):
                t.wait_ge(p.sAC.h, p.sAC.n)
                ev0 = p.sEV.n
                for gi, g0 in enumerate(range(0, nkb, 8)):
                    gl = min(8, nkb - g0)
                    if gi > 0:
                        t.wait_ge(p.sEV.h, ev0 + gi)
                    last = None
                    for k in range(gl):
                        last = t.transpose(p.psT[:, k * 128:(k + 1) * 128],
                                           p.B[:, P_off + (g0 + k) * 128:
                                               P_off + (g0 + k) * 128 + 128],
                                           p.ident[:])
                    p.sMM.inc(last)

            @p.raw_section("vector")
            def _(v, nkb=nkb, ngr=ngr):
                base = p.sMM.n - ngr
                for gi, g0 in enumerate(range(0, nkb, 8)):
                    gl = min(8, nkb - g0)
                    v.wait_ge(p.sMM.h, base + gi + 1)
                    v.tensor_copy(p.B[:, PT_off + g0 * 128:PT_off + g0 * 128 + gl * 128],
                                  p.psT[:, 0:gl * 128])
                    p.dinc(v, p.sEV)

            @p.raw_section("tensor")
            def _(t, nkb=nkb, hd=hd):
                t.wait_ge(p.sEV.h, p.sEV.n)
                last = None
                for kb in range(nkb):
                    last = t.matmul(p.psA[:, 0:64],
                                    p.B[:, PT_off + kb * 128:PT_off + kb * 128 + 128],
                                    vbuf[:, voff + kb * 256 + hd:voff + kb * 256 + hd + 64],
                                    start=(kb == 0), stop=(kb == nkb - 1))
                p.sMM.inc(last)

            @p.raw_section("vector")
            def _(v, qi=qi, hd=hd):
                v.wait_ge(p.sMM.h, p.sMM.n)
                v.tensor_scalar_mul(dbuf[:, doff + qi * 256 + hd:doff + qi * 256 + hd + 64],
                                    p.psA[:, 0:64], rinv)
                v.drain()
                p.dinc(v, p.sEV)
                p.T.inc(v.drain())


def emit_l0_ffn(p):
    rT = p.din("L0_ffn_router_wT", [D, 8])
    w1 = p.din("L0_ffn_w1", [8, D, 4096])
    b1 = p.din("L0_ffn_b1", [8, 4096])
    w2 = p.din("L0_ffn_w2", [8, 4096, D])
    b2 = p.din("L0_ffn_b2", [8, D])
    idf = p.din("c_ident_f32", [128, 128], F32)

    XN = 16384
    CAP = 513
    emit_ln(p, dst=p.A, dst_off=0)
    emit_transpose(p, dst=p.A, dst_off=XN, src=p.A, src_off=0)

    emit_weight_load(p, rT, p.W, 0, D, 8)
    @p.raw_section("gpsimd")
    def _(g):
        g.wait_ge(p.T.h, p.T.n)
        i = g.dma_start(out=p.CS[:, 0:128], in_=idf[:])
        p.dW.inc(i, 16)
    IDF = 0

    emit_proj_tm(p, dst=p.E, dst_off=0, src_fm=p.A, src_off=XN, w_sb=p.W, w_off=0,
                 n_in=D, n_out=8, wait_dw=p.dW.n, evac="vector", dst_stride=8)

    @p.section("vector")
    def _(v):
        lg = p.E[:, 0:128].rearrange("p (t e) -> p t e", e=8)
        m = p.ST[:, 128:144]
        v.tensor_reduce(out=m, in_=lg, axis=X, op=OP.max)
        v.drain()
        v.tensor_tensor(out=lg, in0=lg, in1=m.unsqueeze(2).broadcast_to([128, 16, 8]),
                        op=OP.subtract)
        return v.drain()

    @p.section("scalar")
    def _(s):
        return s.activation(out=p.E[:, 0:128], in_=p.E[:, 0:128], func=AF.Exp)

    @p.section("vector")
    def _(v):
        lg = p.E[:, 0:128].rearrange("p (t e) -> p t e", e=8)
        sm = p.ST[:, 144:160]
        v.tensor_reduce(out=sm, in_=lg, axis=X, op=OP.add)
        v.drain()
        v.reciprocal(sm, sm)
        v.drain()
        v.tensor_tensor(out=lg, in0=lg, in1=sm.unsqueeze(2).broadcast_to([128, 16, 8]),
                        op=OP.mult)
        return v.drain()

    @p.raw_section("tensor")
    def _(t):
        p.wait_all(t)
        ev0 = p.sEV.n
        for ti in range(NT):
            if ti >= 1:
                t.wait_ge(p.sEV.h, ev0 + ti)
            i = t.transpose(p.psC[0:8, 0:128],
                            p.E[:, ti * 8:(ti + 1) * 8], p.CS[:, IDF:IDF + 128])
            p.sMM.inc(i)

    @p.raw_section("vector")
    def _(v):
        base = p.sMM.n - NT
        for ti in range(NT):
            v.wait_ge(p.sMM.h, base + ti + 1)
            v.tensor_copy(p.E[0:8, 2048 + ti * 128:2048 + (ti + 1) * 128],
                          p.psC[0:8, 0:128])
            p.dinc(v, p.sEV)
        p.T.inc(v.drain())

    SCF = 2048

    a = p.ST[0:8, 160:161]
    bb = p.ST[0:8, 161:162]
    mid = p.ST[0:8, 162:163]
    cnt = p.ST[0:8, 163:164]
    sel = p.ST[0:8, 164:165]
    d1 = p.ST[0:8, 165:166]
    d2 = p.ST[0:8, 166:167]
    cmpb = p.B[0:8, 0:2048]

    @p.section("vector")
    def _(v):
        v.memset(a, 0.0)
        v.memset(bb, 1.0)
        v.drain()
        for it in range(40):
            v.tensor_tensor(out=mid, in0=a, in1=bb, op=OP.add)
            v.drain()
            v.tensor_scalar_mul(mid, mid, 0.5)
            v.drain()
            v.tensor_single_scalar(out=cmpb, in_=p.E[0:8, SCF:SCF + 2048], scalar=mid,
                                   op=OP.is_ge)
            v.tensor_reduce(out=cnt, in_=cmpb, axis=X, op=OP.add)
            v.drain()
            v.tensor_single_scalar(out=sel, in_=cnt, scalar=float(CAP) - 0.5,
                                   op=OP.is_ge)
            v.drain()
            v.tensor_sub(d1, mid, a)
            v.tensor_sub(d2, bb, mid)
            v.drain()
            v.scalar_tensor_tensor(out=a, in0=d1, scalar=sel, in1=a,
                                   op0=OP.mult, op1=OP.add)
            v.scalar_tensor_tensor(out=bb, in0=d2, scalar=sel, in1=mid,
                                   op0=OP.mult, op1=OP.add)
            v.drain()
        v.tensor_single_scalar(out=cmpb, in_=p.E[0:8, SCF:SCF + 2048], scalar=a,
                               op=OP.is_ge)
        v.drain()
        v.tensor_tensor(out=p.E[0:8, SCF:SCF + 2048], in0=p.E[0:8, SCF:SCF + 2048],
                        in1=cmpb, op=OP.mult)
        return v.drain()

    @p.raw_section("tensor")
    def _(t):
        p.wait_all(t)
        ev0 = p.sEV.n
        for ti in range(NT):
            if ti >= 1:
                t.wait_ge(p.sEV.h, ev0 + ti)
            i = t.transpose(p.psC[:, 0:8],
                            p.E[0:8, SCF + ti * 128:SCF + (ti + 1) * 128],
                            p.CS[0:8, IDF:IDF + 8])
            p.sMM.inc(i)

    @p.raw_section("vector")
    def _(v):
        base = p.sMM.n - NT
        for ti in range(NT):
            v.wait_ge(p.sMM.h, base + ti + 1)
            v.tensor_copy(p.ST[:, 256 + ti * 8:256 + (ti + 1) * 8], p.psC[:, 0:8])
            p.dinc(v, p.sEV)
        p.T.inc(v.drain())

    CW = 256

    for e in range(8):
        emit_moe_expert(p, e, w1, b1, w2, b2, xn_off=XN, cw_col=lambda ti, e=e: CW + ti * 8 + e,
                        act=AF.Gelu, add_xf=False)


def emit_moe_expert(p, e, w1, b1, w2, b2, xn_off, cw_col, act, add_xf,
                    h_scale=1.0, xn_buf=None, h_off=0, h_buf=None,
                    out_dst=None, premul=None):
    xn_buf = xn_buf if xn_buf is not None else p.A
    h_buf = h_buf if h_buf is not None else p.B
    w1e = w1[e] if e is not None else w1
    w2e = w2[e] if e is not None else w2
    b1e = b1[e] if e is not None else b1
    b2e = b2[e] if e is not None else b2
    qbf = (w1e.dtype == BF16)

    @p.raw_section("gpsimd")
    def _(g):
        g.wait_ge(p.T.h, p.T.n)
        i = g.dma_start(out=p.CS[:, 128:160], in_=b1e.rearrange("(c p) -> p c", p=128),
                        allow_slow_non_contiguous=True)
        p.dW.inc(i, 16)
        i = g.dma_start(out=p.Rb[0:1, 0:1024], in_=b2e.unsqueeze(0))
        p.dW.inc(i, 16)
    bias_dw = p.dW.n

    for tg in range(4):
        tgo = tg * 512
        if premul is not None:
            gcol, bcol = premul

            @p.raw_section("vector")
            def _(v, tg=tg):
                p.wait_all(v)
                for dc in range(8):
                    v.tensor_scalar(out=p.W[:, 4096 + dc * 512:4096 + dc * 512 + 512],
                                    in0=xn_buf[:, xn_off + dc * (NT * 128) + tgo:
                                               xn_off + dc * (NT * 128) + tgo + 512],
                                    scalar1=gcol(dc), scalar2=bcol(dc),
                                    op0=OP.mult, op1=OP.add)
                p.dinc(v, p.T)
        for hc in range(32):
            buf = (hc % 2) * 1024

            @p.raw_section("gpsimd")
            def _(g, hc=hc, buf=buf):
                if hc >= 2:
                    g.wait_ge(p.sAC.h, p.l0f_ac0 + hc - 1)
                else:
                    g.wait_ge(p.T.h, p.T.n)
                i = g.dma_start(out=p.W[:, buf:buf + 1024]
                                .rearrange("p (c m) -> p c m", m=128),
                                in_=w1e[:, hc * 128:(hc + 1) * 128]
                                .rearrange("(c p) m -> p c m", p=128))
                p.dW.inc(i, 16)

            if hc == 0:
                p.l0f_ac0 = p.sAC.n

            @p.raw_section("tensor")
            def _(t, hc=hc, buf=buf):
                if hc == 0:
                    p.wait_all(t)
                t.wait_ge(p.dW.h, p.dW.n)
                if hc >= 1:
                    t.wait_ge(p.sAC.h, p.l0f_ac0 + hc)
                last = None
                for dc in range(8):
                    if premul is not None:
                        rhs = p.W[:, 4096 + dc * 512:4096 + dc * 512 + 512]
                    else:
                        rhs = xn_buf[:, xn_off + dc * (NT * 128) + tgo:
                                     xn_off + dc * (NT * 128) + tgo + 512]
                    last = t.matmul(p.psC[:],
                                    p.W[:, buf + dc * 128:buf + dc * 128 + 128],
                                    rhs, start=(dc == 0), stop=(dc == 7))
                p.sMM.inc(last)

            @p.raw_section("scalar")
            def _(s, hc=hc):
                s.wait_ge(p.sMM.h, p.sMM.n)
                if hc == 0:
                    s.wait_ge(p.dW.h, bias_dw)
                s.activation(out=h_buf[:, h_off + hc * 512:h_off + (hc + 1) * 512],
                             in_=p.psC[:],
                             func=act, bias=p.CS[:, 128 + hc:129 + hc], scale=h_scale)
                p.dinc(s, p.sAC)

        for half in range(2):
            tts = (0, 1) if half == 0 else (2, 3)
            if half == 0:
                accs = [p.psS[:, 0:512], p.psS[:, 512:1024],
                        p.psS[:, 1024:1536], p.psS[:, 1536:2048]]
            else:
                accs = [p.psA[:], p.psB[:], p.psC[:], p.psS[:, 0:512]]
            for hc in range(32):
                buf = 2048 + (hc % 2) * 1024

                @p.raw_section("gpsimd")
                def _(g, hc=hc, buf=buf, half=half):
                    if hc >= 2:
                        g.wait_ge(p.sMM.h, p.l0f_mm0 + (hc - 1))
                    elif half == 1 and hc < 2:
                        g.wait_ge(p.sEV.h, p.sEV.n)
                    i = g.dma_start(out=p.W[:, buf:buf + 1024],
                                    in_=w2e[hc * 128:(hc + 1) * 128, :])
                    p.dW.inc(i, 16)

                if hc == 0:
                    p.l0f_mm0 = p.sMM.n

                @p.raw_section("tensor")
                def _(t, hc=hc, buf=buf, tts=tts, accs=accs, half=half):
                    t.wait_ge(p.dW.h, p.dW.n)
                    if hc == 0:
                        t.wait_ge(p.sAC.h, p.sAC.n)
                        if half == 1:
                            t.wait_ge(p.sEV.h, p.sEV.n)
                    last = None
                    for j, tt in enumerate(tts):
                        for oh in range(2):
                            last = t.matmul(accs[j * 2 + oh],
                                            h_buf[:, h_off + hc * 512 + tt * 128:
                                                  h_off + hc * 512 + tt * 128 + 128],
                                            p.W[:, buf + oh * 512:buf + oh * 512 + 512],
                                            start=(hc == 0), stop=(hc == 31))
                    p.sMM.inc(last)

            @p.raw_section("tensor")
            def _(t, tts=tts, accs=accs):
                t.wait_ge(p.dW.h, bias_dw)
                last = None
                for j, tt in enumerate(tts):
                    for oh in range(2):
                        last = t.matmul(accs[j * 2 + oh], p.onesb[:],
                                        p.Rb[0:1, oh * 512:oh * 512 + 512],
                                        start=False, stop=True)
                p.sMM.inc(last)

            @p.raw_section("vector")
            def _(v, tts=tts, accs=accs, tg=tg):
                v.wait_ge(p.sMM.h, p.sMM.n)
                for j, tt in enumerate(tts):
                    ti = tg * 4 + tt
                    for oh in range(2):
                        if out_dst is None:
                            xs = p.xres[:, ti * D + oh * 512: ti * D + oh * 512 + 512]
                        else:
                            obuf, ooff = out_dst
                            xs = obuf[:, ooff + ti * D + oh * 512:
                                      ooff + ti * D + oh * 512 + 512]
                        if cw_col is None:
                            v.tensor_add(xs, xs, accs[j * 2 + oh])
                        else:
                            v.scalar_tensor_tensor(out=xs, in0=accs[j * 2 + oh],
                                                   scalar=p.ST[:, cw_col(ti):cw_col(ti) + 1],
                                                   in1=xs, op0=OP.mult, op1=OP.add)
                p.dinc(v, p.sEV)
                p.T.inc(v.drain())


def emit_l1_attn(p, core_id_dram_names=True):
    WT = p.din("L1_attn_in_proj_weightT", [D, 3 * D])
    bias = p.din("L1_attn_in_proj_bias", [3 * D])
    rho = p.din("rho_c", [128, 8])
    outWT = p.din("L1_attn_out_wT", [D, D])
    outb = p.din("L1_attn_out_b", [D])

    kv_in = p.dint("l1_kv_in", [2 * T, D], BF16)
    kv_out = p.dint("l1_kv_out", [NCORES * 2 * T, D], BF16)

    XN = 16384
    emit_ln(p, dst=p.A, dst_off=0)
    emit_transpose(p, dst=p.A, dst_off=XN, src=p.A, src_off=0)

    for j, (wo, boff) in enumerate(((D, D), (2 * D, 2 * D))):
        emit_weight_load(p, WT, p.W, 0, D, D, out_off=wo)
        emit_row_load(p, bias, 0, D, src_off=boff)
        emit_proj_tm(p, dst=p.B, dst_off=0, src_fm=p.A, src_off=XN, w_sb=p.W, w_off=0,
                     n_in=D, n_out=D, bias_off=0, wait_dw=p.dW.n)

        @p.section("gpsimd")
        def _(g, j=j):
            return g.dma_start(out=kv_in[j * T:(j + 1) * T, :].rearrange("(t p) d -> p t d", p=128),
                               in_=p.B[:].rearrange("p (t d) -> p t d", d=D))

    @p.raw_section("gpsimd")
    def _(g):
        p.wait_all(g)
        g.collective_compute("AllGather", OP.bypass, replica_groups=[list(range(NCORES))],
                             ins=[kv_in[:]], outs=[kv_out[:]]).then_inc(p.sCC.h)
        p.sCC.n += 1

    emit_weight_load(p, WT, p.W, 0, D, D, out_off=0)
    emit_row_load(p, bias, 0, D, src_off=0)
    emit_proj_tm(p, dst=p.B, dst_off=0, src_fm=p.A, src_off=XN, w_sb=p.W, w_off=0,
                 n_in=D, n_out=D, bias_off=0, scale=0.125, wait_dw=p.dW.n)
    emit_transpose(p, dst=p.A, dst_off=0, src=p.B, src_off=0)

    @p.raw_section("gpsimd")
    def _(g):
        g.wait_ge(p.T.h, p.T.n)
        i = g.dma_start(out=p.W[0:128, 0:8], in_=rho[:])
        p.dW.inc(i, 16)
    rel_dw = p.dW.n

    REL = 16384

    @p.raw_section("tensor")
    def _(t):
        p.wait_all(t)
        t.wait_ge(p.dW.h, rel_dw)
        ev0 = p.sEV.n
        job = 0
        for ti in range(NT):
            for h in range(16):
                ch, hin = (h * 64) // 128, (h * 64) % 128
                if job >= 2:
                    t.wait_ge(p.sEV.h, ev0 + job - 1)
                i = t.matmul(p.psA[0:128, 0:8],
                             p.A[hin:hin + 64, ch * 2048 + ti * 128: ch * 2048 + ti * 128 + 128],
                             p.W[hin:hin + 64, 0:8], start=True, stop=True)
                p.sMM.inc(i)
                job += 1

    @p.raw_section("vector")
    def _(v):
        base = p.sMM.n - NT * 16
        job = 0
        for ti in range(NT):
            for h in range(16):
                v.wait_ge(p.sMM.h, base + job + 1)
                v.tensor_copy(p.A[:, REL + ti * 128 + h * 8: REL + ti * 128 + h * 8 + 8],
                              p.psA[:, 0:8])
                p.dinc(v, p.sEV)
                job += 1
        p.T.inc(v.drain())

    for ti in range(NT):
        @p.raw_section("gpsimd")
        def _(g, ti=ti):
            if ti == 0:
                p.wait_all(g)
                g.wait_ge(p.sCC.h, p.sCC.n)
            else:
                g.wait_ge(p.sEV.h, p.sEV.n)
            for s in range(NCORES):
                i = g.dma_start(out=p.W[:, s * 1024:(s + 1) * 1024],
                                in_=kv_out[(s * 2) * T + ti * 128:(s * 2) * T + ti * 128 + 128, :])
                p.dW.inc(i, 16)

        @p.raw_section("vector")
        def _(v, ti=ti):
            v.wait_ge(p.dW.h, p.dW.n)
            for s in range(NCORES):
                v.tensor_tensor(out=p.E[:, 1024:2048], in0=p.B[:, ti * 1024:(ti + 1) * 1024],
                                in1=p.W[:, s * 1024:(s + 1) * 1024], op=OP.mult)
                v.tensor_reduce(out=p.E[:, 0:128].rearrange("p (h s) -> p h s", s=8)[:, :, s:s + 1].squeeze(2),
                                in_=p.E[:, 1024:2048].rearrange("p (h d) -> p h d", d=64),
                                axis=X, op=OP.add)
            v.drain()
            v.tensor_tensor(out=p.E[:, 0:128], in0=p.E[:, 0:128],
                            in1=p.A[:, REL + ti * 128:REL + (ti + 1) * 128], op=OP.add)
            sview = p.E[:, 0:128].rearrange("p (h s) -> p h s", s=8)
            m = p.ST[:, 384:400]
            v.drain()
            v.tensor_reduce(out=m, in_=sview, axis=X, op=OP.max)
            v.drain()
            v.tensor_tensor(out=sview, in0=sview, in1=m.unsqueeze(2).broadcast_to([128, 16, 8]),
                            op=OP.subtract)
            p.dinc(v, p.sEV)

        @p.raw_section("scalar")
        def _(s_, ti=ti):
            s_.wait_ge(p.sEV.h, p.sEV.n)
            s_.activation(out=p.E[:, 0:128], in_=p.E[:, 0:128], func=AF.Exp)
            p.dinc(s_, p.sAC)

        @p.raw_section("gpsimd")
        def _(g, ti=ti):
            g.wait_ge(p.sAC.h, p.sAC.n)
            for s in range(NCORES):
                i = g.dma_start(out=p.W[:, s * 1024:(s + 1) * 1024],
                                in_=kv_out[(s * 2 + 1) * T + ti * 128:(s * 2 + 1) * T + ti * 128 + 128, :])
                p.dW.inc(i, 16)

        @p.raw_section("vector")
        def _(v, ti=ti):
            v.wait_ge(p.sAC.h, p.sAC.n)
            sview = p.E[:, 0:128].rearrange("p (h s) -> p h s", s=8)
            sm = p.ST[:, 400:416]
            v.tensor_reduce(out=sm, in_=sview, axis=X, op=OP.add)
            v.drain()
            v.reciprocal(sm, sm)
            v.drain()
            v.tensor_tensor(out=sview, in0=sview, in1=sm.unsqueeze(2).broadcast_to([128, 16, 8]),
                            op=OP.mult)
            v.drain()
            v.wait_ge(p.dW.h, p.dW.n)
            for s in range(NCORES):
                pb = p.E[:, 0:128].rearrange("p (h s) -> p h s", s=8)[:, :, s:s + 1] \
                    .broadcast_to([128, 16, 64])
                vv = p.W[:, s * 1024:(s + 1) * 1024].rearrange("p (h d) -> p h d", d=64)
                if s == 0:
                    v.tensor_tensor(out=p.E[:, 2048:3072].rearrange("p (h d) -> p h d", d=64),
                                    in0=vv, in1=pb, op=OP.mult)
                else:
                    v.tensor_tensor(out=p.E[:, 1024:2048].rearrange("p (h d) -> p h d", d=64),
                                    in0=vv, in1=pb, op=OP.mult)
                    v.tensor_add(p.E[:, 2048:3072], p.E[:, 2048:3072], p.E[:, 1024:2048])
            v.drain()
            v.tensor_copy(p.A[:, ti * 1024:(ti + 1) * 1024], p.E[:, 2048:3072])
            p.dinc(v, p.sEV)
            p.T.inc(v.drain())

    emit_transpose(p, dst=p.A, dst_off=XN, src=p.A, src_off=0)
    emit_weight_load(p, outWT, p.W, 0, D, D)
    emit_row_load(p, outb, 0, D)
    emit_proj_residual(p, src_fm=(p.A, XN), w=(p.W, 0), n_in=D, n_out=D,
                       wait_dw=p.dW.n, bias_off=0)


def emit_quantize(p, wT_dram, n_in, n_out, q_dram, onescol, eps=1e-5):
    nch = n_in // 128
    dbl = (2 * n_out <= 4096)
    acc = p.ST[:, 416:417]
    tmp = p.ST[:, 417:418]
    thr = p.ST[:, 418:419]
    thr_d = p.dint("q_thr", [1, 1], F32)

    @p.section("vector")
    def _(v):
        v.memset(acc, 0.0)
        return v.drain()

    for c in range(nch):
        bo = (c % 2) * n_out if dbl else 0

        @p.raw_section("gpsimd")
        def _(g, c=c, bo=bo):
            if c == 0:
                g.wait_ge(p.T.h, p.T.n)
            else:
                g.wait_ge(p.sEV.h, p.sEV.n - (1 if (dbl and c >= 2) else 0))
            i = g.dma_start(out=p.E[:, bo:bo + n_out],
                            in_=wT_dram[c * 128:(c + 1) * 128, :])
            p.dW.inc(i, 16)

        @p.raw_section("vector")
        def _(v, c=c, bo=bo):
            v.wait_ge(p.dW.h, p.dW.n)
            v.tensor_reduce(out=tmp, in_=p.E[:, bo:bo + n_out],
                            axis=X, op=OP.add, apply_absolute_value=True)
            v.drain()
            v.tensor_add(acc, acc, tmp)
            p.dinc(v, p.sEV)

    @p.raw_section("tensor")
    def _(t):
        t.wait_ge(p.sEV.h, p.sEV.n)
        i = t.matmul(p.psA[0:1, 0:1], acc, onescol[:], start=True, stop=True)
        p.sMM.inc(i)

    @p.raw_section("vector")
    def _(v):
        v.wait_ge(p.sMM.h, p.sMM.n)
        v.tensor_scalar(out=thr[0:1], in0=p.psA[0:1, 0:1],
                        scalar1=0.5 / (n_in * n_out), scalar2=0.5 * eps,
                        op0=OP.mult, op1=OP.add)
        p.dinc(v, p.sEV)

    @p.raw_section("gpsimd")
    def _(g):
        g.wait_ge(p.sEV.h, p.sEV.n)
        i = g.dma_start(out=thr_d[:], in_=thr[0:1])
        p.dW.inc(i, 16)
        g.wait_ge(p.dW.h, p.dW.n)
        i = g.dma_start(out=thr, in_=thr_d.broadcast_to([128, 1]))
        p.dW.inc(i, 16)

    @p.section("vector")
    def _(v):
        v.wait_ge(p.dW.h, p.dW.n)
        return v.drain()

    for c in range(nch):
        bo = (c % 2) * n_out if dbl else 0

        @p.raw_section("gpsimd")
        def _(g, c=c, bo=bo):
            if c == 0:
                g.wait_ge(p.T.h, p.T.n)
            else:
                g.wait_ge(p.sEV.h, p.sEV.n - (1 if (dbl and c >= 2) else 0))
            i = g.dma_start(out=p.E[:, bo:bo + n_out],
                            in_=wT_dram[c * 128:(c + 1) * 128, :])
            p.dW.inc(i, 16)

        @p.raw_section("vector")
        def _(v, c=c, bo=bo):
            v.wait_ge(p.dW.h, p.dW.n)
            e = p.E[:, bo:bo + n_out]
            v.tensor_single_scalar(out=p.B[:, bo:bo + n_out], in_=e, scalar=thr, op=OP.is_ge)
            v.tensor_scalar(out=p.B[:, 8192 + bo:8192 + bo + n_out], in0=e,
                            scalar1=-1.0, scalar2=thr, op0=OP.mult, op1=OP.is_ge)
            v.drain()
            v.tensor_sub(p.B[:, bo:bo + n_out], p.B[:, bo:bo + n_out],
                         p.B[:, 8192 + bo:8192 + bo + n_out])
            p.dinc(v, p.sEV)

        @p.raw_section("gpsimd")
        def _(g, c=c, bo=bo):
            g.wait_ge(p.sEV.h, p.sEV.n)
            i = g.dma_start(out=q_dram[c * 128:(c + 1) * 128, :],
                            in_=p.B[:, bo:bo + n_out])
            p.dW.inc(i, 16)

    @p.section("vector")
    def _(v):
        return v.drain()


def emit_l1_ffn(p):
    W1T = p.din("L1_ffn_W1T", [D, 4096])
    b1 = p.din("L1_ffn_b1", [4096])
    W2T = p.din("L1_ffn_W2T", [4096, D])
    b2 = p.din("L1_ffn_b2", [D])
    onescol = p.din("c_ones_col", [128, 1])
    w1q = p.dint("l1_w1q", [D, 4096], BF16)
    w2q = p.dint("l1_w2q", [4096, D], BF16)
    oc = p.sb("onescol_sb", [128, 1], F32)

    @p.raw_section("gpsimd")
    def _(g):
        g.wait_ge(p.T.h, p.T.n)
        i = g.dma_start(out=oc[:], in_=onescol[:])
        p.dW.inc(i, 16)

    @p.section("vector")
    def _(v):
        v.wait_ge(p.dW.h, p.dW.n)
        return v.drain()

    emit_quantize(p, W1T, D, 4096, w1q, oc)
    emit_quantize(p, W2T, 4096, D, w2q, oc)

    XN = 16384
    emit_ln(p, dst=p.A, dst_off=0)
    emit_transpose(p, dst=p.A, dst_off=XN, src=p.A, src_off=0)
    emit_moe_expert(p, None, w1q, b1, w2q, b2, xn_off=XN, cw_col=None,
                    act=AF.Relu, add_xf=False)


def emit_l2_attn(p):
    Wq = p.din("L2_attn_Wq", [D, D]);  bq = p.din("L2_attn_bq", [D])
    Wk = p.din("L2_attn_Wk", [D, D]);  bk = p.din("L2_attn_bk", [D])
    Wv = p.din("L2_attn_Wv", [D, D]);  bv = p.din("L2_attn_bv", [D])
    Wo = p.din("L2_attn_Wo", [D, D]);  bo = p.din("L2_attn_bo", [D])
    ds2 = p.din("L2_attn_dist_scale2d", [1, 1])
    W1 = p.din("L2_attn_W1", [D, 4096]); b1 = p.din("L2_attn_b1", [4096])
    W2 = p.din("L2_attn_W2", [4096, D]); b2 = p.din("L2_attn_b2", [D])
    g1 = p.din("L2_attn_ln1_g", [D]); be1 = p.din("L2_attn_ln1_b", [D])
    g2 = p.din("L2_attn_ln2_g", [D]); be2 = p.din("L2_attn_ln2_b", [D])
    cdist = p.din("c_dist", [28, 128, 512])
    pen_d = p.dint("l2_pen", [28, 128, 512], BF16)
    dsb = p.ST[:, 419:420]

    XN = 16384
    @p.raw_section("gpsimd")
    def _(g):
        g.wait_ge(p.T.h, p.T.n)
        i = g.dma_start(out=dsb, in_=ds2.broadcast_to([128, 1]))
        p.dW.inc(i, 16)

    @p.section("vector")
    def _(v):
        v.wait_ge(p.dW.h, p.dW.n)
        v.tensor_scalar_mul(dsb, dsb, -0.125)
        return v.drain()

    for r in range(28):
        @p.raw_section("gpsimd")
        def _(g, r=r):
            if r == 0:
                g.wait_ge(p.T.h, p.T.n)
            else:
                g.wait_ge(p.sEV.h, p.sEV.n)
            i = g.dma_start(out=p.E[:, 0:512], in_=cdist[r])
            p.dW.inc(i, 16)

        @p.raw_section("vector")
        def _(v, r=r):
            v.wait_ge(p.dW.h, p.dW.n)
            v.tensor_scalar_mul(p.B[:, 0:512], p.E[:, 0:512], dsb)
            p.dinc(v, p.sEV)

        @p.raw_section("gpsimd")
        def _(g, r=r):
            g.wait_ge(p.sEV.h, p.sEV.n)
            i = g.dma_start(out=pen_d[r], in_=p.B[:, 0:512])
            p.dW.inc(i, 16)

    emit_ln(p, dst=p.A, dst_off=0)
    emit_transpose(p, dst=p.A, dst_off=XN, src=p.A, src_off=0)

    for e8 in range(8):
        dsl = e8 * 128
        emit_weight_load(p, Wq, p.W, 0, D, 128, out_off=dsl)
        emit_weight_load(p, Wk, p.W, 1024, D, 128, out_off=dsl)
        emit_weight_load(p, Wv, p.W, 2048, D, 128, out_off=dsl)
        emit_row_load(p, bq, 0, 128, src_off=dsl)
        emit_row_load(p, bk, 256, 128, src_off=dsl)
        emit_row_load(p, bv, 512, 128, src_off=dsl)
        dw = p.dW.n
        emit_proj_tm(p, dst=p.B, dst_off=0, src_fm=p.A, src_off=XN, w_sb=p.W, w_off=0,
                     n_in=D, n_out=128, bias_off=0, scale=0.125, wait_dw=dw,
                     dst_stride=128)
        emit_transpose(p, dst=p.B, dst_off=2048, src=p.B, src_off=0, width=128)
        emit_proj_tm(p, dst=p.B, dst_off=4096, src_fm=p.A, src_off=XN, w_sb=p.W,
                     w_off=1024, n_in=D, n_out=128, bias_off=256, wait_dw=dw,
                     dst_stride=128)
        emit_transpose(p, dst=p.B, dst_off=6144, src=p.B, src_off=4096, width=128)
        emit_proj_tm(p, dst=p.B, dst_off=8192, src_fm=p.A, src_off=XN, w_sb=p.W,
                     w_off=2048, n_in=D, n_out=128, bias_off=512, wait_dw=dw,
                     dst_stride=128)
        emit_l2_attn_8(p, e8, pen_d)
        emit_transpose(p, dst=p.B, dst_off=6144, src=p.B, src_off=10240, width=128)
        emit_weight_load(p, Wo, p.W, 3072, 128, D, in_off=dsl)
        if e8 == 7:
            emit_row_load(p, bo, 1024, D)
        emit_proj_residual(p, src_fm=(p.B, 6144), w=(p.W, 3072), n_in=128, n_out=D,
                           wait_dw=p.dW.n, dst=(p.A, 0, D),
                           bias_off=(1024 if e8 == 7 else None))

    @p.raw_section("gpsimd")
    def _(g):
        g.wait_ge(p.T.h, p.T.n)
        i = g.dma_start(out=p.W[:, 0:1024], in_=g1.unsqueeze(0).broadcast_to([128, D]))
        p.dW.inc(i, 16)
        i = g.dma_start(out=p.W[:, 1024:2048], in_=be1.unsqueeze(0).broadcast_to([128, D]))
        p.dW.inc(i, 16)

    @p.section("vector")
    def _(v):
        v.wait_ge(p.dW.h, p.dW.n)
        return v.drain()

    emit_ln(p, dst=p.B, dst_off=0, src=p.A, src_off=0,
            affine=(p.W[:, 0:1024], p.W[:, 1024:2048]))
    emit_transpose(p, dst=p.A, dst_off=XN, src=p.B, src_off=0)
    emit_moe_expert(p, None, W1, b1, W2, b2, xn_off=XN, cw_col=None, act=AF.Relu,
                    add_xf=False, xn_buf=p.A, h_buf=p.A, h_off=0, out_dst=(p.B, 0))
    @p.raw_section("gpsimd")
    def _(g):
        g.wait_ge(p.T.h, p.T.n)
        i = g.dma_start(out=p.W[:, 0:1024], in_=g2.unsqueeze(0).broadcast_to([128, D]))
        p.dW.inc(i, 16)
        i = g.dma_start(out=p.W[:, 1024:2048], in_=be2.unsqueeze(0).broadcast_to([128, D]))
        p.dW.inc(i, 16)

    @p.section("vector")
    def _(v):
        v.wait_ge(p.dW.h, p.dW.n)
        return v.drain()

    emit_ln(p, dst=p.A, dst_off=0, src=p.B, src_off=0,
            affine=(p.W[:, 0:1024], p.W[:, 1024:2048]))

    @p.section("vector")
    def _(v):
        last = None
        for ti in range(NT):
            last = v.tensor_add(p.xres[:, ti * D:(ti + 1) * D],
                                p.xres[:, ti * D:(ti + 1) * D],
                                p.A[:, ti * 1024:(ti + 1) * 1024])
        return last


def emit_l2_attn_8(p, e8, pen_d):
    QF, KF, VT, CO, P_off, PT_off, PEN = 2048, 6144, 8192, 10240, 12288, 14336, 4096
    nmax = p.ST[:, 368:369]
    sums = p.ST[:, 369:373]
    rinv = p.ST[:, 373:374]
    ac_base = p.sAC.n

    for qi in range(NT):
        @p.raw_section("gpsimd")
        def _(g, qi=qi):
            if qi == 0:
                p.wait_all(g)
            else:
                g.wait_ge(p.sAC.h, ac_base + 2 * qi)
            for kc in range(4):
                r = (qi * 128 - kc * 512 + 1536) // 128
                i = g.dma_start(out=p.W[:, PEN + kc * 512:PEN + (kc + 1) * 512],
                                in_=pen_d[r])
                p.dW.inc(i, 16)

        for h in range(2):
            hglob = e8 * 2 + h
            hin = h * 64

            @p.raw_section("tensor")
            def _(t, qi=qi, h=h, hin=hin):
                if qi == 0 and h == 0:
                    p.wait_all(t)
                t.wait_ge(p.dW.h, p.dW.n)
                if not (qi == 0 and h == 0):
                    t.wait_ge(p.sAC.h, p.sAC.n)
                for b in range(4):
                    t.matmul(p.psS[:, b * 512:(b + 1) * 512],
                             p.B[hin:hin + 64, QF + qi * 128:QF + qi * 128 + 128],
                             p.B[hin:hin + 64, KF + b * 512:KF + (b + 1) * 512],
                             start=True, stop=False)
                last = None
                for b in range(4):
                    last = t.matmul(p.psS[:, b * 512:(b + 1) * 512], p.ident[:],
                                    p.W[:, PEN + b * 512:PEN + (b + 1) * 512],
                                    start=False, stop=True)
                p.sMM.inc(last)

            @p.raw_section("vector")
            def _(v, hglob=hglob):
                v.wait_ge(p.sMM.h, p.sMM.n)
                if hglob > 0:
                    v.tensor_scalar_add(p.psS[:, 0:hglob], p.psS[:, 0:hglob], NEG)
                    v.drain()
                v.reduce_max(out=nmax, in_=p.psS[:, 0:2048], axis=X, negate=True)
                p.dinc(v, p.sEV)

            @p.raw_section("scalar")
            def _(s):
                s.wait_ge(p.sEV.h, p.sEV.n)
                for b in range(4):
                    s.activation(out=p.B[:, P_off + b * 512:P_off + (b + 1) * 512],
                                 in_=p.psS[:, b * 512:(b + 1) * 512], func=AF.Exp,
                                 bias=nmax, accum_out=sums[:, b:b + 1])
                p.dinc(s, p.sAC)

            @p.raw_section("vector")
            def _(v):
                v.wait_ge(p.sAC.h, p.sAC.n)
                v.tensor_reduce(out=rinv, in_=sums, axis=X, op=OP.add)
                v.drain()
                v.reciprocal(rinv, rinv)
                p.dinc(v, p.sEV)

            @p.raw_section("tensor")
            def _(t):
                t.wait_ge(p.sAC.h, p.sAC.n)
                ev0 = p.sEV.n
                for gi, g0 in enumerate(range(0, 16, 8)):
                    if gi > 0:
                        t.wait_ge(p.sEV.h, ev0 + gi)
                    last = None
                    for k in range(8):
                        last = t.transpose(p.psT[:, k * 128:(k + 1) * 128],
                                           p.B[:, P_off + (g0 + k) * 128:
                                               P_off + (g0 + k) * 128 + 128],
                                           p.ident[:])
                    p.sMM.inc(last)

            @p.raw_section("vector")
            def _(v):
                base = p.sMM.n - 2
                for gi, g0 in enumerate(range(0, 16, 8)):
                    v.wait_ge(p.sMM.h, base + gi + 1)
                    v.tensor_copy(p.B[:, PT_off + g0 * 128:PT_off + g0 * 128 + 1024],
                                  p.psT[:, 0:1024])
                    p.dinc(v, p.sEV)

            @p.raw_section("tensor")
            def _(t, hin=hin):
                t.wait_ge(p.sEV.h, p.sEV.n)
                last = None
                for kb in range(16):
                    last = t.matmul(p.psA[:, 0:64],
                                    p.B[:, PT_off + kb * 128:PT_off + kb * 128 + 128],
                                    p.B[:, VT + kb * 128 + hin:VT + kb * 128 + hin + 64],
                                    start=(kb == 0), stop=(kb == 15))
                p.sMM.inc(last)

            @p.raw_section("vector")
            def _(v, qi=qi, hin=hin):
                v.wait_ge(p.sMM.h, p.sMM.n)
                v.tensor_scalar_mul(p.B[:, CO + qi * 128 + hin:CO + qi * 128 + hin + 64],
                                    p.psA[:, 0:64], rinv)
                v.drain()
                p.dinc(v, p.sEV)
                p.T.inc(v.drain())


def emit_l2_ffn(p):
    Wg = p.din("L2_ffn_Wg", [D, 8])
    gP = p.din("L2_ffn_gammaP", [128, 64])
    bP = p.din("L2_ffn_betaP", [128, 64])
    W1 = p.din("L2_ffn_W1", [8, D, 4096])
    b1 = p.din("L2_ffn_b1", [8, 4096])
    W2 = p.din("L2_ffn_W2", [8, 4096, D])
    b2 = p.din("L2_ffn_b2", [8, D])

    XN = 16384
    emit_ln(p, dst=p.A, dst_off=0)
    emit_transpose(p, dst=p.A, dst_off=XN, src=p.A, src_off=0)

    @p.raw_section("gpsimd")
    def _(g):
        g.wait_ge(p.T.h, p.T.n)
        i = g.dma_start(out=p.CS[:, 160:224], in_=gP[:])
        p.dW.inc(i, 16)
        i = g.dma_start(out=p.CS[:, 224:288], in_=bP[:])
        p.dW.inc(i, 16)

    emit_weight_load(p, Wg, p.W, 0, D, 8)
    emit_proj_tm(p, dst=p.E, dst_off=0, src_fm=p.A, src_off=XN, w_sb=p.W, w_off=0,
                 n_in=D, n_out=8, wait_dw=p.dW.n, evac="vector", dst_stride=8)

    m1 = p.ST[:, 384:400]; m2 = p.ST[:, 400:416]
    e2c = p.ST[:, 420:436]; w1n = p.ST[:, 436:452]; w2n = p.ST[:, 452:468]

    @p.section("vector")
    def _(v):
        lg = p.E[:, 0:128].rearrange("p (t e) -> p t e", e=8)
        eq1 = p.E[:, 128:256].rearrange("p (t e) -> p t e", e=8)
        t2 = p.E[:, 256:384].rearrange("p (t e) -> p t e", e=8)
        v.tensor_reduce(out=m1, in_=lg, axis=X, op=OP.max)
        v.drain()
        v.tensor_tensor(out=eq1, in0=lg, in1=m1.unsqueeze(2).broadcast_to([128, 16, 8]),
                        op=OP.is_equal)
        v.drain()
        v.scalar_tensor_tensor(out=t2, in0=eq1, scalar=NEG, in1=lg, op0=OP.mult, op1=OP.add)
        v.drain()
        v.tensor_reduce(out=m2, in_=t2, axis=X, op=OP.max)
        v.drain()
        v.tensor_sub(e2c, m2, m1)
        return v.drain()

    @p.section("scalar")
    def _(s):
        return s.activation(out=e2c, in_=e2c, func=AF.Exp)

    @p.section("vector")
    def _(v):
        lg = p.E[:, 0:128].rearrange("p (t e) -> p t e", e=8)
        eq1 = p.E[:, 128:256].rearrange("p (t e) -> p t e", e=8)
        t2 = p.E[:, 256:384].rearrange("p (t e) -> p t e", e=8)
        eq2 = p.E[:, 384:512].rearrange("p (t e) -> p t e", e=8)
        cw = p.ST[:, 256:384].rearrange("p (t e) -> p t e", e=8)
        v.tensor_scalar_add(w1n, e2c, 1.0)
        v.drain()
        v.reciprocal(w1n, w1n)
        v.drain()
        v.tensor_tensor(out=w2n, in0=e2c, in1=w1n, op=OP.mult)
        v.tensor_tensor(out=eq2, in0=t2, in1=m2.unsqueeze(2).broadcast_to([128, 16, 8]),
                        op=OP.is_equal)
        v.drain()
        v.tensor_tensor(out=cw, in0=eq1, in1=w1n.unsqueeze(2).broadcast_to([128, 16, 8]),
                        op=OP.mult)
        v.tensor_tensor(out=eq2, in0=eq2, in1=w2n.unsqueeze(2).broadcast_to([128, 16, 8]),
                        op=OP.mult)
        v.drain()
        v.tensor_add(cw, cw, eq2)
        return v.drain()

    @p.section("vector")
    def _(v):
        last = None
        for ti in range(NT):
            last = v.tensor_add(p.xres[:, ti * D:(ti + 1) * D],
                                p.xres[:, ti * D:(ti + 1) * D],
                                p.A[:, ti * 1024:(ti + 1) * 1024])
        return last

    emit_ln(p, dst=p.B, dst_off=0, src=p.A, src_off=0, eps=1e-6)
    emit_transpose(p, dst=p.A, dst_off=XN, src=p.B, src_off=0)

    CW = 256
    for e in range(8):
        emit_moe_expert(p, e, W1, b1, W2, b2, xn_off=XN,
                        cw_col=lambda ti, e=e: CW + ti * 8 + e,
                        act=AF.Relu, add_xf=False,
                        premul=(lambda dc, e=e: p.CS[:, 160 + e * 8 + dc:161 + e * 8 + dc],
                                lambda dc, e=e: p.CS[:, 224 + e * 8 + dc:225 + e * 8 + dc]))


def emit_l3_attn(p):
    WqT = p.din("L3_attn_WqT", [D, D])
    WkT = p.din("L3_attn_WkT", [D, 256])
    WvT = p.din("L3_attn_WvT", [D, 256])
    WoT = p.din("L3_attn_WoT", [D, D])
    mb = p.din("c_l3_bandmask", [2, 128, 176], BF16)
    ms = p.din("c_l3_stridemask", [NT, 128, 44], BF16)
    v_d = p.dint("l3_v", [T, 256], BF16)

    XN = 16384
    Eb = p.E[:].bitcast(BF16)
    emit_ln(p, dst=p.A, dst_off=0)
    emit_transpose(p, dst=p.A, dst_off=XN, src=p.A, src_off=0)

    emit_weight_load(p, WkT, p.W, 0, D, 256)
    emit_weight_load(p, WvT, p.W, 2048, D, 256)
    dw = p.dW.n
    emit_proj_tm(p, dst=p.B, dst_off=0, src_fm=p.A, src_off=XN, w_sb=p.W, w_off=0,
                 n_in=D, n_out=256, wait_dw=dw, dst_stride=256)
    emit_proj_tm(p, dst=Eb, dst_off=4096, src_fm=p.A, src_off=XN, w_sb=p.W,
                 w_off=2048, n_in=D, n_out=256, dst_stride=256)
    emit_transpose(p, dst=p.B, dst_off=4096, src=p.B, src_off=0, width=256)
    emit_weight_load(p, WqT, p.W, 0, D, D)
    emit_proj_tm(p, dst=p.A, dst_off=0, src_fm=p.A, src_off=XN, w_sb=p.W, w_off=0,
                 n_in=D, n_out=D, scale=0.125, wait_dw=p.dW.n)
    @p.raw_section("gpsimd")
    def _(g):
        p.wait_all(g)
        for g4 in range(4):
            ch, base = g4 // 2, (g4 % 2) * 64
            i = g.dma_start(out=p.W[0:64, g4 * 2048:(g4 + 1) * 2048],
                            in_=p.B[base:base + 64, 4096 + ch * 2048:4096 + (ch + 1) * 2048])
            p.dW.inc(i, 16)
            i = g.dma_start(out=p.W[64:128, g4 * 2048:(g4 + 1) * 2048],
                            in_=p.B[base:base + 64, 4096 + ch * 2048:4096 + (ch + 1) * 2048])
            p.dW.inc(i, 16)
        i = g.dma_start(out=v_d.rearrange("(t p) d -> p t d", p=128),
                        in_=Eb[:, 4096:8192].rearrange("p (t d) -> p t d", d=256))
        p.dW.inc(i, 16)
        g.wait_ge(p.dW.h, p.dW.n)
        i = g.dma_start(out=Eb[0:44, 896:1152],
                        in_=v_d[45:2024:46, :])
        p.dW.inc(i, 16)
        i = g.dma_start(out=Eb[:, 1280:1456], in_=mb[0])
        p.dW.inc(i, 16)
        i = g.dma_start(out=Eb[:, 1456:1632], in_=mb[1])
        p.dW.inc(i, 16)

    @p.section("vector")
    def _(v):
        v.wait_ge(p.dW.h, p.dW.n)
        v.memset(Eb[:, 640:896], 0.0)
        return v.drain()

    emit_transpose(p, dst=p.B, dst_off=0, src=p.A, src_off=0)

    nmax = p.ST[:, 368:369]
    sums = p.ST[:, 369:370]
    rinv = p.ST[:, 373:374]

    for qi in range(NT):
        @p.raw_section("gpsimd")
        def _(g, qi=qi):
            if qi == 0:
                p.wait_all(g)
            else:
                g.wait_ge(p.sAC.h, p.sAC.n)
            i = g.dma_start(out=Eb[:, 1664:1708], in_=ms[qi])
            p.dW.inc(i, 16)
            if qi > 0:
                i = g.dma_start(out=Eb[0:48, 640:896],
                                in_=Eb[80:128, 4096 + (qi - 1) * 256:4096 + qi * 256])
                p.dW.inc(i, 16)

        for h in range(16):
            g4 = h // 4
            qch, qbase = h // 2, (h % 2) * 64
            kbase = qbase

            @p.raw_section("tensor")
            def _(t, qi=qi, h=h, g4=g4, qch=qch, qbase=qbase):
                if qi == 0 and h == 0:
                    p.wait_all(t)
                t.wait_ge(p.dW.h, p.dW.n)
                if not (qi == 0 and h == 0):
                    t.wait_ge(p.sAC.h, p.sAC.n)
                qf = p.B[qbase:qbase + 64,
                         qch * 2048 + qi * 128:qch * 2048 + qi * 128 + 128]
                ka = max(0, qi * 128 - 48)
                t.matmul(p.psS[:, 0:48],
                         qf, p.W[qbase:qbase + 64, g4 * 2048 + ka:g4 * 2048 + ka + 48],
                         start=True, stop=True)
                t.matmul(p.psS[:, 48:176],
                         qf, p.W[qbase:qbase + 64, g4 * 2048 + qi * 128:g4 * 2048 + qi * 128 + 128],
                         start=True, stop=True)
                t.matmul(p.psS[:, 176:220],
                         qf, p.W[qbase:qbase + 64, g4 * 2048 + 45:g4 * 2048 + 2024:46],
                         start=True, stop=True)
                mb_off = 1280 if qi == 0 else 1456
                t.matmul(p.psS[:, 0:176], p.ident[:], Eb[:, mb_off:mb_off + 176],
                         start=False, stop=True)
                last = t.matmul(p.psS[:, 176:220], p.ident[:], Eb[:, 1664:1708],
                                start=False, stop=True)
                p.sMM.inc(last)

            @p.raw_section("vector")
            def _(v):
                v.wait_ge(p.sMM.h, p.sMM.n)
                v.reduce_max(out=nmax, in_=p.psS[:, 0:220], axis=X, negate=True)
                p.dinc(v, p.sEV)

            @p.raw_section("scalar")
            def _(s):
                s.wait_ge(p.sEV.h, p.sEV.n)
                s.activation(out=Eb[:, 0:220], in_=p.psS[:, 0:220], func=AF.Exp,
                             bias=nmax, accum_out=sums)
                p.dinc(s, p.sAC)

            @p.raw_section("vector")
            def _(v):
                v.wait_ge(p.sAC.h, p.sAC.n)
                v.reciprocal(rinv, sums)
                p.dinc(v, p.sEV)

            @p.raw_section("tensor")
            def _(t):
                t.wait_ge(p.sAC.h, p.sAC.n)
                t.transpose(p.psT[0:48, 0:128], Eb[:, 0:48], p.ident[:])
                t.transpose(p.psT[:, 128:256], Eb[:, 48:176], p.ident[:])
                i = t.transpose(p.psT[0:44, 256:384], Eb[:, 176:220], p.ident[:])
                p.sMM.inc(i)

            @p.raw_section("vector")
            def _(v):
                v.wait_ge(p.sMM.h, p.sMM.n)
                v.tensor_copy(Eb[0:48, 256:384], p.psT[0:48, 0:128])
                v.tensor_copy(Eb[:, 384:512], p.psT[:, 128:256])
                v.tensor_copy(Eb[0:44, 512:640], p.psT[0:44, 256:384])
                p.dinc(v, p.sEV)

            @p.raw_section("tensor")
            def _(t, qi=qi, g4=g4):
                t.wait_ge(p.sEV.h, p.sEV.n)
                vd = g4 * 64
                t.matmul(p.psA[:, 0:64], Eb[0:48, 256:384],
                         Eb[0:48, 640 + vd:640 + vd + 64], start=True, stop=False)
                t.matmul(p.psA[:, 0:64], Eb[:, 384:512],
                         Eb[:, 4096 + qi * 256 + vd:4096 + qi * 256 + vd + 64],
                         start=False, stop=False)
                i = t.matmul(p.psA[:, 0:64], Eb[0:44, 512:640],
                             Eb[0:44, 896 + vd:896 + vd + 64], start=False, stop=True)
                p.sMM.inc(i)

            @p.raw_section("vector")
            def _(v, qi=qi, h=h):
                v.wait_ge(p.sMM.h, p.sMM.n)
                v.tensor_scalar_mul(p.A[:, qi * 1024 + h * 64:qi * 1024 + h * 64 + 64],
                                    p.psA[:, 0:64], rinv)
                v.drain()
                p.dinc(v, p.sEV)
                p.T.inc(v.drain())

    emit_transpose(p, dst=p.A, dst_off=XN, src=p.A, src_off=0)
    emit_weight_load(p, WoT, p.W, 0, D, D)
    emit_proj_residual(p, src_fm=(p.A, XN), w=(p.W, 0), n_in=D, n_out=D,
                       wait_dw=p.dW.n)


def emit_l3_ffn(p):
    w1 = p.din("L3_ffn_w1", [8, D, 4096])
    b1 = p.din("L3_ffn_b1", [8, 4096])
    w2 = p.din("L3_ffn_w2", [8, 4096, D])
    b2 = p.din("L3_ffn_b2", [8, D])
    gw = p.din("L3_ffn_gate_w", [D, 8])
    gb = p.din("L3_ffn_gate_b", [8])

    XN = 16384
    emit_ln(p, dst=p.A, dst_off=0)
    emit_transpose(p, dst=p.A, dst_off=XN, src=p.A, src_off=0)

    emit_weight_load(p, gw, p.W, 0, D, 8)
    emit_row_load(p, gb, 0, 8)
    emit_proj_tm(p, dst=p.E, dst_off=0, src_fm=p.A, src_off=XN, w_sb=p.W, w_off=0,
                 n_in=D, n_out=8, bias_off=0, wait_dw=p.dW.n, evac="vector",
                 dst_stride=8)

    m1 = p.ST[:, 384:400]
    CW = 256

    @p.section("vector")
    def _(v):
        lg = p.E[:, 0:128].rearrange("p (t e) -> p t e", e=8)
        cw = p.ST[:, 256:384].rearrange("p (t e) -> p t e", e=8)
        v.tensor_reduce(out=m1, in_=lg, axis=X, op=OP.max)
        v.drain()
        v.tensor_tensor(out=cw, in0=lg, in1=m1.unsqueeze(2).broadcast_to([128, 16, 8]),
                        op=OP.is_equal)
        return v.drain()

    for e in range(8):
        emit_moe_expert(p, e, w1, b1, w2, b2, xn_off=XN,
                        cw_col=lambda ti, e=e: CW + ti * 8 + e,
                        act=AF.Relu, add_xf=False)


def stage_host(inputs):
    H = dict(host_constants())
    x = np.ascontiguousarray(np.asarray(inputs["L0_attn_x"], dtype=np.float32))
    H["xin"] = [np.ascontiguousarray(x[c]) for c in range(NCORES)]
    for k, v in inputs.items():
        if k != "L0_attn_x":
            H[k] = np.ascontiguousarray(np.asarray(v, dtype=np.float32))
    H["L0_ffn_router_wT"] = np.ascontiguousarray(H["L0_ffn_router_w"].T)
    H["c_ident_f32"] = np.eye(128, dtype=np.float32)
    H["c_ones_col"] = np.ones((128, 1), dtype=np.float32)
    H["L1_attn_in_proj_weightT"] = np.ascontiguousarray(H["L1_attn_in_proj_weight"].T)
    H["L1_attn_out_wT"] = np.ascontiguousarray(H["L1_attn_out_w"].T)
    H["L1_ffn_W1T"] = np.ascontiguousarray(H["L1_ffn_W1"].T)
    H["L1_ffn_W2T"] = np.ascontiguousarray(H["L1_ffn_W2"].T)
    rpk = H["L1_attn_rel_pos_keys"]
    H["rho_c"] = [np.ascontiguousarray(np.vstack([rpk[np.arange(8) - c + 16, :64].T]*2)) for c in range(NCORES)]
    H["L2_attn_dist_scale2d"] = np.asarray(H["L2_attn_dist_scale"], dtype=np.float32).reshape(1, 1)
    a128 = np.arange(128); b512 = np.arange(512)
    dist = np.empty((28, 128, 512), dtype=np.float32)
    for r in range(28):
        dist[r] = np.abs((r * 128 - 1536) + a128[:, None] - b512[None, :]).astype(np.float32)
    H["c_dist"] = dist
    H["L2_ffn_gammaP"] = np.ascontiguousarray(
        H["L2_ffn_gamma"].reshape(8, 8, 128).transpose(2, 0, 1).reshape(128, 64))
    H["L2_ffn_betaP"] = np.ascontiguousarray(
        H["L2_ffn_beta"].reshape(8, 8, 128).transpose(2, 0, 1).reshape(128, 64))
    H["L3_attn_WqT"] = np.ascontiguousarray(H["L3_attn_Wq"].T)
    H["L3_attn_WkT"] = np.ascontiguousarray(H["L3_attn_Wk"].T)
    H["L3_attn_WvT"] = np.ascontiguousarray(H["L3_attn_Wv"].T)
    H["L3_attn_WoT"] = np.ascontiguousarray(H["L3_attn_Wo"].T)
    a = np.arange(128)[:, None]
    j = np.arange(176)[None, :]
    rel = j - 48 - a
    base_valid = (rel >= -45) & (rel <= 0)
    m1v = np.where(base_valid, 0.0, NEG).astype(np.float32)
    m0v = np.where(base_valid & (j >= 48), 0.0, NEG).astype(np.float32)
    H["c_l3_bandmask"] = np.stack([m0v, m1v]).astype(BF)
    sidx = np.arange(45, 2048, 46)
    sm = np.empty((NT, 128, 44), dtype=np.float32)
    for qi in range(NT):
        q = qi * 128 + np.arange(128)[:, None]
        sm[qi] = np.where(sidx[None, :] <= q - 46, 0.0, NEG)
    H["c_l3_stridemask"] = sm.astype(BF)
    return H


def run_launch(p, H):
    in_maps = []
    for c in range(NCORES):
        m = {}
        for name in p.inputs:
            v = H[name]
            m[name] = v[c] if isinstance(v, list) else v
        in_maps.append(m)
    res = run_bass_kernel_spmd(p.nc, in_maps, core_ids=list(range(NCORES)))
    return res.results


LAYER_EMITTERS = [emit_l0_attn, emit_l0_ffn, emit_l1_attn, emit_l1_ffn,
                  emit_l2_attn, emit_l2_ffn, emit_l3_attn, emit_l3_ffn]


def build_layer_program(emitters):
    p = Prog()
    p.setup()
    emit_load_consts(p)
    emit_load_x(p)
    for em in emitters:
        em(p)
    emit_store_x(p)
    p.ctx.close()
    return p


def kernel(**inputs):
    H = stage_host(inputs)
    x = H["xin"]
    for em in LAYER_EMITTERS:
        p = build_layer_program([em])
        H["xin"] = x
        results = run_launch(p, H)
        x = [np.ascontiguousarray(results[c]["xout"]) for c in range(NCORES)]
    return np.stack(x, axis=0).astype(np.float32)


# version 02
# speedup vs baseline: 1.0056x; 1.0056x over previous
import numpy as np
import ml_dtypes
from contextlib import ExitStack

import concourse.bass as bass
import concourse.mybir as mybir
from concourse.bass_utils import run_bass_kernel_spmd

F32 = mybir.dt.float32
BF16 = mybir.dt.bfloat16
AF = mybir.ActivationFunctionType
OP = mybir.AluOpType
BF = ml_dtypes.bfloat16

NCORES = 8
T = 2048
D = 1024
NT = 16
NEG = -1.0e30
X = mybir.AxisListType.X


def host_constants():
    c = {}
    c["c_ident_bf"] = np.eye(128, dtype=np.float32).astype(BF)
    c["c_ones_row"] = np.ones((1, 128), dtype=BF)
    half = 32
    ang = (np.arange(T, dtype=np.float32)[:, None] *
           (1.0 / (10000.0 ** (np.arange(half, dtype=np.float32) / np.float32(half)))
            ).astype(np.float32)[None, :])
    c["c_rope_sin"] = np.sin(ang).astype(np.float32)
    c["c_rope_cos"] = np.cos(ang).astype(np.float32)
    a = np.arange(128)
    c["c_causal128"] = np.where(a[:, None] >= a[None, :], 0.0, NEG).astype(np.float32).astype(BF)
    return c


class Sem:
    def __init__(self, prog, name):
        self.h = prog.ctx.enter_context(prog.nc.semaphore(name))
        self.n = 0

    def inc(self, inst, k=1):
        inst.then_inc(self.h, k)
        self.n += k
        return self.n

    def wait(self, eng, val=None):
        eng.wait_ge(self.h, self.n if val is None else val)


class Prog:

    def __init__(self):
        self.nc = bass.Bass("TRN2", target_bir_lowering=False)
        self.ctx = ExitStack()
        self.inputs = {}
        self.outputs = {}
        self._sb = {}

    def din(self, name, shape, dtype=F32):
        if name not in self.inputs:
            self.inputs[name] = self.nc.dram_tensor(name, list(shape), dtype,
                                                    kind="ExternalInput").ap()
        return self.inputs[name]

    def dout(self, name, shape, dtype=F32):
        if name not in self.outputs:
            self.outputs[name] = self.nc.dram_tensor(name, list(shape), dtype,
                                                     kind="ExternalOutput").ap()
        return self.outputs[name]

    def dint(self, name, shape, dtype=F32, addr_space="Local"):
        key = "int_" + name
        if key not in self._sb:
            self._sb[key] = self.nc.dram_tensor(name, list(shape), dtype,
                                                kind="Internal", addr_space=addr_space).ap()
        return self._sb[key]

    def sb(self, name, shape, dtype):
        if name not in self._sb:
            self._sb[name] = self.ctx.enter_context(
                self.nc.sbuf_tensor(name, list(shape), dtype))
        return self._sb[name]

    def ps(self, name, shape, dtype=F32):
        if name not in self._sb:
            self._sb[name] = self.ctx.enter_context(
                self.nc.psum_tensor(name, list(shape), dtype))
        return self._sb[name]

    def setup(self):
        self.block = self.ctx.enter_context(self.nc.Block())
        self.T = Sem(self, "Tl")
        self.dW = Sem(self, "dW")
        self.sMM = Sem(self, "sMM")
        self.sEV = Sem(self, "sEV")
        self.sAC = Sem(self, "sAC")
        self.sGP = Sem(self, "sGP")
        self.sCC = Sem(self, "sCC")
        self.xres = self.sb("xres", [128, NT * D], F32)
        self.A = self.sb("arenaA", [128, 32768], BF16)
        self.B = self.sb("arenaB", [128, 16384], BF16)
        self.E = self.sb("arenaE", [128, 4096], F32)
        self.ST = self.sb("stats", [128, 512], F32)
        self.W = self.sb("wstream", [128, 8192], BF16)
        self.Rb = self.sb("rowbuf", [1, 2048], BF16)
        self.ident = self.sb("identbf", [128, 128], BF16)
        self.onesb = self.sb("onesrow", [1, 128], BF16)
        self.CS = self.sb("cossin", [128, 1024], F32)
        self.psS = self.ps("psS", [128, 2048], F32)
        self.psA = self.ps("psA", [128, 512], F32)
        self.psB = self.ps("psB", [128, 512], F32)
        self.psC = self.ps("psC", [128, 512], F32)
        self.psT = self.ps("psT", [128, 1024], BF16)

    def wait_all(self, eng):
        eng.wait_ge(self.T.h, self.T.n)

    def dinc(self, eng, sem, k=1):
        sem.inc(eng.drain(), k)

    def section(self, engname):
        def deco(fn):
            @getattr(self.block, engname)
            def _(eng):
                self.wait_all(eng)
                last = fn(eng)
                if last is not None:
                    if engname in ("sync", "gpsimd"):
                        self.T.inc(last, 16)
                    elif engname == "tensor":
                        self.T.inc(last, 1)
                    else:
                        self.dinc(eng, self.T)
        return deco

    def raw_section(self, engname):
        def deco(fn):
            @getattr(self.block, engname)
            def _(eng):
                fn(eng)
        return deco


def emit_load_consts(p):
    ib = p.din("c_ident_bf", [128, 128], BF16)
    orow = p.din("c_ones_row", [1, 128], BF16)

    @p.section("sync")
    def _(s):
        i = s.dma_start(out=p.ident[:], in_=ib[:])
        p.T.inc(i, 16)
        return s.dma_start(out=p.onesb[:], in_=orow[:])


def emit_load_x(p, src_name="xin"):
    xin = p.din(src_name, [T, D], F32)

    @p.section("sync")
    def _(s):
        return s.dma_start(out=p.xres[:].rearrange("p (t d) -> p t d", d=D),
                           in_=xin.rearrange("(t p) d -> p t d", p=128))


def emit_store_x(p, dst_name="xout"):
    xout = p.dout(dst_name, [T, D], F32)

    @p.raw_section("gpsimd")
    def _(g):
        p.wait_all(g)
        i = g.dma_start(out=xout.rearrange("(t p) d -> p t d", p=128),
                        in_=p.xres[:].rearrange("p (t d) -> p t d", d=D))
        p.T.inc(i, 16)
        g.wait_ge(p.T.h, p.T.n)


def emit_ln(p, dst, dst_off, width=D, src=None, src_off=0, eps=1e-5,
            ntiles=NT, src_stride=None, dst_stride=None, stat_col=0,
            scratch=None, affine=None):
    src = src if src is not None else p.xres
    src_stride = src_stride if src_stride is not None else width
    dst_stride = dst_stride if dst_stride is not None else width
    scratch = scratch if scratch is not None else p.E[:, 0:width]
    mean = p.ST[:, stat_col:stat_col + ntiles]
    rstd = p.ST[:, stat_col + ntiles:stat_col + 2 * ntiles]
    sumsq = p.ST[:, stat_col + 2 * ntiles:stat_col + 3 * ntiles]

    @p.section("vector")
    def _(v):
        for ti in range(ntiles):
            st = src[:, src_off + ti * src_stride: src_off + ti * src_stride + width]
            v.tensor_reduce(out=mean[:, ti:ti + 1], in_=st, axis=X, op=OP.add)
            v.tensor_mul(scratch, st, st)
            v.tensor_reduce(out=sumsq[:, ti:ti + 1], in_=scratch, axis=X, op=OP.add)
        v.tensor_scalar_mul(mean[:], mean[:], 1.0 / width)
        v.tensor_scalar_mul(sumsq[:], sumsq[:], 1.0 / width)
        v.drain()
        v.tensor_tensor(out=rstd[:], in0=mean[:], in1=mean[:], op=OP.mult)
        v.drain()
        v.tensor_sub(rstd[:], sumsq[:], rstd[:])
        v.drain()
        return v.tensor_scalar_add(rstd[:], rstd[:], float(eps))

    @p.section("scalar")
    def _(s):
        return s.sqrt(rstd[:], rstd[:])

    @p.section("vector")
    def _(v):
        v.reciprocal(rstd[:], rstd[:])
        v.drain()
        last = None
        for ti in range(ntiles):
            st = src[:, src_off + ti * src_stride: src_off + ti * src_stride + width]
            dt_ = dst[:, dst_off + ti * dst_stride: dst_off + ti * dst_stride + width]
            if affine is None:
                last = v.tensor_scalar(out=dt_, in0=st, scalar1=mean[:, ti:ti + 1],
                                       scalar2=rstd[:, ti:ti + 1],
                                       op0=OP.subtract, op1=OP.mult)
            else:
                g_ap, b_ap = affine
                v.tensor_scalar(out=scratch, in0=st, scalar1=mean[:, ti:ti + 1],
                                scalar2=rstd[:, ti:ti + 1],
                                op0=OP.subtract, op1=OP.mult)
                v.tensor_mul(scratch, scratch, g_ap)
                last = v.tensor_tensor(out=dt_, in0=scratch, in1=b_ap, op=OP.add)
        return last


def emit_transpose(p, dst, dst_off, src, src_off, ntiles=NT, width=D,
                   src_stride=None, dst_stride=None):
    src_stride = src_stride if src_stride is not None else width
    dst_stride = dst_stride if dst_stride is not None else ntiles * 128
    nch = width // 128
    groups = []
    for c in range(nch):
        for g0 in range(0, ntiles, 8):
            groups.append((c, g0, min(8, ntiles - g0)))

    @p.raw_section("tensor")
    def _(t):
        p.wait_all(t)
        ev0 = p.sEV.n
        for gi, (c, g0, gl) in enumerate(groups):
            if gi >= 1:
                t.wait_ge(p.sEV.h, ev0 + gi)
            last = None
            for k in range(gl):
                ti = g0 + k
                sc = src_off + ti * src_stride + c * 128
                last = t.transpose(p.psT[:, k * 128:(k + 1) * 128],
                                   src[:, sc:sc + 128], p.ident[:])
            p.sMM.inc(last)

    @p.raw_section("vector")
    def _(v):
        base_mm = p.sMM.n - len(groups)
        last = None
        for gi, (c, g0, gl) in enumerate(groups):
            v.wait_ge(p.sMM.h, base_mm + gi + 1)
            dc = dst_off + c * dst_stride + g0 * 128
            last = v.tensor_copy(dst[:, dc:dc + gl * 128], p.psT[:, 0:gl * 128])
            p.sEV.inc(last)
        p.T.inc(v.drain())


def emit_weight_load(p, w_dram, dst, dst_off, n_in, n_out, in_off=0, out_off=0):
    @p.raw_section("gpsimd")
    def _(g):
        g.wait_ge(p.T.h, p.T.n)
        src = w_dram[in_off:in_off + n_in, out_off:out_off + n_out]
        i = g.dma_start(out=dst[:, dst_off:dst_off + (n_in // 128) * n_out]
                        .rearrange("p (c o) -> p c o", o=n_out),
                        in_=src.rearrange("(c p) o -> p c o", p=128))
        p.dW.inc(i, 16)


def emit_row_load(p, dram_vec, dst_off, n, src_off=0):
    @p.raw_section("gpsimd")
    def _(g):
        g.wait_ge(p.T.h, p.T.n)
        i = g.dma_start(out=p.Rb[0:1, dst_off:dst_off + n],
                        in_=dram_vec[src_off:src_off + n].unsqueeze(0))
        p.dW.inc(i, 16)


def emit_proj_tm(p, dst, dst_off, src_fm, src_off, w_sb, w_off, n_in, n_out,
                 bias_off=None, act=None, scale=1.0, ntiles=NT,
                 src_stride=None, dst_stride=None, wait_dw=None, evac="scalar"):
    src_stride = src_stride if src_stride is not None else ntiles * 128
    dst_stride = dst_stride if dst_stride is not None else n_out
    nkc = n_in // 128
    nog = (n_out + 511) // 512
    banks = [p.psA, p.psB]
    njobs = ntiles * nog
    sem = p.sEV if evac == "vector" else p.sAC

    @p.raw_section("tensor")
    def _(t):
        p.wait_all(t)
        if wait_dw is not None:
            t.wait_ge(p.dW.h, wait_dw)
        ev0 = sem.n
        job = 0
        for ti in range(ntiles):
            for og in range(nog):
                ow = min(512, n_out - og * 512)
                ps = banks[job % 2]
                if job >= 2:
                    t.wait_ge(sem.h, ev0 + job - 1)
                last = None
                for kc in range(nkc):
                    last = t.matmul(ps[:, 0:ow],
                                    src_fm[:, src_off + kc * src_stride + ti * 128:
                                           src_off + kc * src_stride + ti * 128 + 128],
                                    w_sb[:, w_off + kc * n_out + og * 512:
                                         w_off + kc * n_out + og * 512 + ow],
                                    start=(kc == 0),
                                    stop=(kc == nkc - 1 and bias_off is None))
                if bias_off is not None:
                    last = t.matmul(ps[:, 0:ow], p.onesb[:],
                                    p.Rb[0:1, bias_off + og * 512:bias_off + og * 512 + ow],
                                    start=False, stop=True)
                p.sMM.inc(last)
                job += 1

    engname = "vector" if evac == "vector" else "scalar"

    @p.raw_section(engname)
    def _(e):
        base_mm = p.sMM.n - njobs
        job = 0
        last = None
        for ti in range(ntiles):
            for og in range(nog):
                ow = min(512, n_out - og * 512)
                ps = banks[job % 2]
                e.wait_ge(p.sMM.h, base_mm + job + 1)
                dcol = dst_off + ti * dst_stride + og * 512
                if evac == "vector":
                    if scale == 1.0:
                        last = e.tensor_copy(dst[:, dcol:dcol + ow], ps[:, 0:ow])
                    else:
                        last = e.tensor_scalar_mul(dst[:, dcol:dcol + ow], ps[:, 0:ow], scale)
                else:
                    last = e.activation(out=dst[:, dcol:dcol + ow], in_=ps[:, 0:ow],
                                        func=(act or AF.Copy), scale=scale)
                sem.inc(last)
                job += 1
        p.T.inc(e.drain())


def emit_proj_residual(p, src_fm, w, n_in, n_out, wait_dw, bias_off=None, ntiles=NT,
                       dst=None):
    sbuf, soff = src_fm
    wbuf, woff = w
    if dst is None:
        dbuf, doff, dstride = p.xres, 0, D
    else:
        dbuf, doff, dstride = dst
    nkc = n_in // 128
    banks = [p.psA, p.psB]
    nog = n_out // 512
    njobs = ntiles * nog

    @p.raw_section("tensor")
    def _(t):
        p.wait_all(t)
        t.wait_ge(p.dW.h, wait_dw)
        ev0 = p.sEV.n
        job = 0
        for ti in range(ntiles):
            for og in range(nog):
                ps = banks[job % 2]
                if job >= 2:
                    t.wait_ge(p.sEV.h, ev0 + job - 1)
                last = None
                for kc in range(nkc):
                    last = t.matmul(ps[:], sbuf[:, soff + kc * (ntiles * 128) + ti * 128:
                                                soff + kc * (ntiles * 128) + ti * 128 + 128],
                                    wbuf[:, woff + kc * n_out + og * 512:
                                         woff + kc * n_out + og * 512 + 512],
                                    start=(kc == 0),
                                    stop=(kc == nkc - 1 and bias_off is None))
                if bias_off is not None:
                    last = t.matmul(ps[:], p.onesb[:],
                                    p.Rb[0:1, bias_off + og * 512:bias_off + og * 512 + 512],
                                    start=False, stop=True)
                p.sMM.inc(last)
                job += 1

    @p.raw_section("vector")
    def _(v):
        base = p.sMM.n - njobs
        job = 0
        last = None
        for ti in range(ntiles):
            for og in range(nog):
                ps = banks[job % 2]
                v.wait_ge(p.sMM.h, base + job + 1)
                xs = dbuf[:, doff + ti * dstride + og * 512: doff + ti * dstride + og * 512 + 512]
                last = v.tensor_add(xs, xs, ps[:])
                p.sEV.inc(last)
                job += 1
        p.T.inc(v.drain())


def emit_rope_quarter(p, src, src_off, dst, dst_off, width=256):
    scr = p.ST[:, 112:368]
    nh = width // 64

    @p.section("vector")
    def _(v):
        last = None
        for ti in range(NT):
            q = src[:, src_off + ti * width: src_off + (ti + 1) * width]
            qh = q.rearrange("p (h d) -> p h d", d=64)
            q1 = qh[:, :, 0:32]
            q2 = qh[:, :, 32:64]
            o = dst[:, dst_off + ti * width: dst_off + (ti + 1) * width]
            oh = o.rearrange("p (h d) -> p h d", d=64)
            o1 = oh[:, :, 0:32]
            o2 = oh[:, :, 32:64]
            cos = p.CS[:, ti * 32:(ti + 1) * 32].unsqueeze(1).broadcast_to([128, nh, 32])
            sin = p.CS[:, 512 + ti * 32:512 + (ti + 1) * 32].unsqueeze(1).broadcast_to([128, nh, 32])
            t1 = scr[:, 0:nh * 32].rearrange("p (h d) -> p h d", d=32)
            t2 = scr[:, 128:128 + nh * 32].rearrange("p (h d) -> p h d", d=32)
            v.tensor_tensor(out=t1, in0=q1, in1=cos, op=OP.mult)
            v.tensor_tensor(out=t2, in0=q2, in1=sin, op=OP.mult)
            v.tensor_tensor(out=o1, in0=t1, in1=t2, op=OP.subtract)
            v.tensor_tensor(out=t1, in0=q1, in1=sin, op=OP.mult)
            v.tensor_tensor(out=t2, in0=q2, in1=cos, op=OP.mult)
            last = v.tensor_tensor(out=o2, in0=t1, in1=t2, op=OP.add)
        return last


def emit_dump(p, name, buf, off, cols, dtype=BF16):
    d = p.dout(name, [128, cols], dtype)

    @p.raw_section("gpsimd")
    def _(g):
        p.wait_all(g)
        i = g.dma_start(out=d[:], in_=buf[:, off:off + cols])
        p.T.inc(i, 16)

def emit_l0_attn(p):
    Wq = p.din("L0_attn_W_q", [D, D])
    Wdkv = p.din("L0_attn_W_dkv", [D, 256])
    Wuk = p.din("L0_attn_W_uk", [256, D])
    Wuv = p.din("L0_attn_W_uv", [256, D])
    Wo = p.din("L0_attn_W_o", [D, D])
    lng = p.din("L0_attn_ln_gamma", [256])
    lnb = p.din("L0_attn_ln_beta", [256])
    cosd = p.din("c_rope_cos", [T, 32])
    sind = p.din("c_rope_sin", [T, 32])
    causd = p.din("c_causal128", [128, 128], BF16)

    XN = 16384
    COS, SIN, CAUS = 3072, 3584, 4096
    GAM = 4352
    emit_ln(p, dst=p.A, dst_off=0)
    emit_transpose(p, dst=p.A, dst_off=XN, src=p.A, src_off=0)

    emit_weight_load(p, Wdkv, p.W, 0, D, 256)
    emit_proj_tm(p, dst=p.E, dst_off=0, src_fm=p.A, src_off=XN, w_sb=p.W, w_off=0,
                 n_in=D, n_out=256, wait_dw=p.dW.n, evac="vector", dst_stride=256)

    mean = p.ST[:, 64:80]
    rstd = p.ST[:, 80:96]
    sumsq = p.ST[:, 96:112]
    scr = p.ST[:, 112:368]

    @p.section("vector")
    def _(v):
        for ti in range(NT):
            st = p.E[:, ti * 256:(ti + 1) * 256]
            v.tensor_reduce(out=mean[:, ti:ti + 1], in_=st, axis=X, op=OP.add)
            v.tensor_mul(scr, st, st)
            v.tensor_reduce(out=sumsq[:, ti:ti + 1], in_=scr, axis=X, op=OP.add)
        v.tensor_scalar_mul(mean[:], mean[:], 1.0 / 256)
        v.tensor_scalar_mul(sumsq[:], sumsq[:], 1.0 / 256)
        v.drain()
        v.tensor_tensor(out=rstd[:], in0=mean[:], in1=mean[:], op=OP.mult)
        v.drain()
        v.tensor_sub(rstd[:], sumsq[:], rstd[:])
        v.drain()
        return v.tensor_scalar_add(rstd[:], rstd[:], 1e-5)

    @p.section("scalar")
    def _(s):
        return s.sqrt(rstd[:], rstd[:])

    @p.section("vector")
    def _(v):
        v.reciprocal(rstd[:], rstd[:])
        v.drain()
        last = None
        for ti in range(NT):
            last = v.tensor_scalar(out=p.B[:, ti * 256:(ti + 1) * 256],
                                   in0=p.E[:, ti * 256:(ti + 1) * 256],
                                   scalar1=mean[:, ti:ti + 1], scalar2=rstd[:, ti:ti + 1],
                                   op0=OP.subtract, op1=OP.mult)
        return last

    emit_transpose(p, dst=p.B, dst_off=4096, src=p.B, src_off=0, width=256)
    LATF = 4096

    gam = p.ST[:, 374:376]

    BET = 4224

    @p.raw_section("gpsimd")
    def _(g):
        g.wait_ge(p.T.h, p.T.n)
        i = g.dma_start(out=gam, in_=lng.rearrange("(c p) -> p c", p=128),
                        allow_slow_non_contiguous=True)
        p.dW.inc(i, 16)
        i = g.dma_start(out=p.W[:, BET:BET + 2], in_=lnb.rearrange("(c p) -> p c", p=128),
                        allow_slow_non_contiguous=True)
        p.dW.inc(i, 16)
        i = g.dma_start(out=p.CS[:, 0:512].rearrange("p (t f) -> p t f", f=32),
                        in_=cosd.rearrange("(t p) f -> p t f", p=128))
        p.dW.inc(i, 16)
        i = g.dma_start(out=p.CS[:, 512:1024].rearrange("p (t f) -> p t f", f=32),
                        in_=sind.rearrange("(t p) f -> p t f", p=128))
        p.dW.inc(i, 16)
        i = g.dma_start(out=p.W[:, CAUS:CAUS + 128], in_=causd[:])
        p.dW.inc(i, 16)

    bet = p.W[:, BET:BET + 2]

    for qq in range(4):
        hsl = qq * 256
        emit_weight_load(p, Wq, p.W, 0, D, 256, out_off=hsl)
        emit_weight_load(p, Wuk, p.W, 2048, 256, 256, out_off=hsl)
        emit_weight_load(p, Wuv, p.W, 2560, 256, 256, out_off=hsl)
        dw_after = p.dW.n

        @p.raw_section("tensor")
        def _(t, dw_after=dw_after):
            t.wait_ge(p.dW.h, dw_after)
            for j, woff in enumerate((2048, 2560)):
                last = None
                for c in range(2):
                    last = t.matmul(p.psC[0:1, j * 256:(j + 1) * 256],
                                    bet[:, c:c + 1],
                                    p.W[:, woff + c * 256:woff + (c + 1) * 256],
                                    start=(c == 0), stop=(c == 1))
                p.sMM.inc(last)

        @p.raw_section("vector")
        def _(v):
            v.wait_ge(p.sMM.h, p.sMM.n)
            v.tensor_copy(p.Rb[0:1, 0:512], p.psC[0:1, 0:512])
            for woff in (2048, 2560):
                for c in range(2):
                    v.tensor_scalar_mul(p.W[:, woff + c * 256:woff + (c + 1) * 256],
                                        p.W[:, woff + c * 256:woff + (c + 1) * 256],
                                        gam[:, c:c + 1])
            p.dinc(v, p.T)

        emit_proj_tm(p, dst=p.E, dst_off=0, src_fm=p.A, src_off=XN, w_sb=p.W, w_off=0,
                     n_in=D, n_out=256, scale=0.125, wait_dw=dw_after,
                     dst_stride=256, evac="vector")
        emit_rope_quarter(p, src=p.E, src_off=0, dst=p.A, dst_off=0)
        emit_transpose(p, dst=p.A, dst_off=4096, src=p.A, src_off=0, width=256)
        emit_proj_tm(p, dst=p.E, dst_off=0, src_fm=p.B, src_off=LATF, w_sb=p.W,
                     w_off=2048, n_in=256, n_out=256, bias_off=0, dst_stride=256,
                     evac="vector")
        emit_rope_quarter(p, src=p.E, src_off=0, dst=p.A, dst_off=8192)
        emit_transpose(p, dst=p.A, dst_off=12288, src=p.A, src_off=8192, width=256)
        emit_proj_tm(p, dst=p.B, dst_off=8192, src_fm=p.B, src_off=LATF, w_sb=p.W,
                     w_off=2560, n_in=256, n_out=256, bias_off=256, dst_stride=256)
        emit_causal_attn_quarter(p, qfm=(p.A, 4096), kfm=(p.A, 12288),
                                 vtm=(p.B, 8192), dst=(p.B, 0),
                                 caus=(p.W, CAUS))
        emit_transpose(p, dst=p.B, dst_off=12288, src=p.B, src_off=0, width=256)
        emit_weight_load(p, Wo, p.W, 4608, 256, D, in_off=256 * qq)
        emit_proj_residual(p, src_fm=(p.B, 12288), w=(p.W, 4608), n_in=256, n_out=D,
                           wait_dw=p.dW.n)


def emit_causal_attn_quarter(p, qfm, kfm, vtm, dst, caus):
    qbuf, qoff = qfm
    kbuf, koff = kfm
    vbuf, voff = vtm
    dbuf, doff = dst
    cbuf, coff = caus
    P_off = 12288
    PT_off = 14336
    nmax = p.ST[:, 368:369]
    sums = p.ST[:, 369:373]
    rinv = p.ST[:, 373:374]

    for h in range(4):
        hd = h * 64
        ch = hd // 128
        hin = hd % 128
        for qi in range(NT):
            nk = (qi + 1) * 128
            nb = (nk + 511) // 512

            @p.raw_section("tensor")
            def _(t, qi=qi, nk=nk, nb=nb, ch=ch, hin=hin):
                p.wait_all(t)
                for b in range(nb):
                    kw = min(512, nk - b * 512)
                    t.matmul(p.psS[:, b * 512:b * 512 + kw],
                             qbuf[hin:hin + 64, qoff + ch * 2048 + qi * 128:
                                  qoff + ch * 2048 + qi * 128 + 128],
                             kbuf[hin:hin + 64, koff + ch * 2048 + b * 512:
                                  koff + ch * 2048 + b * 512 + kw],
                             start=True, stop=(b != nb - 1))
                last = t.matmul(p.psS[:, nk - 128:nk], p.ident[:],
                                cbuf[:, coff:coff + 128], start=False, stop=True)
                p.sMM.inc(last)

            @p.raw_section("vector")
            def _(v, nk=nk):
                v.wait_ge(p.sMM.h, p.sMM.n)
                v.reduce_max(out=nmax, in_=p.psS[:, 0:nk], axis=X, negate=True)
                p.dinc(v, p.sEV)

            @p.raw_section("scalar")
            def _(s, nk=nk, nb=nb):
                s.wait_ge(p.sEV.h, p.sEV.n)
                last = None
                for b in range(nb):
                    kw = min(512, nk - b * 512)
                    last = s.activation(out=p.B[:, P_off + b * 512:P_off + b * 512 + kw],
                                        in_=p.psS[:, b * 512:b * 512 + kw], func=AF.Exp,
                                        bias=nmax, accum_out=sums[:, b:b + 1])
                p.dinc(s, p.sAC)

            @p.raw_section("vector")
            def _(v, nb=nb):
                v.wait_ge(p.sAC.h, p.sAC.n)
                v.tensor_reduce(out=rinv, in_=sums[:, 0:nb], axis=X, op=OP.add)
                v.drain()
                v.reciprocal(rinv, rinv)
                p.dinc(v, p.sEV)

            nkb = qi + 1
            ngr = (nkb + 7) // 8

            @p.raw_section("tensor")
            def _(t, nkb=nkb, ngr=ngr):
                t.wait_ge(p.sAC.h, p.sAC.n)
                ev0 = p.sEV.n
                for gi, g0 in enumerate(range(0, nkb, 8)):
                    gl = min(8, nkb - g0)
                    if gi > 0:
                        t.wait_ge(p.sEV.h, ev0 + gi)
                    last = None
                    for k in range(gl):
                        last = t.transpose(p.psT[:, k * 128:(k + 1) * 128],
                                           p.B[:, P_off + (g0 + k) * 128:
                                               P_off + (g0 + k) * 128 + 128],
                                           p.ident[:])
                    p.sMM.inc(last)

            @p.raw_section("vector")
            def _(v, nkb=nkb, ngr=ngr):
                base = p.sMM.n - ngr
                for gi, g0 in enumerate(range(0, nkb, 8)):
                    gl = min(8, nkb - g0)
                    v.wait_ge(p.sMM.h, base + gi + 1)
                    v.tensor_copy(p.B[:, PT_off + g0 * 128:PT_off + g0 * 128 + gl * 128],
                                  p.psT[:, 0:gl * 128])
                    p.dinc(v, p.sEV)

            @p.raw_section("tensor")
            def _(t, nkb=nkb, hd=hd):
                t.wait_ge(p.sEV.h, p.sEV.n)
                last = None
                for kb in range(nkb):
                    last = t.matmul(p.psA[:, 0:64],
                                    p.B[:, PT_off + kb * 128:PT_off + kb * 128 + 128],
                                    vbuf[:, voff + kb * 256 + hd:voff + kb * 256 + hd + 64],
                                    start=(kb == 0), stop=(kb == nkb - 1))
                p.sMM.inc(last)

            @p.raw_section("vector")
            def _(v, qi=qi, hd=hd):
                v.wait_ge(p.sMM.h, p.sMM.n)
                v.tensor_scalar_mul(dbuf[:, doff + qi * 256 + hd:doff + qi * 256 + hd + 64],
                                    p.psA[:, 0:64], rinv)
                v.drain()
                p.dinc(v, p.sEV)
                p.T.inc(v.drain())


def emit_l0_ffn(p):
    rT = p.din("L0_ffn_router_wT", [D, 8])
    w1 = p.din("L0_ffn_w1", [8, D, 4096])
    b1 = p.din("L0_ffn_b1", [8, 4096])
    w2 = p.din("L0_ffn_w2", [8, 4096, D])
    b2 = p.din("L0_ffn_b2", [8, D])
    idf = p.din("c_ident_f32", [128, 128], F32)

    XN = 16384
    CAP = 513
    emit_ln(p, dst=p.A, dst_off=0)
    emit_transpose(p, dst=p.A, dst_off=XN, src=p.A, src_off=0)

    emit_weight_load(p, rT, p.W, 0, D, 8)
    @p.raw_section("gpsimd")
    def _(g):
        g.wait_ge(p.T.h, p.T.n)
        i = g.dma_start(out=p.CS[:, 0:128], in_=idf[:])
        p.dW.inc(i, 16)
    IDF = 0

    emit_proj_tm(p, dst=p.E, dst_off=0, src_fm=p.A, src_off=XN, w_sb=p.W, w_off=0,
                 n_in=D, n_out=8, wait_dw=p.dW.n, evac="vector", dst_stride=8)

    @p.section("vector")
    def _(v):
        lg = p.E[:, 0:128].rearrange("p (t e) -> p t e", e=8)
        m = p.ST[:, 128:144]
        v.tensor_reduce(out=m, in_=lg, axis=X, op=OP.max)
        v.drain()
        v.tensor_tensor(out=lg, in0=lg, in1=m.unsqueeze(2).broadcast_to([128, 16, 8]),
                        op=OP.subtract)
        return v.drain()

    @p.section("scalar")
    def _(s):
        return s.activation(out=p.E[:, 0:128], in_=p.E[:, 0:128], func=AF.Exp)

    @p.section("vector")
    def _(v):
        lg = p.E[:, 0:128].rearrange("p (t e) -> p t e", e=8)
        sm = p.ST[:, 144:160]
        v.tensor_reduce(out=sm, in_=lg, axis=X, op=OP.add)
        v.drain()
        v.reciprocal(sm, sm)
        v.drain()
        v.tensor_tensor(out=lg, in0=lg, in1=sm.unsqueeze(2).broadcast_to([128, 16, 8]),
                        op=OP.mult)
        return v.drain()

    @p.raw_section("tensor")
    def _(t):
        p.wait_all(t)
        ev0 = p.sEV.n
        for ti in range(NT):
            if ti >= 1:
                t.wait_ge(p.sEV.h, ev0 + ti)
            i = t.transpose(p.psC[0:8, 0:128],
                            p.E[:, ti * 8:(ti + 1) * 8], p.CS[:, IDF:IDF + 128])
            p.sMM.inc(i)

    @p.raw_section("vector")
    def _(v):
        base = p.sMM.n - NT
        for ti in range(NT):
            v.wait_ge(p.sMM.h, base + ti + 1)
            v.tensor_copy(p.E[0:8, 2048 + ti * 128:2048 + (ti + 1) * 128],
                          p.psC[0:8, 0:128])
            p.dinc(v, p.sEV)
        p.T.inc(v.drain())

    SCF = 2048

    a = p.ST[0:8, 160:161]
    bb = p.ST[0:8, 161:162]
    mid = p.ST[0:8, 162:163]
    cnt = p.ST[0:8, 163:164]
    sel = p.ST[0:8, 164:165]
    d1 = p.ST[0:8, 165:166]
    d2 = p.ST[0:8, 166:167]
    cmpb = p.B[0:8, 0:2048]

    @p.section("vector")
    def _(v):
        v.memset(a, 0.0)
        v.memset(bb, 1.0)
        v.drain()
        for it in range(40):
            v.tensor_tensor(out=mid, in0=a, in1=bb, op=OP.add)
            v.drain()
            v.tensor_scalar_mul(mid, mid, 0.5)
            v.drain()
            v.tensor_single_scalar(out=cmpb, in_=p.E[0:8, SCF:SCF + 2048], scalar=mid,
                                   op=OP.is_ge)
            v.tensor_reduce(out=cnt, in_=cmpb, axis=X, op=OP.add)
            v.drain()
            v.tensor_single_scalar(out=sel, in_=cnt, scalar=float(CAP) - 0.5,
                                   op=OP.is_ge)
            v.drain()
            v.tensor_sub(d1, mid, a)
            v.tensor_sub(d2, bb, mid)
            v.drain()
            v.scalar_tensor_tensor(out=a, in0=d1, scalar=sel, in1=a,
                                   op0=OP.mult, op1=OP.add)
            v.scalar_tensor_tensor(out=bb, in0=d2, scalar=sel, in1=mid,
                                   op0=OP.mult, op1=OP.add)
            v.drain()
        v.tensor_single_scalar(out=cmpb, in_=p.E[0:8, SCF:SCF + 2048], scalar=a,
                               op=OP.is_ge)
        v.drain()
        v.tensor_tensor(out=p.E[0:8, SCF:SCF + 2048], in0=p.E[0:8, SCF:SCF + 2048],
                        in1=cmpb, op=OP.mult)
        return v.drain()

    @p.raw_section("tensor")
    def _(t):
        p.wait_all(t)
        ev0 = p.sEV.n
        for ti in range(NT):
            if ti >= 1:
                t.wait_ge(p.sEV.h, ev0 + ti)
            i = t.transpose(p.psC[:, 0:8],
                            p.E[0:8, SCF + ti * 128:SCF + (ti + 1) * 128],
                            p.CS[0:8, IDF:IDF + 8])
            p.sMM.inc(i)

    @p.raw_section("vector")
    def _(v):
        base = p.sMM.n - NT
        for ti in range(NT):
            v.wait_ge(p.sMM.h, base + ti + 1)
            v.tensor_copy(p.ST[:, 256 + ti * 8:256 + (ti + 1) * 8], p.psC[:, 0:8])
            p.dinc(v, p.sEV)
        p.T.inc(v.drain())

    CW = 256

    for e in range(8):
        emit_moe_expert(p, e, w1, b1, w2, b2, xn_off=XN, cw_col=lambda ti, e=e: CW + ti * 8 + e,
                        act=AF.Gelu, add_xf=False)


def emit_moe_expert(p, e, w1, b1, w2, b2, xn_off, cw_col, act, add_xf,
                    h_scale=1.0, xn_buf=None, h_off=0, h_buf=None,
                    out_dst=None, premul=None):
    xn_buf = xn_buf if xn_buf is not None else p.A
    h_buf = h_buf if h_buf is not None else p.B
    w1e = w1[e] if e is not None else w1
    w2e = w2[e] if e is not None else w2
    b1e = b1[e] if e is not None else b1
    b2e = b2[e] if e is not None else b2
    qbf = (w1e.dtype == BF16)

    @p.raw_section("gpsimd")
    def _(g):
        g.wait_ge(p.T.h, p.T.n)
        i = g.dma_start(out=p.CS[:, 128:160], in_=b1e.rearrange("(c p) -> p c", p=128),
                        allow_slow_non_contiguous=True)
        p.dW.inc(i, 16)
        i = g.dma_start(out=p.Rb[0:1, 0:1024], in_=b2e.unsqueeze(0))
        p.dW.inc(i, 16)
    bias_dw = p.dW.n

    for tg in range(4):
        tgo = tg * 512
        if premul is not None:
            gcol, bcol = premul

            @p.raw_section("vector")
            def _(v, tg=tg):
                p.wait_all(v)
                for dc in range(8):
                    v.tensor_scalar(out=p.W[:, 4096 + dc * 512:4096 + dc * 512 + 512],
                                    in0=xn_buf[:, xn_off + dc * (NT * 128) + tgo:
                                               xn_off + dc * (NT * 128) + tgo + 512],
                                    scalar1=gcol(dc), scalar2=bcol(dc),
                                    op0=OP.mult, op1=OP.add)
                p.dinc(v, p.T)
        for hc in range(32):
            buf = (hc % 2) * 1024

            @p.raw_section("gpsimd")
            def _(g, hc=hc, buf=buf):
                if hc >= 2:
                    g.wait_ge(p.sAC.h, p.l0f_ac0 + hc - 1)
                else:
                    g.wait_ge(p.T.h, p.T.n)
                i = g.dma_start(out=p.W[:, buf:buf + 1024]
                                .rearrange("p (c m) -> p c m", m=128),
                                in_=w1e[:, hc * 128:(hc + 1) * 128]
                                .rearrange("(c p) m -> p c m", p=128))
                p.dW.inc(i, 16)

            if hc == 0:
                p.l0f_ac0 = p.sAC.n

            @p.raw_section("tensor")
            def _(t, hc=hc, buf=buf):
                if hc == 0:
                    p.wait_all(t)
                t.wait_ge(p.dW.h, p.dW.n)
                if hc >= 1:
                    t.wait_ge(p.sAC.h, p.l0f_ac0 + hc)
                last = None
                for dc in range(8):
                    if premul is not None:
                        rhs = p.W[:, 4096 + dc * 512:4096 + dc * 512 + 512]
                    else:
                        rhs = xn_buf[:, xn_off + dc * (NT * 128) + tgo:
                                     xn_off + dc * (NT * 128) + tgo + 512]
                    last = t.matmul(p.psC[:],
                                    p.W[:, buf + dc * 128:buf + dc * 128 + 128],
                                    rhs, start=(dc == 0), stop=(dc == 7))
                p.sMM.inc(last)

            @p.raw_section("scalar")
            def _(s, hc=hc):
                s.wait_ge(p.sMM.h, p.sMM.n)
                if hc == 0:
                    s.wait_ge(p.dW.h, bias_dw)
                s.activation(out=h_buf[:, h_off + hc * 512:h_off + (hc + 1) * 512],
                             in_=p.psC[:],
                             func=act, bias=p.CS[:, 128 + hc:129 + hc], scale=h_scale)
                p.dinc(s, p.sAC)

        for half in range(2):
            tts = (0, 1) if half == 0 else (2, 3)
            if half == 0:
                accs = [p.psS[:, 0:512], p.psS[:, 512:1024],
                        p.psS[:, 1024:1536], p.psS[:, 1536:2048]]
            else:
                accs = [p.psA[:], p.psB[:], p.psC[:], p.psS[:, 0:512]]
            for hc in range(32):
                buf = 2048 + (hc % 2) * 1024

                @p.raw_section("gpsimd")
                def _(g, hc=hc, buf=buf, half=half):
                    if hc >= 2:
                        g.wait_ge(p.sMM.h, p.l0f_mm0 + (hc - 1))
                    elif half == 1 and hc < 2:
                        g.wait_ge(p.sEV.h, p.sEV.n)
                    i = g.dma_start(out=p.W[:, buf:buf + 1024],
                                    in_=w2e[hc * 128:(hc + 1) * 128, :])
                    p.dW.inc(i, 16)

                if hc == 0:
                    p.l0f_mm0 = p.sMM.n

                @p.raw_section("tensor")
                def _(t, hc=hc, buf=buf, tts=tts, accs=accs, half=half):
                    t.wait_ge(p.dW.h, p.dW.n)
                    if hc == 0:
                        t.wait_ge(p.sAC.h, p.sAC.n)
                        if half == 1:
                            t.wait_ge(p.sEV.h, p.sEV.n)
                    last = None
                    for j, tt in enumerate(tts):
                        for oh in range(2):
                            last = t.matmul(accs[j * 2 + oh],
                                            h_buf[:, h_off + hc * 512 + tt * 128:
                                                  h_off + hc * 512 + tt * 128 + 128],
                                            p.W[:, buf + oh * 512:buf + oh * 512 + 512],
                                            start=(hc == 0), stop=(hc == 31))
                    p.sMM.inc(last)

            @p.raw_section("tensor")
            def _(t, tts=tts, accs=accs):
                t.wait_ge(p.dW.h, bias_dw)
                last = None
                for j, tt in enumerate(tts):
                    for oh in range(2):
                        last = t.matmul(accs[j * 2 + oh], p.onesb[:],
                                        p.Rb[0:1, oh * 512:oh * 512 + 512],
                                        start=False, stop=True)
                p.sMM.inc(last)

            @p.raw_section("vector")
            def _(v, tts=tts, accs=accs, tg=tg):
                v.wait_ge(p.sMM.h, p.sMM.n)
                for j, tt in enumerate(tts):
                    ti = tg * 4 + tt
                    for oh in range(2):
                        if out_dst is None:
                            xs = p.xres[:, ti * D + oh * 512: ti * D + oh * 512 + 512]
                        else:
                            obuf, ooff = out_dst
                            xs = obuf[:, ooff + ti * D + oh * 512:
                                      ooff + ti * D + oh * 512 + 512]
                        if cw_col is None:
                            v.tensor_add(xs, xs, accs[j * 2 + oh])
                        else:
                            v.scalar_tensor_tensor(out=xs, in0=accs[j * 2 + oh],
                                                   scalar=p.ST[:, cw_col(ti):cw_col(ti) + 1],
                                                   in1=xs, op0=OP.mult, op1=OP.add)
                p.dinc(v, p.sEV)
                p.T.inc(v.drain())


def emit_l1_attn(p, core_id_dram_names=True):
    WT = p.din("L1_attn_in_proj_weightT", [D, 3 * D])
    bias = p.din("L1_attn_in_proj_bias", [3 * D])
    rho = p.din("rho_c", [128, 8])
    outWT = p.din("L1_attn_out_wT", [D, D])
    outb = p.din("L1_attn_out_b", [D])

    kv_in = p.dint("l1_kv_in", [2 * T, D], BF16)
    kv_out = p.dint("l1_kv_out", [NCORES * 2 * T, D], BF16)

    XN = 16384
    emit_ln(p, dst=p.A, dst_off=0)
    emit_transpose(p, dst=p.A, dst_off=XN, src=p.A, src_off=0)

    for j, (wo, boff) in enumerate(((D, D), (2 * D, 2 * D))):
        emit_weight_load(p, WT, p.W, 0, D, D, out_off=wo)
        emit_row_load(p, bias, 0, D, src_off=boff)
        emit_proj_tm(p, dst=p.B, dst_off=0, src_fm=p.A, src_off=XN, w_sb=p.W, w_off=0,
                     n_in=D, n_out=D, bias_off=0, wait_dw=p.dW.n)

        @p.section("gpsimd")
        def _(g, j=j):
            return g.dma_start(out=kv_in[j * T:(j + 1) * T, :].rearrange("(t p) d -> p t d", p=128),
                               in_=p.B[:].rearrange("p (t d) -> p t d", d=D))

    @p.raw_section("gpsimd")
    def _(g):
        p.wait_all(g)
        g.collective_compute("AllGather", OP.bypass, replica_groups=[list(range(NCORES))],
                             ins=[kv_in[:]], outs=[kv_out[:]]).then_inc(p.sCC.h)
        p.sCC.n += 1

    emit_weight_load(p, WT, p.W, 0, D, D, out_off=0)
    emit_row_load(p, bias, 0, D, src_off=0)
    emit_proj_tm(p, dst=p.B, dst_off=0, src_fm=p.A, src_off=XN, w_sb=p.W, w_off=0,
                 n_in=D, n_out=D, bias_off=0, scale=0.125, wait_dw=p.dW.n)
    emit_transpose(p, dst=p.A, dst_off=0, src=p.B, src_off=0)

    @p.raw_section("gpsimd")
    def _(g):
        g.wait_ge(p.T.h, p.T.n)
        i = g.dma_start(out=p.W[0:128, 0:8], in_=rho[:])
        p.dW.inc(i, 16)
    rel_dw = p.dW.n

    REL = 16384

    @p.raw_section("tensor")
    def _(t):
        p.wait_all(t)
        t.wait_ge(p.dW.h, rel_dw)
        ev0 = p.sEV.n
        job = 0
        for ti in range(NT):
            for h in range(16):
                ch, hin = (h * 64) // 128, (h * 64) % 128
                if job >= 2:
                    t.wait_ge(p.sEV.h, ev0 + job - 1)
                i = t.matmul(p.psA[0:128, 0:8],
                             p.A[hin:hin + 64, ch * 2048 + ti * 128: ch * 2048 + ti * 128 + 128],
                             p.W[hin:hin + 64, 0:8], start=True, stop=True)
                p.sMM.inc(i)
                job += 1

    @p.raw_section("vector")
    def _(v):
        base = p.sMM.n - NT * 16
        job = 0
        for ti in range(NT):
            for h in range(16):
                v.wait_ge(p.sMM.h, base + job + 1)
                v.tensor_copy(p.A[:, REL + ti * 128 + h * 8: REL + ti * 128 + h * 8 + 8],
                              p.psA[:, 0:8])
                p.dinc(v, p.sEV)
                job += 1
        p.T.inc(v.drain())

    for ti in range(NT):
        @p.raw_section("gpsimd")
        def _(g, ti=ti):
            if ti == 0:
                p.wait_all(g)
                g.wait_ge(p.sCC.h, p.sCC.n)
            else:
                g.wait_ge(p.sEV.h, p.sEV.n)
            for s in range(NCORES):
                i = g.dma_start(out=p.W[:, s * 1024:(s + 1) * 1024],
                                in_=kv_out[(s * 2) * T + ti * 128:(s * 2) * T + ti * 128 + 128, :])
                p.dW.inc(i, 16)

        @p.raw_section("vector")
        def _(v, ti=ti):
            v.wait_ge(p.dW.h, p.dW.n)
            for s in range(NCORES):
                v.tensor_tensor(out=p.E[:, 1024:2048], in0=p.B[:, ti * 1024:(ti + 1) * 1024],
                                in1=p.W[:, s * 1024:(s + 1) * 1024], op=OP.mult)
                v.tensor_reduce(out=p.E[:, 0:128].rearrange("p (h s) -> p h s", s=8)[:, :, s:s + 1].squeeze(2),
                                in_=p.E[:, 1024:2048].rearrange("p (h d) -> p h d", d=64),
                                axis=X, op=OP.add)
            v.drain()
            v.tensor_tensor(out=p.E[:, 0:128], in0=p.E[:, 0:128],
                            in1=p.A[:, REL + ti * 128:REL + (ti + 1) * 128], op=OP.add)
            sview = p.E[:, 0:128].rearrange("p (h s) -> p h s", s=8)
            m = p.ST[:, 384:400]
            v.drain()
            v.tensor_reduce(out=m, in_=sview, axis=X, op=OP.max)
            v.drain()
            v.tensor_tensor(out=sview, in0=sview, in1=m.unsqueeze(2).broadcast_to([128, 16, 8]),
                            op=OP.subtract)
            p.dinc(v, p.sEV)

        @p.raw_section("scalar")
        def _(s_, ti=ti):
            s_.wait_ge(p.sEV.h, p.sEV.n)
            s_.activation(out=p.E[:, 0:128], in_=p.E[:, 0:128], func=AF.Exp)
            p.dinc(s_, p.sAC)

        @p.raw_section("gpsimd")
        def _(g, ti=ti):
            g.wait_ge(p.sAC.h, p.sAC.n)
            for s in range(NCORES):
                i = g.dma_start(out=p.W[:, s * 1024:(s + 1) * 1024],
                                in_=kv_out[(s * 2 + 1) * T + ti * 128:(s * 2 + 1) * T + ti * 128 + 128, :])
                p.dW.inc(i, 16)

        @p.raw_section("vector")
        def _(v, ti=ti):
            v.wait_ge(p.sAC.h, p.sAC.n)
            sview = p.E[:, 0:128].rearrange("p (h s) -> p h s", s=8)
            sm = p.ST[:, 400:416]
            v.tensor_reduce(out=sm, in_=sview, axis=X, op=OP.add)
            v.drain()
            v.reciprocal(sm, sm)
            v.drain()
            v.tensor_tensor(out=sview, in0=sview, in1=sm.unsqueeze(2).broadcast_to([128, 16, 8]),
                            op=OP.mult)
            v.drain()
            v.wait_ge(p.dW.h, p.dW.n)
            for s in range(NCORES):
                pb = p.E[:, 0:128].rearrange("p (h s) -> p h s", s=8)[:, :, s:s + 1] \
                    .broadcast_to([128, 16, 64])
                vv = p.W[:, s * 1024:(s + 1) * 1024].rearrange("p (h d) -> p h d", d=64)
                if s == 0:
                    v.tensor_tensor(out=p.E[:, 2048:3072].rearrange("p (h d) -> p h d", d=64),
                                    in0=vv, in1=pb, op=OP.mult)
                else:
                    v.tensor_tensor(out=p.E[:, 1024:2048].rearrange("p (h d) -> p h d", d=64),
                                    in0=vv, in1=pb, op=OP.mult)
                    v.tensor_add(p.E[:, 2048:3072], p.E[:, 2048:3072], p.E[:, 1024:2048])
            v.drain()
            v.tensor_copy(p.A[:, ti * 1024:(ti + 1) * 1024], p.E[:, 2048:3072])
            p.dinc(v, p.sEV)
            p.T.inc(v.drain())

    emit_transpose(p, dst=p.A, dst_off=XN, src=p.A, src_off=0)
    emit_weight_load(p, outWT, p.W, 0, D, D)
    emit_row_load(p, outb, 0, D)
    emit_proj_residual(p, src_fm=(p.A, XN), w=(p.W, 0), n_in=D, n_out=D,
                       wait_dw=p.dW.n, bias_off=0)


def emit_quantize(p, wT_dram, n_in, n_out, q_dram, onescol, eps=1e-5):
    nch = n_in // 128
    dbl = (2 * n_out <= 4096)
    acc = p.ST[:, 416:417]
    tmp = p.ST[:, 417:418]
    thr = p.ST[:, 418:419]
    thr_d = p.dint("q_thr", [1, 1], F32)

    @p.section("vector")
    def _(v):
        v.memset(acc, 0.0)
        return v.drain()

    for c in range(nch):
        bo = (c % 2) * n_out if dbl else 0

        @p.raw_section("gpsimd")
        def _(g, c=c, bo=bo):
            if c == 0:
                g.wait_ge(p.T.h, p.T.n)
            else:
                g.wait_ge(p.sEV.h, p.sEV.n - (1 if (dbl and c >= 2) else 0))
            i = g.dma_start(out=p.E[:, bo:bo + n_out],
                            in_=wT_dram[c * 128:(c + 1) * 128, :])
            p.dW.inc(i, 16)

        @p.raw_section("vector")
        def _(v, c=c, bo=bo):
            v.wait_ge(p.dW.h, p.dW.n)
            v.tensor_reduce(out=tmp, in_=p.E[:, bo:bo + n_out],
                            axis=X, op=OP.add, apply_absolute_value=True)
            v.drain()
            v.tensor_add(acc, acc, tmp)
            p.dinc(v, p.sEV)

    @p.raw_section("tensor")
    def _(t):
        t.wait_ge(p.sEV.h, p.sEV.n)
        i = t.matmul(p.psA[0:1, 0:1], acc, onescol[:], start=True, stop=True)
        p.sMM.inc(i)

    @p.raw_section("vector")
    def _(v):
        v.wait_ge(p.sMM.h, p.sMM.n)
        v.tensor_scalar(out=thr[0:1], in0=p.psA[0:1, 0:1],
                        scalar1=0.5 / (n_in * n_out), scalar2=0.5 * eps,
                        op0=OP.mult, op1=OP.add)
        p.dinc(v, p.sEV)

    @p.raw_section("gpsimd")
    def _(g):
        g.wait_ge(p.sEV.h, p.sEV.n)
        i = g.dma_start(out=thr_d[:], in_=thr[0:1])
        p.dW.inc(i, 16)
        g.wait_ge(p.dW.h, p.dW.n)
        i = g.dma_start(out=thr, in_=thr_d.broadcast_to([128, 1]))
        p.dW.inc(i, 16)

    @p.section("vector")
    def _(v):
        v.wait_ge(p.dW.h, p.dW.n)
        return v.drain()

    for c in range(nch):
        bo = (c % 2) * n_out if dbl else 0

        @p.raw_section("gpsimd")
        def _(g, c=c, bo=bo):
            if c == 0:
                g.wait_ge(p.T.h, p.T.n)
            else:
                g.wait_ge(p.sEV.h, p.sEV.n - (1 if (dbl and c >= 2) else 0))
            i = g.dma_start(out=p.E[:, bo:bo + n_out],
                            in_=wT_dram[c * 128:(c + 1) * 128, :])
            p.dW.inc(i, 16)

        @p.raw_section("vector")
        def _(v, c=c, bo=bo):
            v.wait_ge(p.dW.h, p.dW.n)
            e = p.E[:, bo:bo + n_out]
            v.tensor_single_scalar(out=p.B[:, bo:bo + n_out], in_=e, scalar=thr, op=OP.is_ge)
            v.tensor_scalar(out=p.B[:, 8192 + bo:8192 + bo + n_out], in0=e,
                            scalar1=-1.0, scalar2=thr, op0=OP.mult, op1=OP.is_ge)
            v.drain()
            v.tensor_sub(p.B[:, bo:bo + n_out], p.B[:, bo:bo + n_out],
                         p.B[:, 8192 + bo:8192 + bo + n_out])
            p.dinc(v, p.sEV)

        @p.raw_section("gpsimd")
        def _(g, c=c, bo=bo):
            g.wait_ge(p.sEV.h, p.sEV.n)
            i = g.dma_start(out=q_dram[c * 128:(c + 1) * 128, :],
                            in_=p.B[:, bo:bo + n_out])
            p.dW.inc(i, 16)

    @p.section("vector")
    def _(v):
        return v.drain()


def emit_l1_ffn(p):
    W1T = p.din("L1_ffn_W1T", [D, 4096])
    b1 = p.din("L1_ffn_b1", [4096])
    W2T = p.din("L1_ffn_W2T", [4096, D])
    b2 = p.din("L1_ffn_b2", [D])
    onescol = p.din("c_ones_col", [128, 1])
    w1q = p.dint("l1_w1q", [D, 4096], BF16)
    w2q = p.dint("l1_w2q", [4096, D], BF16)
    oc = p.sb("onescol_sb", [128, 1], F32)

    @p.raw_section("gpsimd")
    def _(g):
        g.wait_ge(p.T.h, p.T.n)
        i = g.dma_start(out=oc[:], in_=onescol[:])
        p.dW.inc(i, 16)

    @p.section("vector")
    def _(v):
        v.wait_ge(p.dW.h, p.dW.n)
        return v.drain()

    emit_quantize(p, W1T, D, 4096, w1q, oc)
    emit_quantize(p, W2T, 4096, D, w2q, oc)

    XN = 16384
    emit_ln(p, dst=p.A, dst_off=0)
    emit_transpose(p, dst=p.A, dst_off=XN, src=p.A, src_off=0)
    emit_moe_expert(p, None, w1q, b1, w2q, b2, xn_off=XN, cw_col=None,
                    act=AF.Relu, add_xf=False)


def emit_l2_attn(p):
    Wq = p.din("L2_attn_Wq", [D, D]);  bq = p.din("L2_attn_bq", [D])
    Wk = p.din("L2_attn_Wk", [D, D]);  bk = p.din("L2_attn_bk", [D])
    Wv = p.din("L2_attn_Wv", [D, D]);  bv = p.din("L2_attn_bv", [D])
    Wo = p.din("L2_attn_Wo", [D, D]);  bo = p.din("L2_attn_bo", [D])
    ds2 = p.din("L2_attn_dist_scale2d", [1, 1])
    W1 = p.din("L2_attn_W1", [D, 4096]); b1 = p.din("L2_attn_b1", [4096])
    W2 = p.din("L2_attn_W2", [4096, D]); b2 = p.din("L2_attn_b2", [D])
    g1 = p.din("L2_attn_ln1_g", [D]); be1 = p.din("L2_attn_ln1_b", [D])
    g2 = p.din("L2_attn_ln2_g", [D]); be2 = p.din("L2_attn_ln2_b", [D])
    cdist = p.din("c_dist", [28, 128, 512])
    pen_d = p.dint("l2_pen", [28, 128, 512], BF16)
    dsb = p.ST[:, 419:420]

    XN = 16384
    @p.raw_section("gpsimd")
    def _(g):
        g.wait_ge(p.T.h, p.T.n)
        i = g.dma_start(out=dsb, in_=ds2.broadcast_to([128, 1]))
        p.dW.inc(i, 16)

    @p.section("vector")
    def _(v):
        v.wait_ge(p.dW.h, p.dW.n)
        v.tensor_scalar_mul(dsb, dsb, -0.125)
        return v.drain()

    for r in range(28):
        @p.raw_section("gpsimd")
        def _(g, r=r):
            if r == 0:
                g.wait_ge(p.T.h, p.T.n)
            else:
                g.wait_ge(p.sEV.h, p.sEV.n)
            i = g.dma_start(out=p.E[:, 0:512], in_=cdist[r])
            p.dW.inc(i, 16)

        @p.raw_section("vector")
        def _(v, r=r):
            v.wait_ge(p.dW.h, p.dW.n)
            v.tensor_scalar_mul(p.B[:, 0:512], p.E[:, 0:512], dsb)
            p.dinc(v, p.sEV)

        @p.raw_section("gpsimd")
        def _(g, r=r):
            g.wait_ge(p.sEV.h, p.sEV.n)
            i = g.dma_start(out=pen_d[r], in_=p.B[:, 0:512])
            p.dW.inc(i, 16)

    emit_ln(p, dst=p.A, dst_off=0)
    emit_transpose(p, dst=p.A, dst_off=XN, src=p.A, src_off=0)

    for e8 in range(8):
        dsl = e8 * 128
        emit_weight_load(p, Wq, p.W, 0, D, 128, out_off=dsl)
        emit_weight_load(p, Wk, p.W, 1024, D, 128, out_off=dsl)
        emit_weight_load(p, Wv, p.W, 2048, D, 128, out_off=dsl)
        emit_row_load(p, bq, 0, 128, src_off=dsl)
        emit_row_load(p, bk, 256, 128, src_off=dsl)
        emit_row_load(p, bv, 512, 128, src_off=dsl)
        dw = p.dW.n
        emit_proj_tm(p, dst=p.B, dst_off=0, src_fm=p.A, src_off=XN, w_sb=p.W, w_off=0,
                     n_in=D, n_out=128, bias_off=0, scale=0.125, wait_dw=dw,
                     dst_stride=128)
        emit_transpose(p, dst=p.B, dst_off=2048, src=p.B, src_off=0, width=128)
        emit_proj_tm(p, dst=p.B, dst_off=4096, src_fm=p.A, src_off=XN, w_sb=p.W,
                     w_off=1024, n_in=D, n_out=128, bias_off=256, wait_dw=dw,
                     dst_stride=128)
        emit_transpose(p, dst=p.B, dst_off=6144, src=p.B, src_off=4096, width=128)
        emit_proj_tm(p, dst=p.B, dst_off=8192, src_fm=p.A, src_off=XN, w_sb=p.W,
                     w_off=2048, n_in=D, n_out=128, bias_off=512, wait_dw=dw,
                     dst_stride=128)
        emit_l2_attn_8(p, e8, pen_d)
        emit_transpose(p, dst=p.B, dst_off=6144, src=p.B, src_off=10240, width=128)
        emit_weight_load(p, Wo, p.W, 3072, 128, D, in_off=dsl)
        if e8 == 7:
            emit_row_load(p, bo, 1024, D)
        emit_proj_residual(p, src_fm=(p.B, 6144), w=(p.W, 3072), n_in=128, n_out=D,
                           wait_dw=p.dW.n, dst=(p.A, 0, D),
                           bias_off=(1024 if e8 == 7 else None))

    @p.raw_section("gpsimd")
    def _(g):
        g.wait_ge(p.T.h, p.T.n)
        i = g.dma_start(out=p.W[:, 0:1024], in_=g1.unsqueeze(0).broadcast_to([128, D]))
        p.dW.inc(i, 16)
        i = g.dma_start(out=p.W[:, 1024:2048], in_=be1.unsqueeze(0).broadcast_to([128, D]))
        p.dW.inc(i, 16)

    @p.section("vector")
    def _(v):
        v.wait_ge(p.dW.h, p.dW.n)
        return v.drain()

    emit_ln(p, dst=p.B, dst_off=0, src=p.A, src_off=0,
            affine=(p.W[:, 0:1024], p.W[:, 1024:2048]))
    emit_transpose(p, dst=p.A, dst_off=XN, src=p.B, src_off=0)
    emit_moe_expert(p, None, W1, b1, W2, b2, xn_off=XN, cw_col=None, act=AF.Relu,
                    add_xf=False, xn_buf=p.A, h_buf=p.A, h_off=0, out_dst=(p.B, 0))
    @p.raw_section("gpsimd")
    def _(g):
        g.wait_ge(p.T.h, p.T.n)
        i = g.dma_start(out=p.W[:, 0:1024], in_=g2.unsqueeze(0).broadcast_to([128, D]))
        p.dW.inc(i, 16)
        i = g.dma_start(out=p.W[:, 1024:2048], in_=be2.unsqueeze(0).broadcast_to([128, D]))
        p.dW.inc(i, 16)

    @p.section("vector")
    def _(v):
        v.wait_ge(p.dW.h, p.dW.n)
        return v.drain()

    emit_ln(p, dst=p.A, dst_off=0, src=p.B, src_off=0,
            affine=(p.W[:, 0:1024], p.W[:, 1024:2048]))

    @p.section("vector")
    def _(v):
        last = None
        for ti in range(NT):
            last = v.tensor_add(p.xres[:, ti * D:(ti + 1) * D],
                                p.xres[:, ti * D:(ti + 1) * D],
                                p.A[:, ti * 1024:(ti + 1) * 1024])
        return last


def emit_l2_attn_8(p, e8, pen_d):
    QF, KF, VT, CO, P_off, PT_off, PEN = 2048, 6144, 8192, 10240, 12288, 14336, 4096
    nmax = p.ST[:, 368:369]
    sums = p.ST[:, 369:373]
    rinv = p.ST[:, 373:374]
    ac_base = p.sAC.n

    for qi in range(NT):
        @p.raw_section("gpsimd")
        def _(g, qi=qi):
            if qi == 0:
                p.wait_all(g)
            else:
                g.wait_ge(p.sAC.h, ac_base + 2 * qi)
            for kc in range(4):
                r = (qi * 128 - kc * 512 + 1536) // 128
                i = g.dma_start(out=p.W[:, PEN + kc * 512:PEN + (kc + 1) * 512],
                                in_=pen_d[r])
                p.dW.inc(i, 16)

        for h in range(2):
            hglob = e8 * 2 + h
            hin = h * 64

            @p.raw_section("tensor")
            def _(t, qi=qi, h=h, hin=hin):
                if qi == 0 and h == 0:
                    p.wait_all(t)
                t.wait_ge(p.dW.h, p.dW.n)
                if not (qi == 0 and h == 0):
                    t.wait_ge(p.sAC.h, p.sAC.n)
                for b in range(4):
                    t.matmul(p.psS[:, b * 512:(b + 1) * 512],
                             p.B[hin:hin + 64, QF + qi * 128:QF + qi * 128 + 128],
                             p.B[hin:hin + 64, KF + b * 512:KF + (b + 1) * 512],
                             start=True, stop=False)
                last = None
                for b in range(4):
                    last = t.matmul(p.psS[:, b * 512:(b + 1) * 512], p.ident[:],
                                    p.W[:, PEN + b * 512:PEN + (b + 1) * 512],
                                    start=False, stop=True)
                p.sMM.inc(last)

            @p.raw_section("vector")
            def _(v, hglob=hglob):
                v.wait_ge(p.sMM.h, p.sMM.n)
                if hglob > 0:
                    v.tensor_scalar_add(p.psS[:, 0:hglob], p.psS[:, 0:hglob], NEG)
                    v.drain()
                v.reduce_max(out=nmax, in_=p.psS[:, 0:2048], axis=X, negate=True)
                p.dinc(v, p.sEV)

            @p.raw_section("scalar")
            def _(s):
                s.wait_ge(p.sEV.h, p.sEV.n)
                for b in range(4):
                    s.activation(out=p.B[:, P_off + b * 512:P_off + (b + 1) * 512],
                                 in_=p.psS[:, b * 512:(b + 1) * 512], func=AF.Exp,
                                 bias=nmax, accum_out=sums[:, b:b + 1])
                p.dinc(s, p.sAC)

            @p.raw_section("vector")
            def _(v):
                v.wait_ge(p.sAC.h, p.sAC.n)
                v.tensor_reduce(out=rinv, in_=sums, axis=X, op=OP.add)
                v.drain()
                v.reciprocal(rinv, rinv)
                p.dinc(v, p.sEV)

            @p.raw_section("tensor")
            def _(t):
                t.wait_ge(p.sAC.h, p.sAC.n)
                ev0 = p.sEV.n
                for gi, g0 in enumerate(range(0, 16, 8)):
                    if gi > 0:
                        t.wait_ge(p.sEV.h, ev0 + gi)
                    last = None
                    for k in range(8):
                        last = t.transpose(p.psT[:, k * 128:(k + 1) * 128],
                                           p.B[:, P_off + (g0 + k) * 128:
                                               P_off + (g0 + k) * 128 + 128],
                                           p.ident[:])
                    p.sMM.inc(last)

            @p.raw_section("vector")
            def _(v):
                base = p.sMM.n - 2
                for gi, g0 in enumerate(range(0, 16, 8)):
                    v.wait_ge(p.sMM.h, base + gi + 1)
                    v.tensor_copy(p.B[:, PT_off + g0 * 128:PT_off + g0 * 128 + 1024],
                                  p.psT[:, 0:1024])
                    p.dinc(v, p.sEV)

            @p.raw_section("tensor")
            def _(t, hin=hin):
                t.wait_ge(p.sEV.h, p.sEV.n)
                last = None
                for kb in range(16):
                    last = t.matmul(p.psA[:, 0:64],
                                    p.B[:, PT_off + kb * 128:PT_off + kb * 128 + 128],
                                    p.B[:, VT + kb * 128 + hin:VT + kb * 128 + hin + 64],
                                    start=(kb == 0), stop=(kb == 15))
                p.sMM.inc(last)

            @p.raw_section("vector")
            def _(v, qi=qi, hin=hin):
                v.wait_ge(p.sMM.h, p.sMM.n)
                v.tensor_scalar_mul(p.B[:, CO + qi * 128 + hin:CO + qi * 128 + hin + 64],
                                    p.psA[:, 0:64], rinv)
                v.drain()
                p.dinc(v, p.sEV)
                p.T.inc(v.drain())


def emit_l2_ffn(p):
    Wg = p.din("L2_ffn_Wg", [D, 8])
    gP = p.din("L2_ffn_gammaP", [128, 64])
    bP = p.din("L2_ffn_betaP", [128, 64])
    W1 = p.din("L2_ffn_W1", [8, D, 4096])
    b1 = p.din("L2_ffn_b1", [8, 4096])
    W2 = p.din("L2_ffn_W2", [8, 4096, D])
    b2 = p.din("L2_ffn_b2", [8, D])

    XN = 16384
    emit_ln(p, dst=p.A, dst_off=0)
    emit_transpose(p, dst=p.A, dst_off=XN, src=p.A, src_off=0)

    @p.raw_section("gpsimd")
    def _(g):
        g.wait_ge(p.T.h, p.T.n)
        i = g.dma_start(out=p.CS[:, 160:224], in_=gP[:])
        p.dW.inc(i, 16)
        i = g.dma_start(out=p.CS[:, 224:288], in_=bP[:])
        p.dW.inc(i, 16)

    emit_weight_load(p, Wg, p.W, 0, D, 8)
    emit_proj_tm(p, dst=p.E, dst_off=0, src_fm=p.A, src_off=XN, w_sb=p.W, w_off=0,
                 n_in=D, n_out=8, wait_dw=p.dW.n, evac="vector", dst_stride=8)

    m1 = p.ST[:, 384:400]; m2 = p.ST[:, 400:416]
    e2c = p.ST[:, 420:436]; w1n = p.ST[:, 436:452]; w2n = p.ST[:, 452:468]

    @p.section("vector")
    def _(v):
        lg = p.E[:, 0:128].rearrange("p (t e) -> p t e", e=8)
        eq1 = p.E[:, 128:256].rearrange("p (t e) -> p t e", e=8)
        t2 = p.E[:, 256:384].rearrange("p (t e) -> p t e", e=8)
        v.tensor_reduce(out=m1, in_=lg, axis=X, op=OP.max)
        v.drain()
        v.tensor_tensor(out=eq1, in0=lg, in1=m1.unsqueeze(2).broadcast_to([128, 16, 8]),
                        op=OP.is_equal)
        v.drain()
        v.scalar_tensor_tensor(out=t2, in0=eq1, scalar=NEG, in1=lg, op0=OP.mult, op1=OP.add)
        v.drain()
        v.tensor_reduce(out=m2, in_=t2, axis=X, op=OP.max)
        v.drain()
        v.tensor_sub(e2c, m2, m1)
        return v.drain()

    @p.section("scalar")
    def _(s):
        return s.activation(out=e2c, in_=e2c, func=AF.Exp)

    @p.section("vector")
    def _(v):
        lg = p.E[:, 0:128].rearrange("p (t e) -> p t e", e=8)
        eq1 = p.E[:, 128:256].rearrange("p (t e) -> p t e", e=8)
        t2 = p.E[:, 256:384].rearrange("p (t e) -> p t e", e=8)
        eq2 = p.E[:, 384:512].rearrange("p (t e) -> p t e", e=8)
        cw = p.ST[:, 256:384].rearrange("p (t e) -> p t e", e=8)
        v.tensor_scalar_add(w1n, e2c, 1.0)
        v.drain()
        v.reciprocal(w1n, w1n)
        v.drain()
        v.tensor_tensor(out=w2n, in0=e2c, in1=w1n, op=OP.mult)
        v.tensor_tensor(out=eq2, in0=t2, in1=m2.unsqueeze(2).broadcast_to([128, 16, 8]),
                        op=OP.is_equal)
        v.drain()
        v.tensor_tensor(out=cw, in0=eq1, in1=w1n.unsqueeze(2).broadcast_to([128, 16, 8]),
                        op=OP.mult)
        v.tensor_tensor(out=eq2, in0=eq2, in1=w2n.unsqueeze(2).broadcast_to([128, 16, 8]),
                        op=OP.mult)
        v.drain()
        v.tensor_add(cw, cw, eq2)
        return v.drain()

    @p.section("vector")
    def _(v):
        last = None
        for ti in range(NT):
            last = v.tensor_add(p.xres[:, ti * D:(ti + 1) * D],
                                p.xres[:, ti * D:(ti + 1) * D],
                                p.A[:, ti * 1024:(ti + 1) * 1024])
        return last

    emit_ln(p, dst=p.B, dst_off=0, src=p.A, src_off=0, eps=1e-6)
    emit_transpose(p, dst=p.A, dst_off=XN, src=p.B, src_off=0)

    CW = 256
    for e in range(8):
        emit_moe_expert(p, e, W1, b1, W2, b2, xn_off=XN,
                        cw_col=lambda ti, e=e: CW + ti * 8 + e,
                        act=AF.Relu, add_xf=False,
                        premul=(lambda dc, e=e: p.CS[:, 160 + e * 8 + dc:161 + e * 8 + dc],
                                lambda dc, e=e: p.CS[:, 224 + e * 8 + dc:225 + e * 8 + dc]))


def emit_l3_attn(p):
    WqT = p.din("L3_attn_WqT", [D, D])
    WkT = p.din("L3_attn_WkT", [D, 256])
    WvT = p.din("L3_attn_WvT", [D, 256])
    WoT = p.din("L3_attn_WoT", [D, D])
    mb = p.din("c_l3_bandmask", [2, 128, 176], BF16)
    ms = p.din("c_l3_stridemask", [NT, 128, 44], BF16)
    v_d = p.dint("l3_v", [T, 256], BF16)

    XN = 16384
    Eb = p.E[:].bitcast(BF16)
    emit_ln(p, dst=p.A, dst_off=0)
    emit_transpose(p, dst=p.A, dst_off=XN, src=p.A, src_off=0)

    emit_weight_load(p, WkT, p.W, 0, D, 256)
    emit_weight_load(p, WvT, p.W, 2048, D, 256)
    dw = p.dW.n
    emit_proj_tm(p, dst=p.B, dst_off=0, src_fm=p.A, src_off=XN, w_sb=p.W, w_off=0,
                 n_in=D, n_out=256, wait_dw=dw, dst_stride=256)
    emit_proj_tm(p, dst=Eb, dst_off=4096, src_fm=p.A, src_off=XN, w_sb=p.W,
                 w_off=2048, n_in=D, n_out=256, dst_stride=256)
    emit_transpose(p, dst=p.B, dst_off=4096, src=p.B, src_off=0, width=256)
    emit_weight_load(p, WqT, p.W, 0, D, D)
    emit_proj_tm(p, dst=p.A, dst_off=0, src_fm=p.A, src_off=XN, w_sb=p.W, w_off=0,
                 n_in=D, n_out=D, scale=0.125, wait_dw=p.dW.n)
    @p.raw_section("gpsimd")
    def _(g):
        p.wait_all(g)
        for g4 in range(4):
            ch, base = g4 // 2, (g4 % 2) * 64
            i = g.dma_start(out=p.W[0:64, g4 * 2048:(g4 + 1) * 2048],
                            in_=p.B[base:base + 64, 4096 + ch * 2048:4096 + (ch + 1) * 2048])
            p.dW.inc(i, 16)
            i = g.dma_start(out=p.W[64:128, g4 * 2048:(g4 + 1) * 2048],
                            in_=p.B[base:base + 64, 4096 + ch * 2048:4096 + (ch + 1) * 2048])
            p.dW.inc(i, 16)
        i = g.dma_start(out=v_d.rearrange("(t p) d -> p t d", p=128),
                        in_=Eb[:, 4096:8192].rearrange("p (t d) -> p t d", d=256))
        p.dW.inc(i, 16)
        g.wait_ge(p.dW.h, p.dW.n)
        i = g.dma_start(out=Eb[0:44, 896:1152],
                        in_=v_d[45:2024:46, :])
        p.dW.inc(i, 16)
        i = g.dma_start(out=Eb[:, 1280:1456], in_=mb[0])
        p.dW.inc(i, 16)
        i = g.dma_start(out=Eb[:, 1456:1632], in_=mb[1])
        p.dW.inc(i, 16)

    @p.section("vector")
    def _(v):
        v.wait_ge(p.dW.h, p.dW.n)
        v.memset(Eb[:, 640:896], 0.0)
        return v.drain()

    emit_transpose(p, dst=p.B, dst_off=0, src=p.A, src_off=0)

    nmax = p.ST[:, 368:369]
    sums = p.ST[:, 369:370]
    rinv = p.ST[:, 373:374]

    for qi in range(NT):
        @p.raw_section("gpsimd")
        def _(g, qi=qi):
            if qi == 0:
                p.wait_all(g)
            else:
                g.wait_ge(p.sAC.h, p.sAC.n)
            i = g.dma_start(out=Eb[:, 1664:1708], in_=ms[qi])
            p.dW.inc(i, 16)
            if qi > 0:
                i = g.dma_start(out=Eb[0:48, 640:896],
                                in_=Eb[80:128, 4096 + (qi - 1) * 256:4096 + qi * 256])
                p.dW.inc(i, 16)

        for h in range(16):
            g4 = h // 4
            qch, qbase = h // 2, (h % 2) * 64
            kbase = qbase

            @p.raw_section("tensor")
            def _(t, qi=qi, h=h, g4=g4, qch=qch, qbase=qbase):
                if qi == 0 and h == 0:
                    p.wait_all(t)
                t.wait_ge(p.dW.h, p.dW.n)
                if not (qi == 0 and h == 0):
                    t.wait_ge(p.sAC.h, p.sAC.n)
                qf = p.B[qbase:qbase + 64,
                         qch * 2048 + qi * 128:qch * 2048 + qi * 128 + 128]
                ka = max(0, qi * 128 - 48)
                t.matmul(p.psS[:, 0:48],
                         qf, p.W[qbase:qbase + 64, g4 * 2048 + ka:g4 * 2048 + ka + 48],
                         start=True, stop=True)
                t.matmul(p.psS[:, 48:176],
                         qf, p.W[qbase:qbase + 64, g4 * 2048 + qi * 128:g4 * 2048 + qi * 128 + 128],
                         start=True, stop=True)
                t.matmul(p.psS[:, 176:220],
                         qf, p.W[qbase:qbase + 64, g4 * 2048 + 45:g4 * 2048 + 2024:46],
                         start=True, stop=True)
                mb_off = 1280 if qi == 0 else 1456
                t.matmul(p.psS[:, 0:176], p.ident[:], Eb[:, mb_off:mb_off + 176],
                         start=False, stop=True)
                last = t.matmul(p.psS[:, 176:220], p.ident[:], Eb[:, 1664:1708],
                                start=False, stop=True)
                p.sMM.inc(last)

            @p.raw_section("vector")
            def _(v):
                v.wait_ge(p.sMM.h, p.sMM.n)
                v.reduce_max(out=nmax, in_=p.psS[:, 0:220], axis=X, negate=True)
                p.dinc(v, p.sEV)

            @p.raw_section("scalar")
            def _(s):
                s.wait_ge(p.sEV.h, p.sEV.n)
                s.activation(out=Eb[:, 0:220], in_=p.psS[:, 0:220], func=AF.Exp,
                             bias=nmax, accum_out=sums)
                p.dinc(s, p.sAC)

            @p.raw_section("vector")
            def _(v):
                v.wait_ge(p.sAC.h, p.sAC.n)
                v.reciprocal(rinv, sums)
                p.dinc(v, p.sEV)

            @p.raw_section("tensor")
            def _(t):
                t.wait_ge(p.sAC.h, p.sAC.n)
                t.transpose(p.psT[0:48, 0:128], Eb[:, 0:48], p.ident[:])
                t.transpose(p.psT[:, 128:256], Eb[:, 48:176], p.ident[:])
                i = t.transpose(p.psT[0:44, 256:384], Eb[:, 176:220], p.ident[:])
                p.sMM.inc(i)

            @p.raw_section("vector")
            def _(v):
                v.wait_ge(p.sMM.h, p.sMM.n)
                v.tensor_copy(Eb[0:48, 256:384], p.psT[0:48, 0:128])
                v.tensor_copy(Eb[:, 384:512], p.psT[:, 128:256])
                v.tensor_copy(Eb[0:44, 512:640], p.psT[0:44, 256:384])
                p.dinc(v, p.sEV)

            @p.raw_section("tensor")
            def _(t, qi=qi, g4=g4):
                t.wait_ge(p.sEV.h, p.sEV.n)
                vd = g4 * 64
                t.matmul(p.psA[:, 0:64], Eb[0:48, 256:384],
                         Eb[0:48, 640 + vd:640 + vd + 64], start=True, stop=False)
                t.matmul(p.psA[:, 0:64], Eb[:, 384:512],
                         Eb[:, 4096 + qi * 256 + vd:4096 + qi * 256 + vd + 64],
                         start=False, stop=False)
                i = t.matmul(p.psA[:, 0:64], Eb[0:44, 512:640],
                             Eb[0:44, 896 + vd:896 + vd + 64], start=False, stop=True)
                p.sMM.inc(i)

            @p.raw_section("vector")
            def _(v, qi=qi, h=h):
                v.wait_ge(p.sMM.h, p.sMM.n)
                v.tensor_scalar_mul(p.A[:, qi * 1024 + h * 64:qi * 1024 + h * 64 + 64],
                                    p.psA[:, 0:64], rinv)
                v.drain()
                p.dinc(v, p.sEV)
                p.T.inc(v.drain())

    emit_transpose(p, dst=p.A, dst_off=XN, src=p.A, src_off=0)
    emit_weight_load(p, WoT, p.W, 0, D, D)
    emit_proj_residual(p, src_fm=(p.A, XN), w=(p.W, 0), n_in=D, n_out=D,
                       wait_dw=p.dW.n)


def emit_l3_ffn(p):
    w1 = p.din("L3_ffn_w1", [8, D, 4096])
    b1 = p.din("L3_ffn_b1", [8, 4096])
    w2 = p.din("L3_ffn_w2", [8, 4096, D])
    b2 = p.din("L3_ffn_b2", [8, D])
    gw = p.din("L3_ffn_gate_w", [D, 8])
    gb = p.din("L3_ffn_gate_b", [8])

    XN = 16384
    emit_ln(p, dst=p.A, dst_off=0)
    emit_transpose(p, dst=p.A, dst_off=XN, src=p.A, src_off=0)

    emit_weight_load(p, gw, p.W, 0, D, 8)
    emit_row_load(p, gb, 0, 8)
    emit_proj_tm(p, dst=p.E, dst_off=0, src_fm=p.A, src_off=XN, w_sb=p.W, w_off=0,
                 n_in=D, n_out=8, bias_off=0, wait_dw=p.dW.n, evac="vector",
                 dst_stride=8)

    m1 = p.ST[:, 384:400]
    CW = 256

    @p.section("vector")
    def _(v):
        lg = p.E[:, 0:128].rearrange("p (t e) -> p t e", e=8)
        cw = p.ST[:, 256:384].rearrange("p (t e) -> p t e", e=8)
        v.tensor_reduce(out=m1, in_=lg, axis=X, op=OP.max)
        v.drain()
        v.tensor_tensor(out=cw, in0=lg, in1=m1.unsqueeze(2).broadcast_to([128, 16, 8]),
                        op=OP.is_equal)
        return v.drain()

    for e in range(8):
        emit_moe_expert(p, e, w1, b1, w2, b2, xn_off=XN,
                        cw_col=lambda ti, e=e: CW + ti * 8 + e,
                        act=AF.Relu, add_xf=False)


def stage_host(inputs):
    H = dict(host_constants())
    x = np.ascontiguousarray(np.asarray(inputs["L0_attn_x"], dtype=np.float32))
    H["xin"] = [np.ascontiguousarray(x[c]) for c in range(NCORES)]
    for k, v in inputs.items():
        if k != "L0_attn_x":
            H[k] = np.ascontiguousarray(np.asarray(v, dtype=np.float32))
    H["L0_ffn_router_wT"] = np.ascontiguousarray(H["L0_ffn_router_w"].T)
    H["c_ident_f32"] = np.eye(128, dtype=np.float32)
    H["c_ones_col"] = np.ones((128, 1), dtype=np.float32)
    H["L1_attn_in_proj_weightT"] = np.ascontiguousarray(H["L1_attn_in_proj_weight"].T)
    H["L1_attn_out_wT"] = np.ascontiguousarray(H["L1_attn_out_w"].T)
    H["L1_ffn_W1T"] = np.ascontiguousarray(H["L1_ffn_W1"].T)
    H["L1_ffn_W2T"] = np.ascontiguousarray(H["L1_ffn_W2"].T)
    rpk = H["L1_attn_rel_pos_keys"]
    H["rho_c"] = [np.ascontiguousarray(np.vstack([rpk[np.arange(8) - c + 16, :64].T]*2)) for c in range(NCORES)]
    H["L2_attn_dist_scale2d"] = np.asarray(H["L2_attn_dist_scale"], dtype=np.float32).reshape(1, 1)
    a128 = np.arange(128); b512 = np.arange(512)
    dist = np.empty((28, 128, 512), dtype=np.float32)
    for r in range(28):
        dist[r] = np.abs((r * 128 - 1536) + a128[:, None] - b512[None, :]).astype(np.float32)
    H["c_dist"] = dist
    H["L2_ffn_gammaP"] = np.ascontiguousarray(
        H["L2_ffn_gamma"].reshape(8, 8, 128).transpose(2, 0, 1).reshape(128, 64))
    H["L2_ffn_betaP"] = np.ascontiguousarray(
        H["L2_ffn_beta"].reshape(8, 8, 128).transpose(2, 0, 1).reshape(128, 64))
    H["L3_attn_WqT"] = np.ascontiguousarray(H["L3_attn_Wq"].T)
    H["L3_attn_WkT"] = np.ascontiguousarray(H["L3_attn_Wk"].T)
    H["L3_attn_WvT"] = np.ascontiguousarray(H["L3_attn_Wv"].T)
    H["L3_attn_WoT"] = np.ascontiguousarray(H["L3_attn_Wo"].T)
    a = np.arange(128)[:, None]
    j = np.arange(176)[None, :]
    rel = j - 48 - a
    base_valid = (rel >= -45) & (rel <= 0)
    m1v = np.where(base_valid, 0.0, NEG).astype(np.float32)
    m0v = np.where(base_valid & (j >= 48), 0.0, NEG).astype(np.float32)
    H["c_l3_bandmask"] = np.stack([m0v, m1v]).astype(BF)
    sidx = np.arange(45, 2048, 46)
    sm = np.empty((NT, 128, 44), dtype=np.float32)
    for qi in range(NT):
        q = qi * 128 + np.arange(128)[:, None]
        sm[qi] = np.where(sidx[None, :] <= q - 46, 0.0, NEG)
    H["c_l3_stridemask"] = sm.astype(BF)
    return H


def run_launch(p, H):
    in_maps = []
    for c in range(NCORES):
        m = {}
        for name in p.inputs:
            v = H[name]
            m[name] = v[c] if isinstance(v, list) else v
        in_maps.append(m)
    res = run_bass_kernel_spmd(p.nc, in_maps, core_ids=list(range(NCORES)))
    return res.results


LAYER_EMITTERS = [emit_l0_attn, emit_l0_ffn, emit_l1_attn, emit_l1_ffn,
                  emit_l2_attn, emit_l2_ffn, emit_l3_attn, emit_l3_ffn]


def build_layer_program(emitters):
    p = Prog()
    p.setup()
    emit_load_consts(p)
    emit_load_x(p)
    for em in emitters:
        em(p)
    emit_store_x(p)
    p.ctx.close()
    return p


def kernel(**inputs):
    H = stage_host(inputs)
    p = build_layer_program(LAYER_EMITTERS)
    results = run_launch(p, H)
    return np.stack([results[c]["xout"] for c in range(NCORES)], axis=0).astype(np.float32)


# version 03
# speedup vs baseline: 1.7874x; 1.7775x over previous
import numpy as np
import ml_dtypes
from contextlib import ExitStack

import concourse.bass as bass
import concourse.mybir as mybir
from concourse.bass_utils import run_bass_kernel_spmd

F32 = mybir.dt.float32
BF16 = mybir.dt.bfloat16
AF = mybir.ActivationFunctionType
OP = mybir.AluOpType
BF = ml_dtypes.bfloat16

NCORES = 8
T = 2048
D = 1024
NT = 16
NEG = -1.0e30
X = mybir.AxisListType.X


def host_constants():
    c = {}
    c["c_ident_bf"] = np.eye(128, dtype=np.float32).astype(BF)
    c["c_ones_row"] = np.ones((1, 128), dtype=BF)
    half = 32
    ang = (np.arange(T, dtype=np.float32)[:, None] *
           (1.0 / (10000.0 ** (np.arange(half, dtype=np.float32) / np.float32(half)))
            ).astype(np.float32)[None, :])
    c["c_rope_sin"] = np.sin(ang).astype(np.float32)
    c["c_rope_cos"] = np.cos(ang).astype(np.float32)
    a = np.arange(128)
    c["c_causal128"] = np.where(a[:, None] >= a[None, :], 0.0, NEG).astype(np.float32).astype(BF)
    return c


class Sem:
    def __init__(self, prog, name):
        self.h = prog.ctx.enter_context(prog.nc.semaphore(name))
        self.n = 0

    def inc(self, inst, k=1):
        inst.then_inc(self.h, k)
        self.n += k
        return self.n

    def wait(self, eng, val=None):
        eng.wait_ge(self.h, self.n if val is None else val)


class Prog:

    def __init__(self):
        self.nc = bass.Bass("TRN2", target_bir_lowering=False)
        self.ctx = ExitStack()
        self.inputs = {}
        self.outputs = {}
        self._sb = {}

    def din(self, name, shape, dtype=F32):
        if name not in self.inputs:
            self.inputs[name] = self.nc.dram_tensor(name, list(shape), dtype,
                                                    kind="ExternalInput").ap()
        return self.inputs[name]

    def dout(self, name, shape, dtype=F32):
        if name not in self.outputs:
            self.outputs[name] = self.nc.dram_tensor(name, list(shape), dtype,
                                                     kind="ExternalOutput").ap()
        return self.outputs[name]

    def dint(self, name, shape, dtype=F32, addr_space="Local"):
        key = "int_" + name
        if key not in self._sb:
            self._sb[key] = self.nc.dram_tensor(name, list(shape), dtype,
                                                kind="Internal", addr_space=addr_space).ap()
        return self._sb[key]

    def sb(self, name, shape, dtype):
        if name not in self._sb:
            self._sb[name] = self.ctx.enter_context(
                self.nc.sbuf_tensor(name, list(shape), dtype))
        return self._sb[name]

    def ps(self, name, shape, dtype=F32):
        if name not in self._sb:
            self._sb[name] = self.ctx.enter_context(
                self.nc.psum_tensor(name, list(shape), dtype))
        return self._sb[name]

    def setup(self):
        self.block = self.ctx.enter_context(self.nc.Block())
        self.T = Sem(self, "Tl")
        self.dW = Sem(self, "dW")
        self.sMM = Sem(self, "sMM")
        self.sEV = Sem(self, "sEV")
        self.sAC = Sem(self, "sAC")
        self.sGP = Sem(self, "sGP")
        self.sCC = Sem(self, "sCC")
        self.xres = self.sb("xres", [128, NT * D], F32)
        self.A = self.sb("arenaA", [128, 32768], BF16)
        self.B = self.sb("arenaB", [128, 16384], BF16)
        self.E = self.sb("arenaE", [128, 4096], F32)
        self.ST = self.sb("stats", [128, 512], F32)
        self.W = self.sb("wstream", [128, 8192], BF16)
        self.Rb = self.sb("rowbuf", [1, 2048], BF16)
        self.ident = self.sb("identbf", [128, 128], BF16)
        self.onesb = self.sb("onesrow", [1, 128], BF16)
        self.CS = self.sb("cossin", [128, 1024], F32)
        self.psS = self.ps("psS", [128, 2048], F32)
        self.psA = self.ps("psA", [128, 512], F32)
        self.psB = self.ps("psB", [128, 512], F32)
        self.psC = self.ps("psC", [128, 512], F32)
        self.psT = self.ps("psT", [128, 1024], BF16)

    def wait_all(self, eng):
        eng.wait_ge(self.T.h, self.T.n)

    def dinc(self, eng, sem, k=1):
        sem.inc(eng.drain(), k)

    def section(self, engname):
        def deco(fn):
            @getattr(self.block, engname)
            def _(eng):
                self.wait_all(eng)
                last = fn(eng)
                if last is not None:
                    if engname in ("sync", "gpsimd"):
                        self.T.inc(last, 16)
                    elif engname == "tensor":
                        self.T.inc(last, 1)
                    else:
                        self.dinc(eng, self.T)
        return deco

    def raw_section(self, engname):
        def deco(fn):
            @getattr(self.block, engname)
            def _(eng):
                fn(eng)
        return deco


def emit_load_consts(p):
    ib = p.din("c_ident_bf", [128, 128], BF16)
    orow = p.din("c_ones_row", [1, 128], BF16)

    @p.section("sync")
    def _(s):
        i = s.dma_start(out=p.ident[:], in_=ib[:])
        p.T.inc(i, 16)
        return s.dma_start(out=p.onesb[:], in_=orow[:])


def emit_load_x(p, src_name="xin"):
    xin = p.din(src_name, [T, D], F32)

    @p.section("sync")
    def _(s):
        return s.dma_start(out=p.xres[:].rearrange("p (t d) -> p t d", d=D),
                           in_=xin.rearrange("(t p) d -> p t d", p=128))


def emit_store_x(p, dst_name="xout"):
    xout = p.dout(dst_name, [T, D], F32)

    @p.raw_section("gpsimd")
    def _(g):
        p.wait_all(g)
        i = g.dma_start(out=xout.rearrange("(t p) d -> p t d", p=128),
                        in_=p.xres[:].rearrange("p (t d) -> p t d", d=D))
        p.T.inc(i, 16)
        g.wait_ge(p.T.h, p.T.n)


def emit_ln(p, dst, dst_off, width=D, src=None, src_off=0, eps=1e-5,
            ntiles=NT, src_stride=None, dst_stride=None, stat_col=0,
            scratch=None, affine=None):
    src = src if src is not None else p.xres
    src_stride = src_stride if src_stride is not None else width
    dst_stride = dst_stride if dst_stride is not None else width
    scratch = scratch if scratch is not None else p.E[:, 0:width]
    mean = p.ST[:, stat_col:stat_col + ntiles]
    rstd = p.ST[:, stat_col + ntiles:stat_col + 2 * ntiles]
    sumsq = p.ST[:, stat_col + 2 * ntiles:stat_col + 3 * ntiles]

    @p.section("vector")
    def _(v):
        for ti in range(ntiles):
            st = src[:, src_off + ti * src_stride: src_off + ti * src_stride + width]
            v.tensor_reduce(out=mean[:, ti:ti + 1], in_=st, axis=X, op=OP.add)
            v.tensor_mul(scratch, st, st)
            v.tensor_reduce(out=sumsq[:, ti:ti + 1], in_=scratch, axis=X, op=OP.add)
        v.tensor_scalar_mul(mean[:], mean[:], 1.0 / width)
        v.tensor_scalar_mul(sumsq[:], sumsq[:], 1.0 / width)
        v.drain()
        v.tensor_tensor(out=rstd[:], in0=mean[:], in1=mean[:], op=OP.mult)
        v.drain()
        v.tensor_sub(rstd[:], sumsq[:], rstd[:])
        v.drain()
        return v.tensor_scalar_add(rstd[:], rstd[:], float(eps))

    @p.section("scalar")
    def _(s):
        return s.sqrt(rstd[:], rstd[:])

    @p.section("vector")
    def _(v):
        v.reciprocal(rstd[:], rstd[:])
        v.drain()
        last = None
        for ti in range(ntiles):
            st = src[:, src_off + ti * src_stride: src_off + ti * src_stride + width]
            dt_ = dst[:, dst_off + ti * dst_stride: dst_off + ti * dst_stride + width]
            if affine is None:
                last = v.tensor_scalar(out=dt_, in0=st, scalar1=mean[:, ti:ti + 1],
                                       scalar2=rstd[:, ti:ti + 1],
                                       op0=OP.subtract, op1=OP.mult)
            else:
                g_ap, b_ap = affine
                v.tensor_scalar(out=scratch, in0=st, scalar1=mean[:, ti:ti + 1],
                                scalar2=rstd[:, ti:ti + 1],
                                op0=OP.subtract, op1=OP.mult)
                v.tensor_mul(scratch, scratch, g_ap)
                last = v.tensor_tensor(out=dt_, in0=scratch, in1=b_ap, op=OP.add)
        return last


def emit_transpose(p, dst, dst_off, src, src_off, ntiles=NT, width=D,
                   src_stride=None, dst_stride=None):
    src_stride = src_stride if src_stride is not None else width
    dst_stride = dst_stride if dst_stride is not None else ntiles * 128
    nch = width // 128
    groups = []
    for c in range(nch):
        for g0 in range(0, ntiles, 8):
            groups.append((c, g0, min(8, ntiles - g0)))

    @p.raw_section("tensor")
    def _(t):
        p.wait_all(t)
        ev0 = p.sEV.n
        for gi, (c, g0, gl) in enumerate(groups):
            if gi >= 1:
                t.wait_ge(p.sEV.h, ev0 + gi)
            last = None
            for k in range(gl):
                ti = g0 + k
                sc = src_off + ti * src_stride + c * 128
                last = t.transpose(p.psT[:, k * 128:(k + 1) * 128],
                                   src[:, sc:sc + 128], p.ident[:])
            p.sMM.inc(last)

    @p.raw_section("vector")
    def _(v):
        base_mm = p.sMM.n - len(groups)
        last = None
        for gi, (c, g0, gl) in enumerate(groups):
            v.wait_ge(p.sMM.h, base_mm + gi + 1)
            dc = dst_off + c * dst_stride + g0 * 128
            last = v.tensor_copy(dst[:, dc:dc + gl * 128], p.psT[:, 0:gl * 128])
            p.sEV.inc(last)
        p.T.inc(v.drain())


def emit_weight_load(p, w_dram, dst, dst_off, n_in, n_out, in_off=0, out_off=0):
    @p.raw_section("gpsimd")
    def _(g):
        g.wait_ge(p.T.h, p.T.n)
        src = w_dram[in_off:in_off + n_in, out_off:out_off + n_out]
        i = g.dma_start(out=dst[:, dst_off:dst_off + (n_in // 128) * n_out]
                        .rearrange("p (c o) -> p c o", o=n_out),
                        in_=src.rearrange("(c p) o -> p c o", p=128))
        p.dW.inc(i, 16)


def emit_row_load(p, dram_vec, dst_off, n, src_off=0):
    @p.raw_section("gpsimd")
    def _(g):
        g.wait_ge(p.T.h, p.T.n)
        i = g.dma_start(out=p.Rb[0:1, dst_off:dst_off + n],
                        in_=dram_vec[src_off:src_off + n].unsqueeze(0))
        p.dW.inc(i, 16)


def emit_proj_tm(p, dst, dst_off, src_fm, src_off, w_sb, w_off, n_in, n_out,
                 bias_off=None, act=None, scale=1.0, ntiles=NT,
                 src_stride=None, dst_stride=None, wait_dw=None, evac="scalar"):
    src_stride = src_stride if src_stride is not None else ntiles * 128
    dst_stride = dst_stride if dst_stride is not None else n_out
    nkc = n_in // 128
    nog = (n_out + 511) // 512
    banks = [p.psA, p.psB]
    njobs = ntiles * nog
    sem = p.sEV if evac == "vector" else p.sAC

    @p.raw_section("tensor")
    def _(t):
        p.wait_all(t)
        if wait_dw is not None:
            t.wait_ge(p.dW.h, wait_dw)
        ev0 = sem.n
        job = 0
        for ti in range(ntiles):
            for og in range(nog):
                ow = min(512, n_out - og * 512)
                ps = banks[job % 2]
                if job >= 2:
                    t.wait_ge(sem.h, ev0 + job - 1)
                last = None
                for kc in range(nkc):
                    last = t.matmul(ps[:, 0:ow],
                                    src_fm[:, src_off + kc * src_stride + ti * 128:
                                           src_off + kc * src_stride + ti * 128 + 128],
                                    w_sb[:, w_off + kc * n_out + og * 512:
                                         w_off + kc * n_out + og * 512 + ow],
                                    start=(kc == 0),
                                    stop=(kc == nkc - 1 and bias_off is None))
                if bias_off is not None:
                    last = t.matmul(ps[:, 0:ow], p.onesb[:],
                                    p.Rb[0:1, bias_off + og * 512:bias_off + og * 512 + ow],
                                    start=False, stop=True)
                p.sMM.inc(last)
                job += 1

    engname = "vector" if evac == "vector" else "scalar"

    @p.raw_section(engname)
    def _(e):
        base_mm = p.sMM.n - njobs
        job = 0
        last = None
        for ti in range(ntiles):
            for og in range(nog):
                ow = min(512, n_out - og * 512)
                ps = banks[job % 2]
                e.wait_ge(p.sMM.h, base_mm + job + 1)
                dcol = dst_off + ti * dst_stride + og * 512
                if evac == "vector":
                    if scale == 1.0:
                        last = e.tensor_copy(dst[:, dcol:dcol + ow], ps[:, 0:ow])
                    else:
                        last = e.tensor_scalar_mul(dst[:, dcol:dcol + ow], ps[:, 0:ow], scale)
                else:
                    last = e.activation(out=dst[:, dcol:dcol + ow], in_=ps[:, 0:ow],
                                        func=(act or AF.Copy), scale=scale)
                sem.inc(last)
                job += 1
        p.T.inc(e.drain())


def emit_proj_residual(p, src_fm, w, n_in, n_out, wait_dw, bias_off=None, ntiles=NT,
                       dst=None):
    sbuf, soff = src_fm
    wbuf, woff = w
    if dst is None:
        dbuf, doff, dstride = p.xres, 0, D
    else:
        dbuf, doff, dstride = dst
    nkc = n_in // 128
    banks = [p.psA, p.psB]
    nog = n_out // 512
    njobs = ntiles * nog

    @p.raw_section("tensor")
    def _(t):
        p.wait_all(t)
        t.wait_ge(p.dW.h, wait_dw)
        ev0 = p.sEV.n
        job = 0
        for ti in range(ntiles):
            for og in range(nog):
                ps = banks[job % 2]
                if job >= 2:
                    t.wait_ge(p.sEV.h, ev0 + job - 1)
                last = None
                for kc in range(nkc):
                    last = t.matmul(ps[:], sbuf[:, soff + kc * (ntiles * 128) + ti * 128:
                                                soff + kc * (ntiles * 128) + ti * 128 + 128],
                                    wbuf[:, woff + kc * n_out + og * 512:
                                         woff + kc * n_out + og * 512 + 512],
                                    start=(kc == 0),
                                    stop=(kc == nkc - 1 and bias_off is None))
                if bias_off is not None:
                    last = t.matmul(ps[:], p.onesb[:],
                                    p.Rb[0:1, bias_off + og * 512:bias_off + og * 512 + 512],
                                    start=False, stop=True)
                p.sMM.inc(last)
                job += 1

    @p.raw_section("vector")
    def _(v):
        base = p.sMM.n - njobs
        job = 0
        last = None
        for ti in range(ntiles):
            for og in range(nog):
                ps = banks[job % 2]
                v.wait_ge(p.sMM.h, base + job + 1)
                xs = dbuf[:, doff + ti * dstride + og * 512: doff + ti * dstride + og * 512 + 512]
                last = v.tensor_add(xs, xs, ps[:])
                p.sEV.inc(last)
                job += 1
        p.T.inc(v.drain())


def emit_rope_quarter(p, src, src_off, dst, dst_off, width=256):
    scr = p.ST[:, 112:368]
    nh = width // 64

    @p.section("vector")
    def _(v):
        last = None
        for ti in range(NT):
            q = src[:, src_off + ti * width: src_off + (ti + 1) * width]
            qh = q.rearrange("p (h d) -> p h d", d=64)
            q1 = qh[:, :, 0:32]
            q2 = qh[:, :, 32:64]
            o = dst[:, dst_off + ti * width: dst_off + (ti + 1) * width]
            oh = o.rearrange("p (h d) -> p h d", d=64)
            o1 = oh[:, :, 0:32]
            o2 = oh[:, :, 32:64]
            cos = p.CS[:, ti * 32:(ti + 1) * 32].unsqueeze(1).broadcast_to([128, nh, 32])
            sin = p.CS[:, 512 + ti * 32:512 + (ti + 1) * 32].unsqueeze(1).broadcast_to([128, nh, 32])
            t1 = scr[:, 0:nh * 32].rearrange("p (h d) -> p h d", d=32)
            t2 = scr[:, 128:128 + nh * 32].rearrange("p (h d) -> p h d", d=32)
            v.tensor_tensor(out=t1, in0=q1, in1=cos, op=OP.mult)
            v.tensor_tensor(out=t2, in0=q2, in1=sin, op=OP.mult)
            v.tensor_tensor(out=o1, in0=t1, in1=t2, op=OP.subtract)
            v.tensor_tensor(out=t1, in0=q1, in1=sin, op=OP.mult)
            v.tensor_tensor(out=t2, in0=q2, in1=cos, op=OP.mult)
            last = v.tensor_tensor(out=o2, in0=t1, in1=t2, op=OP.add)
        return last


def emit_dump(p, name, buf, off, cols, dtype=BF16):
    d = p.dout(name, [128, cols], dtype)

    @p.raw_section("gpsimd")
    def _(g):
        p.wait_all(g)
        i = g.dma_start(out=d[:], in_=buf[:, off:off + cols])
        p.T.inc(i, 16)

def emit_l0_attn(p):
    Wq = p.din("L0_attn_W_q", [D, D])
    Wdkv = p.din("L0_attn_W_dkv", [D, 256])
    Wuk = p.din("L0_attn_W_uk", [256, D])
    Wuv = p.din("L0_attn_W_uv", [256, D])
    Wo = p.din("L0_attn_W_o", [D, D])
    lng = p.din("L0_attn_ln_gamma", [256])
    lnb = p.din("L0_attn_ln_beta", [256])
    cosd = p.din("c_rope_cos", [T, 32])
    sind = p.din("c_rope_sin", [T, 32])
    causd = p.din("c_causal128", [128, 128], BF16)

    XN = 16384
    COS, SIN, CAUS = 3072, 3584, 4096
    GAM = 4352
    emit_ln(p, dst=p.A, dst_off=0)
    emit_transpose(p, dst=p.A, dst_off=XN, src=p.A, src_off=0)

    emit_weight_load(p, Wdkv, p.W, 0, D, 256)
    emit_proj_tm(p, dst=p.E, dst_off=0, src_fm=p.A, src_off=XN, w_sb=p.W, w_off=0,
                 n_in=D, n_out=256, wait_dw=p.dW.n, evac="vector", dst_stride=256)

    mean = p.ST[:, 64:80]
    rstd = p.ST[:, 80:96]
    sumsq = p.ST[:, 96:112]
    scr = p.ST[:, 112:368]

    @p.section("vector")
    def _(v):
        for ti in range(NT):
            st = p.E[:, ti * 256:(ti + 1) * 256]
            v.tensor_reduce(out=mean[:, ti:ti + 1], in_=st, axis=X, op=OP.add)
            v.tensor_mul(scr, st, st)
            v.tensor_reduce(out=sumsq[:, ti:ti + 1], in_=scr, axis=X, op=OP.add)
        v.tensor_scalar_mul(mean[:], mean[:], 1.0 / 256)
        v.tensor_scalar_mul(sumsq[:], sumsq[:], 1.0 / 256)
        v.drain()
        v.tensor_tensor(out=rstd[:], in0=mean[:], in1=mean[:], op=OP.mult)
        v.drain()
        v.tensor_sub(rstd[:], sumsq[:], rstd[:])
        v.drain()
        return v.tensor_scalar_add(rstd[:], rstd[:], 1e-5)

    @p.section("scalar")
    def _(s):
        return s.sqrt(rstd[:], rstd[:])

    @p.section("vector")
    def _(v):
        v.reciprocal(rstd[:], rstd[:])
        v.drain()
        last = None
        for ti in range(NT):
            last = v.tensor_scalar(out=p.B[:, ti * 256:(ti + 1) * 256],
                                   in0=p.E[:, ti * 256:(ti + 1) * 256],
                                   scalar1=mean[:, ti:ti + 1], scalar2=rstd[:, ti:ti + 1],
                                   op0=OP.subtract, op1=OP.mult)
        return last

    emit_transpose(p, dst=p.B, dst_off=4096, src=p.B, src_off=0, width=256)
    LATF = 4096

    gam = p.ST[:, 374:376]

    BET = 4224

    @p.raw_section("gpsimd")
    def _(g):
        g.wait_ge(p.T.h, p.T.n)
        i = g.dma_start(out=gam, in_=lng.rearrange("(c p) -> p c", p=128),
                        allow_slow_non_contiguous=True)
        p.dW.inc(i, 16)
        i = g.dma_start(out=p.W[:, BET:BET + 2], in_=lnb.rearrange("(c p) -> p c", p=128),
                        allow_slow_non_contiguous=True)
        p.dW.inc(i, 16)
        i = g.dma_start(out=p.CS[:, 0:512].rearrange("p (t f) -> p t f", f=32),
                        in_=cosd.rearrange("(t p) f -> p t f", p=128))
        p.dW.inc(i, 16)
        i = g.dma_start(out=p.CS[:, 512:1024].rearrange("p (t f) -> p t f", f=32),
                        in_=sind.rearrange("(t p) f -> p t f", p=128))
        p.dW.inc(i, 16)
        i = g.dma_start(out=p.W[:, CAUS:CAUS + 128], in_=causd[:])
        p.dW.inc(i, 16)

    bet = p.W[:, BET:BET + 2]

    for qq in range(4):
        hsl = qq * 256
        emit_weight_load(p, Wq, p.W, 0, D, 256, out_off=hsl)
        emit_weight_load(p, Wuk, p.W, 2048, 256, 256, out_off=hsl)
        emit_weight_load(p, Wuv, p.W, 2560, 256, 256, out_off=hsl)
        dw_after = p.dW.n

        @p.raw_section("tensor")
        def _(t, dw_after=dw_after):
            t.wait_ge(p.dW.h, dw_after)
            for j, woff in enumerate((2048, 2560)):
                last = None
                for c in range(2):
                    last = t.matmul(p.psC[0:1, j * 256:(j + 1) * 256],
                                    bet[:, c:c + 1],
                                    p.W[:, woff + c * 256:woff + (c + 1) * 256],
                                    start=(c == 0), stop=(c == 1))
                p.sMM.inc(last)

        @p.raw_section("vector")
        def _(v):
            v.wait_ge(p.sMM.h, p.sMM.n)
            v.tensor_copy(p.Rb[0:1, 0:512], p.psC[0:1, 0:512])
            for woff in (2048, 2560):
                for c in range(2):
                    v.tensor_scalar_mul(p.W[:, woff + c * 256:woff + (c + 1) * 256],
                                        p.W[:, woff + c * 256:woff + (c + 1) * 256],
                                        gam[:, c:c + 1])
            p.dinc(v, p.T)

        emit_proj_tm(p, dst=p.E, dst_off=0, src_fm=p.A, src_off=XN, w_sb=p.W, w_off=0,
                     n_in=D, n_out=256, scale=0.125, wait_dw=dw_after,
                     dst_stride=256, evac="vector")
        emit_rope_quarter(p, src=p.E, src_off=0, dst=p.A, dst_off=0)
        emit_transpose(p, dst=p.A, dst_off=4096, src=p.A, src_off=0, width=256)
        emit_proj_tm(p, dst=p.E, dst_off=0, src_fm=p.B, src_off=LATF, w_sb=p.W,
                     w_off=2048, n_in=256, n_out=256, bias_off=0, dst_stride=256,
                     evac="vector")
        emit_rope_quarter(p, src=p.E, src_off=0, dst=p.A, dst_off=8192)
        emit_transpose(p, dst=p.A, dst_off=12288, src=p.A, src_off=8192, width=256)
        emit_proj_tm(p, dst=p.B, dst_off=8192, src_fm=p.B, src_off=LATF, w_sb=p.W,
                     w_off=2560, n_in=256, n_out=256, bias_off=256, dst_stride=256)
        emit_causal_attn_quarter(p, qfm=(p.A, 4096), kfm=(p.A, 12288),
                                 vtm=(p.B, 8192), dst=(p.B, 0),
                                 caus=(p.W, CAUS))
        emit_transpose(p, dst=p.B, dst_off=12288, src=p.B, src_off=0, width=256)
        emit_weight_load(p, Wo, p.W, 4608, 256, D, in_off=256 * qq)
        emit_proj_residual(p, src_fm=(p.B, 12288), w=(p.W, 4608), n_in=256, n_out=D,
                           wait_dw=p.dW.n)


def emit_causal_attn_quarter(p, qfm, kfm, vtm, dst, caus):
    qbuf, qoff = qfm
    kbuf, koff = kfm
    vbuf, voff = vtm
    dbuf, doff = dst
    cbuf, coff = caus
    P_off = 12288
    PT_off = 14336
    nmax = p.ST[:, 368:369]
    sums = p.ST[:, 369:373]
    rinv = p.ST[:, 373:374]

    for h in range(4):
        hd = h * 64
        ch = hd // 128
        hin = hd % 128
        for qi in range(NT):
            nk = (qi + 1) * 128
            nb = (nk + 511) // 512

            @p.raw_section("tensor")
            def _(t, qi=qi, nk=nk, nb=nb, ch=ch, hin=hin):
                p.wait_all(t)
                for b in range(nb):
                    kw = min(512, nk - b * 512)
                    t.matmul(p.psS[:, b * 512:b * 512 + kw],
                             qbuf[hin:hin + 64, qoff + ch * 2048 + qi * 128:
                                  qoff + ch * 2048 + qi * 128 + 128],
                             kbuf[hin:hin + 64, koff + ch * 2048 + b * 512:
                                  koff + ch * 2048 + b * 512 + kw],
                             start=True, stop=(b != nb - 1))
                last = t.matmul(p.psS[:, nk - 128:nk], p.ident[:],
                                cbuf[:, coff:coff + 128], start=False, stop=True)
                p.sMM.inc(last)

            @p.raw_section("vector")
            def _(v, nk=nk):
                v.wait_ge(p.sMM.h, p.sMM.n)
                v.reduce_max(out=nmax, in_=p.psS[:, 0:nk], axis=X, negate=True)
                p.dinc(v, p.sEV)

            @p.raw_section("scalar")
            def _(s, nk=nk, nb=nb):
                s.wait_ge(p.sEV.h, p.sEV.n)
                last = None
                for b in range(nb):
                    kw = min(512, nk - b * 512)
                    last = s.activation(out=p.B[:, P_off + b * 512:P_off + b * 512 + kw],
                                        in_=p.psS[:, b * 512:b * 512 + kw], func=AF.Exp,
                                        bias=nmax, accum_out=sums[:, b:b + 1])
                p.dinc(s, p.sAC)

            @p.raw_section("vector")
            def _(v, nb=nb):
                v.wait_ge(p.sAC.h, p.sAC.n)
                v.tensor_reduce(out=rinv, in_=sums[:, 0:nb], axis=X, op=OP.add)
                v.drain()
                v.reciprocal(rinv, rinv)
                p.dinc(v, p.sEV)

            nkb = qi + 1
            ngr = (nkb + 7) // 8

            @p.raw_section("tensor")
            def _(t, nkb=nkb, ngr=ngr):
                t.wait_ge(p.sAC.h, p.sAC.n)
                ev0 = p.sEV.n
                for gi, g0 in enumerate(range(0, nkb, 8)):
                    gl = min(8, nkb - g0)
                    if gi > 0:
                        t.wait_ge(p.sEV.h, ev0 + gi)
                    last = None
                    for k in range(gl):
                        last = t.transpose(p.psT[:, k * 128:(k + 1) * 128],
                                           p.B[:, P_off + (g0 + k) * 128:
                                               P_off + (g0 + k) * 128 + 128],
                                           p.ident[:])
                    p.sMM.inc(last)

            @p.raw_section("vector")
            def _(v, nkb=nkb, ngr=ngr):
                base = p.sMM.n - ngr
                for gi, g0 in enumerate(range(0, nkb, 8)):
                    gl = min(8, nkb - g0)
                    v.wait_ge(p.sMM.h, base + gi + 1)
                    v.tensor_copy(p.B[:, PT_off + g0 * 128:PT_off + g0 * 128 + gl * 128],
                                  p.psT[:, 0:gl * 128])
                    p.dinc(v, p.sEV)

            @p.raw_section("tensor")
            def _(t, nkb=nkb, hd=hd):
                t.wait_ge(p.sEV.h, p.sEV.n)
                last = None
                for kb in range(nkb):
                    last = t.matmul(p.psA[:, 0:64],
                                    p.B[:, PT_off + kb * 128:PT_off + kb * 128 + 128],
                                    vbuf[:, voff + kb * 256 + hd:voff + kb * 256 + hd + 64],
                                    start=(kb == 0), stop=(kb == nkb - 1))
                p.sMM.inc(last)

            @p.raw_section("vector")
            def _(v, qi=qi, hd=hd):
                v.wait_ge(p.sMM.h, p.sMM.n)
                v.tensor_scalar_mul(dbuf[:, doff + qi * 256 + hd:doff + qi * 256 + hd + 64],
                                    p.psA[:, 0:64], rinv)
                v.drain()
                p.dinc(v, p.sEV)
                p.T.inc(v.drain())


def emit_l0_ffn(p):
    rT = p.din("L0_ffn_router_wT", [D, 8])
    w1 = p.din("L0_ffn_w1", [8, D, 4096])
    b1 = p.din("L0_ffn_b1", [8, 4096])
    w2 = p.din("L0_ffn_w2", [8, 4096, D])
    b2 = p.din("L0_ffn_b2", [8, D])
    idf = p.din("c_ident_f32", [128, 128], F32)

    XN = 16384
    CAP = 513
    emit_ln(p, dst=p.A, dst_off=0)
    emit_transpose(p, dst=p.A, dst_off=XN, src=p.A, src_off=0)

    emit_weight_load(p, rT, p.W, 0, D, 8)
    @p.raw_section("gpsimd")
    def _(g):
        g.wait_ge(p.T.h, p.T.n)
        i = g.dma_start(out=p.CS[:, 0:128], in_=idf[:])
        p.dW.inc(i, 16)
    IDF = 0

    emit_proj_tm(p, dst=p.E, dst_off=0, src_fm=p.A, src_off=XN, w_sb=p.W, w_off=0,
                 n_in=D, n_out=8, wait_dw=p.dW.n, evac="vector", dst_stride=8)

    @p.section("vector")
    def _(v):
        lg = p.E[:, 0:128].rearrange("p (t e) -> p t e", e=8)
        m = p.ST[:, 128:144]
        v.tensor_reduce(out=m, in_=lg, axis=X, op=OP.max)
        v.drain()
        v.tensor_tensor(out=lg, in0=lg, in1=m.unsqueeze(2).broadcast_to([128, 16, 8]),
                        op=OP.subtract)
        return v.drain()

    @p.section("scalar")
    def _(s):
        return s.activation(out=p.E[:, 0:128], in_=p.E[:, 0:128], func=AF.Exp)

    @p.section("vector")
    def _(v):
        lg = p.E[:, 0:128].rearrange("p (t e) -> p t e", e=8)
        sm = p.ST[:, 144:160]
        v.tensor_reduce(out=sm, in_=lg, axis=X, op=OP.add)
        v.drain()
        v.reciprocal(sm, sm)
        v.drain()
        v.tensor_tensor(out=lg, in0=lg, in1=sm.unsqueeze(2).broadcast_to([128, 16, 8]),
                        op=OP.mult)
        return v.drain()

    @p.raw_section("tensor")
    def _(t):
        p.wait_all(t)
        ev0 = p.sEV.n
        for ti in range(NT):
            if ti >= 1:
                t.wait_ge(p.sEV.h, ev0 + ti)
            i = t.transpose(p.psC[0:8, 0:128],
                            p.E[:, ti * 8:(ti + 1) * 8], p.CS[:, IDF:IDF + 128])
            p.sMM.inc(i)

    @p.raw_section("vector")
    def _(v):
        base = p.sMM.n - NT
        for ti in range(NT):
            v.wait_ge(p.sMM.h, base + ti + 1)
            v.tensor_copy(p.E[0:8, 2048 + ti * 128:2048 + (ti + 1) * 128],
                          p.psC[0:8, 0:128])
            p.dinc(v, p.sEV)
        p.T.inc(v.drain())

    SCF = 2048

    a = p.ST[0:8, 160:161]
    bb = p.ST[0:8, 161:162]
    mid = p.ST[0:8, 162:163]
    cnt = p.ST[0:8, 163:164]
    sel = p.ST[0:8, 164:165]
    d1 = p.ST[0:8, 165:166]
    d2 = p.ST[0:8, 166:167]
    cmpb = p.B[0:8, 0:2048]

    @p.section("vector")
    def _(v):
        v.memset(a, 0.0)
        v.memset(bb, 1.0)
        v.drain()
        for it in range(40):
            v.tensor_tensor(out=mid, in0=a, in1=bb, op=OP.add)
            v.drain()
            v.tensor_scalar_mul(mid, mid, 0.5)
            v.drain()
            v.tensor_single_scalar(out=cmpb, in_=p.E[0:8, SCF:SCF + 2048], scalar=mid,
                                   op=OP.is_ge)
            v.tensor_reduce(out=cnt, in_=cmpb, axis=X, op=OP.add)
            v.drain()
            v.tensor_single_scalar(out=sel, in_=cnt, scalar=float(CAP) - 0.5,
                                   op=OP.is_ge)
            v.drain()
            v.tensor_sub(d1, mid, a)
            v.tensor_sub(d2, bb, mid)
            v.drain()
            v.scalar_tensor_tensor(out=a, in0=d1, scalar=sel, in1=a,
                                   op0=OP.mult, op1=OP.add)
            v.scalar_tensor_tensor(out=bb, in0=d2, scalar=sel, in1=mid,
                                   op0=OP.mult, op1=OP.add)
            v.drain()
        v.tensor_single_scalar(out=cmpb, in_=p.E[0:8, SCF:SCF + 2048], scalar=a,
                               op=OP.is_ge)
        v.drain()
        v.tensor_tensor(out=p.E[0:8, SCF:SCF + 2048], in0=p.E[0:8, SCF:SCF + 2048],
                        in1=cmpb, op=OP.mult)
        return v.drain()

    @p.raw_section("tensor")
    def _(t):
        p.wait_all(t)
        ev0 = p.sEV.n
        for ti in range(NT):
            if ti >= 1:
                t.wait_ge(p.sEV.h, ev0 + ti)
            i = t.transpose(p.psC[:, 0:8],
                            p.E[0:8, SCF + ti * 128:SCF + (ti + 1) * 128],
                            p.CS[0:8, IDF:IDF + 8])
            p.sMM.inc(i)

    @p.raw_section("vector")
    def _(v):
        base = p.sMM.n - NT
        for ti in range(NT):
            v.wait_ge(p.sMM.h, base + ti + 1)
            v.tensor_copy(p.ST[:, 256 + ti * 8:256 + (ti + 1) * 8], p.psC[:, 0:8])
            p.dinc(v, p.sEV)
        p.T.inc(v.drain())

    CW = 256

    for e in range(8):
        emit_moe_expert(p, e, w1, b1, w2, b2, xn_off=XN, cw_col=lambda ti, e=e: CW + ti * 8 + e,
                        act=AF.Gelu, add_xf=False)


def emit_moe_expert(p, e, w1, b1, w2, b2, xn_off, cw_col, act, add_xf,
                    h_scale=1.0, xn_buf=None, h_off=0, h_buf=None,
                    out_dst=None, premul=None, premul_dst=None):
    xn_buf = xn_buf if xn_buf is not None else p.A
    h_buf = h_buf if h_buf is not None else p.B
    w1e = w1[e] if e is not None else w1
    w2e = w2[e] if e is not None else w2
    b1e = b1[e] if e is not None else b1
    b2e = b2[e] if e is not None else b2

    @p.raw_section("gpsimd")
    def _(g):
        g.wait_ge(p.T.h, p.T.n)
        i = g.dma_start(out=p.CS[:, 128:160], in_=b1e.rearrange("(c p) -> p c", p=128),
                        allow_slow_non_contiguous=True)
        p.dW.inc(i, 16)
        i = g.dma_start(out=p.Rb[0:1, 0:1024], in_=b2e.unsqueeze(0))
        p.dW.inc(i, 16)
    bias_dw = p.dW.n

    if premul is not None:
        gcol, bcol = premul
        nbuf, noff = premul_dst

        @p.section("vector")
        def _(v):
            last = None
            for dc in range(8):
                last = v.tensor_scalar(out=nbuf[:, noff + dc * 2048:noff + (dc + 1) * 2048],
                                       in0=xn_buf[:, xn_off + dc * (NT * 128):
                                                  xn_off + (dc + 1) * (NT * 128)],
                                       scalar1=gcol(dc), scalar2=bcol(dc),
                                       op0=OP.mult, op1=OP.add)
            return last
        rhs_buf, rhs_off = nbuf, noff
    else:
        rhs_buf, rhs_off = xn_buf, xn_off

    w1_banks = [p.psS[:, 0:512], p.psS[:, 512:1024], p.psS[:, 1024:1536],
                p.psS[:, 1536:2048]]
    w2_pairs = [(p.psA[:], p.psB[:]), (p.psC[:], p.psS[:, 0:512])]

    for g in range(8):
        @p.raw_section("gpsimd")
        def _(gg, g=g):
            if g == 0:
                gg.wait_ge(p.T.h, p.T.n)
            else:
                gg.wait_ge(p.sAC.h, p.sAC.n)
            i = gg.dma_start(out=p.W[:, 0:4096].rearrange("p (c m) -> p c m", m=512),
                             in_=w1e[:, g * 512:(g + 1) * 512]
                             .rearrange("(c p) m -> p c m", p=128))
            p.dW.inc(i, 16)
        w1_dw = p.dW.n

        @p.raw_section("gpsimd")
        def _(gg, g=g):
            if g > 0:
                gg.wait_ge(p.sEV.h, p.sEV.n)
            i = gg.dma_start(out=p.W[:, 4096:8192]
                             .rearrange("p (c m) -> p c m", m=1024),
                             in_=w2e[g * 512:(g + 1) * 512, :]
                             .rearrange("(c p) m -> p c m", p=128))
            p.dW.inc(i, 16)
        w2_dw = p.dW.n

        ac0 = p.sAC.n

        @p.raw_section("tensor")
        def _(t, g=g, ac0=ac0):
            t.wait_ge(p.dW.h, w1_dw)
            if g == 0:
                p.wait_all(t)
            t.wait_ge(p.sEV.h, p.sEV.n)
            for hc in range(4):
                for dc in range(8):
                    for tg in range(4):
                        if dc == 0 and hc >= 1:
                            t.wait_ge(p.sAC.h, ac0 + 4 * (hc - 1) + tg + 1)
                        last = t.matmul(
                            w1_banks[tg],
                            p.W[:, dc * 512 + hc * 128:dc * 512 + hc * 128 + 128],
                            rhs_buf[:, rhs_off + dc * (NT * 128) + tg * 512:
                                    rhs_off + dc * (NT * 128) + tg * 512 + 512],
                            start=(dc == 0), stop=(dc == 7))
                        if dc == 7:
                            p.sMM.inc(last)

        @p.raw_section("scalar")
        def _(s, g=g):
            base = p.sMM.n - 16
            if g == 0:
                s.wait_ge(p.dW.h, bias_dw)
            job = 0
            for hc in range(4):
                for tg in range(4):
                    s.wait_ge(p.sMM.h, base + job + 1)
                    s.activation(out=h_buf[:, h_off + hc * 2048 + tg * 512:
                                           h_off + hc * 2048 + tg * 512 + 512],
                                 in_=w1_banks[tg], func=act,
                                 bias=p.CS[:, 128 + g * 4 + hc:129 + g * 4 + hc],
                                 scale=h_scale)
                    p.dinc(s, p.sAC)
                    job += 1

        ev0 = p.sEV.n
        ac_all = p.sAC.n

        @p.raw_section("tensor")
        def _(t, g=g, ev0=ev0, ac_all=ac_all):
            t.wait_ge(p.dW.h, w2_dw)
            t.wait_ge(p.sAC.h, ac_all)
            for ti in range(16):
                pa, pb = w2_pairs[ti % 2]
                if ti >= 2:
                    t.wait_ge(p.sEV.h, ev0 + 2 * (ti - 1))
                last = None
                for hc in range(4):
                    for oh in range(2):
                        ps = pa if oh == 0 else pb
                        last = t.matmul(ps,
                                        h_buf[:, h_off + hc * 2048 + ti * 128:
                                              h_off + hc * 2048 + ti * 128 + 128],
                                        p.W[:, 4096 + hc * 1024 + oh * 512:
                                            4096 + hc * 1024 + oh * 512 + 512],
                                        start=(hc == 0), stop=(hc == 3 and g != 7))
                if g == 7:
                    for oh in range(2):
                        ps = pa if oh == 0 else pb
                        last = t.matmul(ps, p.onesb[:],
                                        p.Rb[0:1, oh * 512:oh * 512 + 512],
                                        start=False, stop=True)
                p.sMM.inc(last)

        @p.raw_section("vector")
        def _(v, g=g):
            base = p.sMM.n - 16
            for ti in range(16):
                pa, pb = w2_pairs[ti % 2]
                v.wait_ge(p.sMM.h, base + ti + 1)
                for oh in range(2):
                    ps = pa if oh == 0 else pb
                    psap = ps
                    if out_dst is None:
                        xs = p.xres[:, ti * D + oh * 512: ti * D + oh * 512 + 512]
                    else:
                        obuf, ooff = out_dst
                        xs = obuf[:, ooff + ti * D + oh * 512:ooff + ti * D + oh * 512 + 512]
                    if cw_col is None:
                        v.tensor_add(xs, xs, psap)
                    else:
                        v.scalar_tensor_tensor(out=xs, in0=psap,
                                               scalar=p.ST[:, cw_col(ti):cw_col(ti) + 1],
                                               in1=xs, op0=OP.mult, op1=OP.add)
                p.dinc(v, p.sEV, 2)
            if g == 7:
                p.T.inc(v.drain())


def emit_l1_attn(p, core_id_dram_names=True):
    WT = p.din("L1_attn_in_proj_weightT", [D, 3 * D])
    bias = p.din("L1_attn_in_proj_bias", [3 * D])
    rho = p.din("rho_c", [128, 8])
    outWT = p.din("L1_attn_out_wT", [D, D])
    outb = p.din("L1_attn_out_b", [D])

    kv_in = p.dint("l1_kv_in", [2 * T, D], BF16)
    kv_out = p.dint("l1_kv_out", [NCORES * 2 * T, D], BF16)

    XN = 16384
    emit_ln(p, dst=p.A, dst_off=0)
    emit_transpose(p, dst=p.A, dst_off=XN, src=p.A, src_off=0)

    for j, (wo, boff) in enumerate(((D, D), (2 * D, 2 * D))):
        emit_weight_load(p, WT, p.W, 0, D, D, out_off=wo)
        emit_row_load(p, bias, 0, D, src_off=boff)
        emit_proj_tm(p, dst=p.B, dst_off=0, src_fm=p.A, src_off=XN, w_sb=p.W, w_off=0,
                     n_in=D, n_out=D, bias_off=0, wait_dw=p.dW.n)

        @p.section("gpsimd")
        def _(g, j=j):
            return g.dma_start(out=kv_in[j * T:(j + 1) * T, :].rearrange("(t p) d -> p t d", p=128),
                               in_=p.B[:].rearrange("p (t d) -> p t d", d=D))

    @p.raw_section("gpsimd")
    def _(g):
        p.wait_all(g)
        g.collective_compute("AllGather", OP.bypass, replica_groups=[list(range(NCORES))],
                             ins=[kv_in[:]], outs=[kv_out[:]]).then_inc(p.sCC.h)
        p.sCC.n += 1

    emit_weight_load(p, WT, p.W, 0, D, D, out_off=0)
    emit_row_load(p, bias, 0, D, src_off=0)
    emit_proj_tm(p, dst=p.B, dst_off=0, src_fm=p.A, src_off=XN, w_sb=p.W, w_off=0,
                 n_in=D, n_out=D, bias_off=0, scale=0.125, wait_dw=p.dW.n)
    emit_transpose(p, dst=p.A, dst_off=0, src=p.B, src_off=0)

    @p.raw_section("gpsimd")
    def _(g):
        g.wait_ge(p.T.h, p.T.n)
        i = g.dma_start(out=p.W[0:128, 0:8], in_=rho[:])
        p.dW.inc(i, 16)
    rel_dw = p.dW.n

    REL = 16384

    @p.raw_section("tensor")
    def _(t):
        p.wait_all(t)
        t.wait_ge(p.dW.h, rel_dw)
        ev0 = p.sEV.n
        job = 0
        for ti in range(NT):
            for h in range(16):
                ch, hin = (h * 64) // 128, (h * 64) % 128
                if job >= 2:
                    t.wait_ge(p.sEV.h, ev0 + job - 1)
                i = t.matmul(p.psA[0:128, 0:8],
                             p.A[hin:hin + 64, ch * 2048 + ti * 128: ch * 2048 + ti * 128 + 128],
                             p.W[hin:hin + 64, 0:8], start=True, stop=True)
                p.sMM.inc(i)
                job += 1

    @p.raw_section("vector")
    def _(v):
        base = p.sMM.n - NT * 16
        job = 0
        for ti in range(NT):
            for h in range(16):
                v.wait_ge(p.sMM.h, base + job + 1)
                v.tensor_copy(p.A[:, REL + ti * 128 + h * 8: REL + ti * 128 + h * 8 + 8],
                              p.psA[:, 0:8])
                p.dinc(v, p.sEV)
                job += 1
        p.T.inc(v.drain())

    for ti in range(NT):
        @p.raw_section("gpsimd")
        def _(g, ti=ti):
            if ti == 0:
                p.wait_all(g)
                g.wait_ge(p.sCC.h, p.sCC.n)
            else:
                g.wait_ge(p.sEV.h, p.sEV.n)
            for s in range(NCORES):
                i = g.dma_start(out=p.W[:, s * 1024:(s + 1) * 1024],
                                in_=kv_out[(s * 2) * T + ti * 128:(s * 2) * T + ti * 128 + 128, :])
                p.dW.inc(i, 16)

        @p.raw_section("vector")
        def _(v, ti=ti):
            v.wait_ge(p.dW.h, p.dW.n)
            for s in range(NCORES):
                v.tensor_tensor(out=p.E[:, 1024:2048], in0=p.B[:, ti * 1024:(ti + 1) * 1024],
                                in1=p.W[:, s * 1024:(s + 1) * 1024], op=OP.mult)
                v.tensor_reduce(out=p.E[:, 0:128].rearrange("p (h s) -> p h s", s=8)[:, :, s:s + 1].squeeze(2),
                                in_=p.E[:, 1024:2048].rearrange("p (h d) -> p h d", d=64),
                                axis=X, op=OP.add)
            v.drain()
            v.tensor_tensor(out=p.E[:, 0:128], in0=p.E[:, 0:128],
                            in1=p.A[:, REL + ti * 128:REL + (ti + 1) * 128], op=OP.add)
            sview = p.E[:, 0:128].rearrange("p (h s) -> p h s", s=8)
            m = p.ST[:, 384:400]
            v.drain()
            v.tensor_reduce(out=m, in_=sview, axis=X, op=OP.max)
            v.drain()
            v.tensor_tensor(out=sview, in0=sview, in1=m.unsqueeze(2).broadcast_to([128, 16, 8]),
                            op=OP.subtract)
            p.dinc(v, p.sEV)

        @p.raw_section("scalar")
        def _(s_, ti=ti):
            s_.wait_ge(p.sEV.h, p.sEV.n)
            s_.activation(out=p.E[:, 0:128], in_=p.E[:, 0:128], func=AF.Exp)
            p.dinc(s_, p.sAC)

        @p.raw_section("gpsimd")
        def _(g, ti=ti):
            g.wait_ge(p.sAC.h, p.sAC.n)
            for s in range(NCORES):
                i = g.dma_start(out=p.W[:, s * 1024:(s + 1) * 1024],
                                in_=kv_out[(s * 2 + 1) * T + ti * 128:(s * 2 + 1) * T + ti * 128 + 128, :])
                p.dW.inc(i, 16)

        @p.raw_section("vector")
        def _(v, ti=ti):
            v.wait_ge(p.sAC.h, p.sAC.n)
            sview = p.E[:, 0:128].rearrange("p (h s) -> p h s", s=8)
            sm = p.ST[:, 400:416]
            v.tensor_reduce(out=sm, in_=sview, axis=X, op=OP.add)
            v.drain()
            v.reciprocal(sm, sm)
            v.drain()
            v.tensor_tensor(out=sview, in0=sview, in1=sm.unsqueeze(2).broadcast_to([128, 16, 8]),
                            op=OP.mult)
            v.drain()
            v.wait_ge(p.dW.h, p.dW.n)
            for s in range(NCORES):
                pb = p.E[:, 0:128].rearrange("p (h s) -> p h s", s=8)[:, :, s:s + 1] \
                    .broadcast_to([128, 16, 64])
                vv = p.W[:, s * 1024:(s + 1) * 1024].rearrange("p (h d) -> p h d", d=64)
                if s == 0:
                    v.tensor_tensor(out=p.E[:, 2048:3072].rearrange("p (h d) -> p h d", d=64),
                                    in0=vv, in1=pb, op=OP.mult)
                else:
                    v.tensor_tensor(out=p.E[:, 1024:2048].rearrange("p (h d) -> p h d", d=64),
                                    in0=vv, in1=pb, op=OP.mult)
                    v.tensor_add(p.E[:, 2048:3072], p.E[:, 2048:3072], p.E[:, 1024:2048])
            v.drain()
            v.tensor_copy(p.A[:, ti * 1024:(ti + 1) * 1024], p.E[:, 2048:3072])
            p.dinc(v, p.sEV)
            p.T.inc(v.drain())

    emit_transpose(p, dst=p.A, dst_off=XN, src=p.A, src_off=0)
    emit_weight_load(p, outWT, p.W, 0, D, D)
    emit_row_load(p, outb, 0, D)
    emit_proj_residual(p, src_fm=(p.A, XN), w=(p.W, 0), n_in=D, n_out=D,
                       wait_dw=p.dW.n, bias_off=0)


def emit_quantize(p, wT_dram, n_in, n_out, q_dram, onescol, eps=1e-5):
    nch = n_in // 128
    dbl = (2 * n_out <= 4096)
    acc = p.ST[:, 416:417]
    tmp = p.ST[:, 417:418]
    thr = p.ST[:, 418:419]
    thr_d = p.dint("q_thr", [1, 1], F32)

    @p.section("vector")
    def _(v):
        v.memset(acc, 0.0)
        return v.drain()

    for c in range(nch):
        bo = (c % 2) * n_out if dbl else 0

        @p.raw_section("gpsimd")
        def _(g, c=c, bo=bo):
            if c == 0:
                g.wait_ge(p.T.h, p.T.n)
            else:
                g.wait_ge(p.sEV.h, p.sEV.n - (1 if (dbl and c >= 2) else 0))
            i = g.dma_start(out=p.E[:, bo:bo + n_out],
                            in_=wT_dram[c * 128:(c + 1) * 128, :])
            p.dW.inc(i, 16)

        @p.raw_section("vector")
        def _(v, c=c, bo=bo):
            v.wait_ge(p.dW.h, p.dW.n)
            v.tensor_reduce(out=tmp, in_=p.E[:, bo:bo + n_out],
                            axis=X, op=OP.add, apply_absolute_value=True)
            v.drain()
            v.tensor_add(acc, acc, tmp)
            p.dinc(v, p.sEV)

    @p.raw_section("tensor")
    def _(t):
        t.wait_ge(p.sEV.h, p.sEV.n)
        i = t.matmul(p.psA[0:1, 0:1], acc, onescol[:], start=True, stop=True)
        p.sMM.inc(i)

    @p.raw_section("vector")
    def _(v):
        v.wait_ge(p.sMM.h, p.sMM.n)
        v.tensor_scalar(out=thr[0:1], in0=p.psA[0:1, 0:1],
                        scalar1=0.5 / (n_in * n_out), scalar2=0.5 * eps,
                        op0=OP.mult, op1=OP.add)
        p.dinc(v, p.sEV)

    @p.raw_section("gpsimd")
    def _(g):
        g.wait_ge(p.sEV.h, p.sEV.n)
        i = g.dma_start(out=thr_d[:], in_=thr[0:1])
        p.dW.inc(i, 16)
        g.wait_ge(p.dW.h, p.dW.n)
        i = g.dma_start(out=thr, in_=thr_d.broadcast_to([128, 1]))
        p.dW.inc(i, 16)

    @p.section("vector")
    def _(v):
        v.wait_ge(p.dW.h, p.dW.n)
        return v.drain()

    for c in range(nch):
        bo = (c % 2) * n_out if dbl else 0

        @p.raw_section("gpsimd")
        def _(g, c=c, bo=bo):
            if c == 0:
                g.wait_ge(p.T.h, p.T.n)
            else:
                g.wait_ge(p.sEV.h, p.sEV.n - (1 if (dbl and c >= 2) else 0))
            i = g.dma_start(out=p.E[:, bo:bo + n_out],
                            in_=wT_dram[c * 128:(c + 1) * 128, :])
            p.dW.inc(i, 16)

        @p.raw_section("vector")
        def _(v, c=c, bo=bo):
            v.wait_ge(p.dW.h, p.dW.n)
            e = p.E[:, bo:bo + n_out]
            v.tensor_single_scalar(out=p.B[:, bo:bo + n_out], in_=e, scalar=thr, op=OP.is_ge)
            v.tensor_scalar(out=p.B[:, 8192 + bo:8192 + bo + n_out], in0=e,
                            scalar1=-1.0, scalar2=thr, op0=OP.mult, op1=OP.is_ge)
            v.drain()
            v.tensor_sub(p.B[:, bo:bo + n_out], p.B[:, bo:bo + n_out],
                         p.B[:, 8192 + bo:8192 + bo + n_out])
            p.dinc(v, p.sEV)

        @p.raw_section("gpsimd")
        def _(g, c=c, bo=bo):
            g.wait_ge(p.sEV.h, p.sEV.n)
            i = g.dma_start(out=q_dram[c * 128:(c + 1) * 128, :],
                            in_=p.B[:, bo:bo + n_out])
            p.dW.inc(i, 16)

    @p.section("vector")
    def _(v):
        return v.drain()


def emit_l1_ffn(p):
    W1T = p.din("L1_ffn_W1T", [D, 4096])
    b1 = p.din("L1_ffn_b1", [4096])
    W2T = p.din("L1_ffn_W2T", [4096, D])
    b2 = p.din("L1_ffn_b2", [D])
    onescol = p.din("c_ones_col", [128, 1])
    w1q = p.dint("l1_w1q", [D, 4096], BF16)
    w2q = p.dint("l1_w2q", [4096, D], BF16)
    oc = p.sb("onescol_sb", [128, 1], F32)

    @p.raw_section("gpsimd")
    def _(g):
        g.wait_ge(p.T.h, p.T.n)
        i = g.dma_start(out=oc[:], in_=onescol[:])
        p.dW.inc(i, 16)

    @p.section("vector")
    def _(v):
        v.wait_ge(p.dW.h, p.dW.n)
        return v.drain()

    emit_quantize(p, W1T, D, 4096, w1q, oc)
    emit_quantize(p, W2T, 4096, D, w2q, oc)

    XN = 16384
    emit_ln(p, dst=p.A, dst_off=0)
    emit_transpose(p, dst=p.A, dst_off=XN, src=p.A, src_off=0)
    emit_moe_expert(p, None, w1q, b1, w2q, b2, xn_off=XN, cw_col=None,
                    act=AF.Relu, add_xf=False)


def emit_l2_attn(p):
    Wq = p.din("L2_attn_Wq", [D, D]);  bq = p.din("L2_attn_bq", [D])
    Wk = p.din("L2_attn_Wk", [D, D]);  bk = p.din("L2_attn_bk", [D])
    Wv = p.din("L2_attn_Wv", [D, D]);  bv = p.din("L2_attn_bv", [D])
    Wo = p.din("L2_attn_Wo", [D, D]);  bo = p.din("L2_attn_bo", [D])
    ds2 = p.din("L2_attn_dist_scale2d", [1, 1])
    W1 = p.din("L2_attn_W1", [D, 4096]); b1 = p.din("L2_attn_b1", [4096])
    W2 = p.din("L2_attn_W2", [4096, D]); b2 = p.din("L2_attn_b2", [D])
    g1 = p.din("L2_attn_ln1_g", [D]); be1 = p.din("L2_attn_ln1_b", [D])
    g2 = p.din("L2_attn_ln2_g", [D]); be2 = p.din("L2_attn_ln2_b", [D])
    cdist = p.din("c_dist", [28, 128, 512])
    pen_d = p.dint("l2_pen", [28, 128, 512], BF16)
    dsb = p.ST[:, 419:420]

    XN = 16384
    @p.raw_section("gpsimd")
    def _(g):
        g.wait_ge(p.T.h, p.T.n)
        i = g.dma_start(out=dsb, in_=ds2.broadcast_to([128, 1]))
        p.dW.inc(i, 16)

    @p.section("vector")
    def _(v):
        v.wait_ge(p.dW.h, p.dW.n)
        v.tensor_scalar_mul(dsb, dsb, -0.125)
        return v.drain()

    for r in range(28):
        @p.raw_section("gpsimd")
        def _(g, r=r):
            if r == 0:
                g.wait_ge(p.T.h, p.T.n)
            else:
                g.wait_ge(p.sEV.h, p.sEV.n)
            i = g.dma_start(out=p.E[:, 0:512], in_=cdist[r])
            p.dW.inc(i, 16)

        @p.raw_section("vector")
        def _(v, r=r):
            v.wait_ge(p.dW.h, p.dW.n)
            v.tensor_scalar_mul(p.B[:, 0:512], p.E[:, 0:512], dsb)
            p.dinc(v, p.sEV)

        @p.raw_section("gpsimd")
        def _(g, r=r):
            g.wait_ge(p.sEV.h, p.sEV.n)
            i = g.dma_start(out=pen_d[r], in_=p.B[:, 0:512])
            p.dW.inc(i, 16)

    emit_ln(p, dst=p.A, dst_off=0)
    emit_transpose(p, dst=p.A, dst_off=XN, src=p.A, src_off=0)

    for e8 in range(8):
        dsl = e8 * 128
        emit_weight_load(p, Wq, p.W, 0, D, 128, out_off=dsl)
        emit_weight_load(p, Wk, p.W, 1024, D, 128, out_off=dsl)
        emit_weight_load(p, Wv, p.W, 2048, D, 128, out_off=dsl)
        emit_row_load(p, bq, 0, 128, src_off=dsl)
        emit_row_load(p, bk, 256, 128, src_off=dsl)
        emit_row_load(p, bv, 512, 128, src_off=dsl)
        dw = p.dW.n
        emit_proj_tm(p, dst=p.B, dst_off=0, src_fm=p.A, src_off=XN, w_sb=p.W, w_off=0,
                     n_in=D, n_out=128, bias_off=0, scale=0.125, wait_dw=dw,
                     dst_stride=128)
        emit_transpose(p, dst=p.B, dst_off=2048, src=p.B, src_off=0, width=128)
        emit_proj_tm(p, dst=p.B, dst_off=4096, src_fm=p.A, src_off=XN, w_sb=p.W,
                     w_off=1024, n_in=D, n_out=128, bias_off=256, wait_dw=dw,
                     dst_stride=128)
        emit_transpose(p, dst=p.B, dst_off=6144, src=p.B, src_off=4096, width=128)
        emit_proj_tm(p, dst=p.B, dst_off=8192, src_fm=p.A, src_off=XN, w_sb=p.W,
                     w_off=2048, n_in=D, n_out=128, bias_off=512, wait_dw=dw,
                     dst_stride=128)
        emit_l2_attn_8(p, e8, pen_d)
        emit_transpose(p, dst=p.B, dst_off=6144, src=p.B, src_off=10240, width=128)
        emit_weight_load(p, Wo, p.W, 3072, 128, D, in_off=dsl)
        if e8 == 7:
            emit_row_load(p, bo, 1024, D)
        emit_proj_residual(p, src_fm=(p.B, 6144), w=(p.W, 3072), n_in=128, n_out=D,
                           wait_dw=p.dW.n, dst=(p.A, 0, D),
                           bias_off=(1024 if e8 == 7 else None))

    @p.raw_section("gpsimd")
    def _(g):
        g.wait_ge(p.T.h, p.T.n)
        i = g.dma_start(out=p.W[:, 0:1024], in_=g1.unsqueeze(0).broadcast_to([128, D]))
        p.dW.inc(i, 16)
        i = g.dma_start(out=p.W[:, 1024:2048], in_=be1.unsqueeze(0).broadcast_to([128, D]))
        p.dW.inc(i, 16)

    @p.section("vector")
    def _(v):
        v.wait_ge(p.dW.h, p.dW.n)
        return v.drain()

    emit_ln(p, dst=p.B, dst_off=0, src=p.A, src_off=0,
            affine=(p.W[:, 0:1024], p.W[:, 1024:2048]))
    emit_transpose(p, dst=p.A, dst_off=XN, src=p.B, src_off=0)
    emit_moe_expert(p, None, W1, b1, W2, b2, xn_off=XN, cw_col=None, act=AF.Relu,
                    add_xf=False, xn_buf=p.A, h_buf=p.A, h_off=0, out_dst=(p.B, 0))
    @p.raw_section("gpsimd")
    def _(g):
        g.wait_ge(p.T.h, p.T.n)
        i = g.dma_start(out=p.W[:, 0:1024], in_=g2.unsqueeze(0).broadcast_to([128, D]))
        p.dW.inc(i, 16)
        i = g.dma_start(out=p.W[:, 1024:2048], in_=be2.unsqueeze(0).broadcast_to([128, D]))
        p.dW.inc(i, 16)

    @p.section("vector")
    def _(v):
        v.wait_ge(p.dW.h, p.dW.n)
        return v.drain()

    emit_ln(p, dst=p.A, dst_off=0, src=p.B, src_off=0,
            affine=(p.W[:, 0:1024], p.W[:, 1024:2048]))

    @p.section("vector")
    def _(v):
        last = None
        for ti in range(NT):
            last = v.tensor_add(p.xres[:, ti * D:(ti + 1) * D],
                                p.xres[:, ti * D:(ti + 1) * D],
                                p.A[:, ti * 1024:(ti + 1) * 1024])
        return last


def emit_l2_attn_8(p, e8, pen_d):
    QF, KF, VT, CO, P_off, PT_off, PEN = 2048, 6144, 8192, 10240, 12288, 14336, 4096
    nmax = p.ST[:, 368:369]
    sums = p.ST[:, 369:373]
    rinv = p.ST[:, 373:374]
    ac_base = p.sAC.n

    for qi in range(NT):
        @p.raw_section("gpsimd")
        def _(g, qi=qi):
            if qi == 0:
                p.wait_all(g)
            else:
                g.wait_ge(p.sAC.h, ac_base + 2 * qi)
            for kc in range(4):
                r = (qi * 128 - kc * 512 + 1536) // 128
                i = g.dma_start(out=p.W[:, PEN + kc * 512:PEN + (kc + 1) * 512],
                                in_=pen_d[r])
                p.dW.inc(i, 16)

        for h in range(2):
            hglob = e8 * 2 + h
            hin = h * 64

            @p.raw_section("tensor")
            def _(t, qi=qi, h=h, hin=hin):
                if qi == 0 and h == 0:
                    p.wait_all(t)
                t.wait_ge(p.dW.h, p.dW.n)
                if not (qi == 0 and h == 0):
                    t.wait_ge(p.sAC.h, p.sAC.n)
                for b in range(4):
                    t.matmul(p.psS[:, b * 512:(b + 1) * 512],
                             p.B[hin:hin + 64, QF + qi * 128:QF + qi * 128 + 128],
                             p.B[hin:hin + 64, KF + b * 512:KF + (b + 1) * 512],
                             start=True, stop=False)
                last = None
                for b in range(4):
                    last = t.matmul(p.psS[:, b * 512:(b + 1) * 512], p.ident[:],
                                    p.W[:, PEN + b * 512:PEN + (b + 1) * 512],
                                    start=False, stop=True)
                p.sMM.inc(last)

            @p.raw_section("vector")
            def _(v, hglob=hglob):
                v.wait_ge(p.sMM.h, p.sMM.n)
                if hglob > 0:
                    v.tensor_scalar_add(p.psS[:, 0:hglob], p.psS[:, 0:hglob], NEG)
                    v.drain()
                v.reduce_max(out=nmax, in_=p.psS[:, 0:2048], axis=X, negate=True)
                p.dinc(v, p.sEV)

            @p.raw_section("scalar")
            def _(s):
                s.wait_ge(p.sEV.h, p.sEV.n)
                for b in range(4):
                    s.activation(out=p.B[:, P_off + b * 512:P_off + (b + 1) * 512],
                                 in_=p.psS[:, b * 512:(b + 1) * 512], func=AF.Exp,
                                 bias=nmax, accum_out=sums[:, b:b + 1])
                p.dinc(s, p.sAC)

            @p.raw_section("vector")
            def _(v):
                v.wait_ge(p.sAC.h, p.sAC.n)
                v.tensor_reduce(out=rinv, in_=sums, axis=X, op=OP.add)
                v.drain()
                v.reciprocal(rinv, rinv)
                p.dinc(v, p.sEV)

            @p.raw_section("tensor")
            def _(t):
                t.wait_ge(p.sAC.h, p.sAC.n)
                ev0 = p.sEV.n
                for gi, g0 in enumerate(range(0, 16, 8)):
                    if gi > 0:
                        t.wait_ge(p.sEV.h, ev0 + gi)
                    last = None
                    for k in range(8):
                        last = t.transpose(p.psT[:, k * 128:(k + 1) * 128],
                                           p.B[:, P_off + (g0 + k) * 128:
                                               P_off + (g0 + k) * 128 + 128],
                                           p.ident[:])
                    p.sMM.inc(last)

            @p.raw_section("vector")
            def _(v):
                base = p.sMM.n - 2
                for gi, g0 in enumerate(range(0, 16, 8)):
                    v.wait_ge(p.sMM.h, base + gi + 1)
                    v.tensor_copy(p.B[:, PT_off + g0 * 128:PT_off + g0 * 128 + 1024],
                                  p.psT[:, 0:1024])
                    p.dinc(v, p.sEV)

            @p.raw_section("tensor")
            def _(t, hin=hin):
                t.wait_ge(p.sEV.h, p.sEV.n)
                last = None
                for kb in range(16):
                    last = t.matmul(p.psA[:, 0:64],
                                    p.B[:, PT_off + kb * 128:PT_off + kb * 128 + 128],
                                    p.B[:, VT + kb * 128 + hin:VT + kb * 128 + hin + 64],
                                    start=(kb == 0), stop=(kb == 15))
                p.sMM.inc(last)

            @p.raw_section("vector")
            def _(v, qi=qi, hin=hin):
                v.wait_ge(p.sMM.h, p.sMM.n)
                v.tensor_scalar_mul(p.B[:, CO + qi * 128 + hin:CO + qi * 128 + hin + 64],
                                    p.psA[:, 0:64], rinv)
                v.drain()
                p.dinc(v, p.sEV)
                p.T.inc(v.drain())


def emit_l2_ffn(p):
    Wg = p.din("L2_ffn_Wg", [D, 8])
    gP = p.din("L2_ffn_gammaP", [128, 64])
    bP = p.din("L2_ffn_betaP", [128, 64])
    W1 = p.din("L2_ffn_W1", [8, D, 4096])
    b1 = p.din("L2_ffn_b1", [8, 4096])
    W2 = p.din("L2_ffn_W2", [8, 4096, D])
    b2 = p.din("L2_ffn_b2", [8, D])

    XN = 16384
    emit_ln(p, dst=p.A, dst_off=0)
    emit_transpose(p, dst=p.A, dst_off=XN, src=p.A, src_off=0)

    @p.raw_section("gpsimd")
    def _(g):
        g.wait_ge(p.T.h, p.T.n)
        i = g.dma_start(out=p.CS[:, 160:224], in_=gP[:])
        p.dW.inc(i, 16)
        i = g.dma_start(out=p.CS[:, 224:288], in_=bP[:])
        p.dW.inc(i, 16)

    emit_weight_load(p, Wg, p.W, 0, D, 8)
    emit_proj_tm(p, dst=p.E, dst_off=0, src_fm=p.A, src_off=XN, w_sb=p.W, w_off=0,
                 n_in=D, n_out=8, wait_dw=p.dW.n, evac="vector", dst_stride=8)

    m1 = p.ST[:, 384:400]; m2 = p.ST[:, 400:416]
    e2c = p.ST[:, 420:436]; w1n = p.ST[:, 436:452]; w2n = p.ST[:, 452:468]

    @p.section("vector")
    def _(v):
        lg = p.E[:, 0:128].rearrange("p (t e) -> p t e", e=8)
        eq1 = p.E[:, 128:256].rearrange("p (t e) -> p t e", e=8)
        t2 = p.E[:, 256:384].rearrange("p (t e) -> p t e", e=8)
        v.tensor_reduce(out=m1, in_=lg, axis=X, op=OP.max)
        v.drain()
        v.tensor_tensor(out=eq1, in0=lg, in1=m1.unsqueeze(2).broadcast_to([128, 16, 8]),
                        op=OP.is_equal)
        v.drain()
        v.scalar_tensor_tensor(out=t2, in0=eq1, scalar=NEG, in1=lg, op0=OP.mult, op1=OP.add)
        v.drain()
        v.tensor_reduce(out=m2, in_=t2, axis=X, op=OP.max)
        v.drain()
        v.tensor_sub(e2c, m2, m1)
        return v.drain()

    @p.section("scalar")
    def _(s):
        return s.activation(out=e2c, in_=e2c, func=AF.Exp)

    @p.section("vector")
    def _(v):
        lg = p.E[:, 0:128].rearrange("p (t e) -> p t e", e=8)
        eq1 = p.E[:, 128:256].rearrange("p (t e) -> p t e", e=8)
        t2 = p.E[:, 256:384].rearrange("p (t e) -> p t e", e=8)
        eq2 = p.E[:, 384:512].rearrange("p (t e) -> p t e", e=8)
        cw = p.ST[:, 256:384].rearrange("p (t e) -> p t e", e=8)
        v.tensor_scalar_add(w1n, e2c, 1.0)
        v.drain()
        v.reciprocal(w1n, w1n)
        v.drain()
        v.tensor_tensor(out=w2n, in0=e2c, in1=w1n, op=OP.mult)
        v.tensor_tensor(out=eq2, in0=t2, in1=m2.unsqueeze(2).broadcast_to([128, 16, 8]),
                        op=OP.is_equal)
        v.drain()
        v.tensor_tensor(out=cw, in0=eq1, in1=w1n.unsqueeze(2).broadcast_to([128, 16, 8]),
                        op=OP.mult)
        v.tensor_tensor(out=eq2, in0=eq2, in1=w2n.unsqueeze(2).broadcast_to([128, 16, 8]),
                        op=OP.mult)
        v.drain()
        v.tensor_add(cw, cw, eq2)
        return v.drain()

    @p.section("vector")
    def _(v):
        last = None
        for ti in range(NT):
            last = v.tensor_add(p.xres[:, ti * D:(ti + 1) * D],
                                p.xres[:, ti * D:(ti + 1) * D],
                                p.A[:, ti * 1024:(ti + 1) * 1024])
        return last

    emit_ln(p, dst=p.B, dst_off=0, src=p.A, src_off=0, eps=1e-6)
    emit_transpose(p, dst=p.A, dst_off=XN, src=p.B, src_off=0)

    CW = 256
    for e in range(8):
        emit_moe_expert(p, e, W1, b1, W2, b2, xn_off=XN,
                        cw_col=lambda ti, e=e: CW + ti * 8 + e,
                        act=AF.Relu, add_xf=False,
                        premul=(lambda dc, e=e: p.CS[:, 160 + e * 8 + dc:161 + e * 8 + dc],
                                lambda dc, e=e: p.CS[:, 224 + e * 8 + dc:225 + e * 8 + dc]),
                        premul_dst=(p.A, 0))


def emit_l3_attn(p):
    WqT = p.din("L3_attn_WqT", [D, D])
    WkT = p.din("L3_attn_WkT", [D, 256])
    WvT = p.din("L3_attn_WvT", [D, 256])
    WoT = p.din("L3_attn_WoT", [D, D])
    mb = p.din("c_l3_bandmask", [2, 128, 176], BF16)
    ms = p.din("c_l3_stridemask", [NT, 128, 44], BF16)
    v_d = p.dint("l3_v", [T, 256], BF16)

    XN = 16384
    Eb = p.E[:].bitcast(BF16)
    emit_ln(p, dst=p.A, dst_off=0)
    emit_transpose(p, dst=p.A, dst_off=XN, src=p.A, src_off=0)

    emit_weight_load(p, WkT, p.W, 0, D, 256)
    emit_weight_load(p, WvT, p.W, 2048, D, 256)
    dw = p.dW.n
    emit_proj_tm(p, dst=p.B, dst_off=0, src_fm=p.A, src_off=XN, w_sb=p.W, w_off=0,
                 n_in=D, n_out=256, wait_dw=dw, dst_stride=256)
    emit_proj_tm(p, dst=Eb, dst_off=4096, src_fm=p.A, src_off=XN, w_sb=p.W,
                 w_off=2048, n_in=D, n_out=256, dst_stride=256)
    emit_transpose(p, dst=p.B, dst_off=4096, src=p.B, src_off=0, width=256)
    emit_weight_load(p, WqT, p.W, 0, D, D)
    emit_proj_tm(p, dst=p.A, dst_off=0, src_fm=p.A, src_off=XN, w_sb=p.W, w_off=0,
                 n_in=D, n_out=D, scale=0.125, wait_dw=p.dW.n)
    @p.raw_section("gpsimd")
    def _(g):
        p.wait_all(g)
        for g4 in range(4):
            ch, base = g4 // 2, (g4 % 2) * 64
            i = g.dma_start(out=p.W[0:64, g4 * 2048:(g4 + 1) * 2048],
                            in_=p.B[base:base + 64, 4096 + ch * 2048:4096 + (ch + 1) * 2048])
            p.dW.inc(i, 16)
            i = g.dma_start(out=p.W[64:128, g4 * 2048:(g4 + 1) * 2048],
                            in_=p.B[base:base + 64, 4096 + ch * 2048:4096 + (ch + 1) * 2048])
            p.dW.inc(i, 16)
        i = g.dma_start(out=v_d.rearrange("(t p) d -> p t d", p=128),
                        in_=Eb[:, 4096:8192].rearrange("p (t d) -> p t d", d=256))
        p.dW.inc(i, 16)
        g.wait_ge(p.dW.h, p.dW.n)
        i = g.dma_start(out=Eb[0:44, 896:1152],
                        in_=v_d[45:2024:46, :])
        p.dW.inc(i, 16)
        i = g.dma_start(out=Eb[:, 1280:1456], in_=mb[0])
        p.dW.inc(i, 16)
        i = g.dma_start(out=Eb[:, 1456:1632], in_=mb[1])
        p.dW.inc(i, 16)

    @p.section("vector")
    def _(v):
        v.wait_ge(p.dW.h, p.dW.n)
        v.memset(Eb[:, 640:896], 0.0)
        return v.drain()

    emit_transpose(p, dst=p.B, dst_off=0, src=p.A, src_off=0)

    nmax = p.ST[:, 368:369]
    sums = p.ST[:, 369:370]
    rinv = p.ST[:, 373:374]

    for qi in range(NT):
        @p.raw_section("gpsimd")
        def _(g, qi=qi):
            if qi == 0:
                p.wait_all(g)
            else:
                g.wait_ge(p.sAC.h, p.sAC.n)
            i = g.dma_start(out=Eb[:, 1664:1708], in_=ms[qi])
            p.dW.inc(i, 16)
            if qi > 0:
                i = g.dma_start(out=Eb[0:48, 640:896],
                                in_=Eb[80:128, 4096 + (qi - 1) * 256:4096 + qi * 256])
                p.dW.inc(i, 16)

        for h in range(16):
            g4 = h // 4
            qch, qbase = h // 2, (h % 2) * 64
            kbase = qbase

            @p.raw_section("tensor")
            def _(t, qi=qi, h=h, g4=g4, qch=qch, qbase=qbase):
                if qi == 0 and h == 0:
                    p.wait_all(t)
                t.wait_ge(p.dW.h, p.dW.n)
                if not (qi == 0 and h == 0):
                    t.wait_ge(p.sAC.h, p.sAC.n)
                qf = p.B[qbase:qbase + 64,
                         qch * 2048 + qi * 128:qch * 2048 + qi * 128 + 128]
                ka = max(0, qi * 128 - 48)
                t.matmul(p.psS[:, 0:48],
                         qf, p.W[qbase:qbase + 64, g4 * 2048 + ka:g4 * 2048 + ka + 48],
                         start=True, stop=True)
                t.matmul(p.psS[:, 48:176],
                         qf, p.W[qbase:qbase + 64, g4 * 2048 + qi * 128:g4 * 2048 + qi * 128 + 128],
                         start=True, stop=True)
                t.matmul(p.psS[:, 176:220],
                         qf, p.W[qbase:qbase + 64, g4 * 2048 + 45:g4 * 2048 + 2024:46],
                         start=True, stop=True)
                mb_off = 1280 if qi == 0 else 1456
                t.matmul(p.psS[:, 0:176], p.ident[:], Eb[:, mb_off:mb_off + 176],
                         start=False, stop=True)
                last = t.matmul(p.psS[:, 176:220], p.ident[:], Eb[:, 1664:1708],
                                start=False, stop=True)
                p.sMM.inc(last)

            @p.raw_section("vector")
            def _(v):
                v.wait_ge(p.sMM.h, p.sMM.n)
                v.reduce_max(out=nmax, in_=p.psS[:, 0:220], axis=X, negate=True)
                p.dinc(v, p.sEV)

            @p.raw_section("scalar")
            def _(s):
                s.wait_ge(p.sEV.h, p.sEV.n)
                s.activation(out=Eb[:, 0:220], in_=p.psS[:, 0:220], func=AF.Exp,
                             bias=nmax, accum_out=sums)
                p.dinc(s, p.sAC)

            @p.raw_section("vector")
            def _(v):
                v.wait_ge(p.sAC.h, p.sAC.n)
                v.reciprocal(rinv, sums)
                p.dinc(v, p.sEV)

            @p.raw_section("tensor")
            def _(t):
                t.wait_ge(p.sAC.h, p.sAC.n)
                t.transpose(p.psT[0:48, 0:128], Eb[:, 0:48], p.ident[:])
                t.transpose(p.psT[:, 128:256], Eb[:, 48:176], p.ident[:])
                i = t.transpose(p.psT[0:44, 256:384], Eb[:, 176:220], p.ident[:])
                p.sMM.inc(i)

            @p.raw_section("vector")
            def _(v):
                v.wait_ge(p.sMM.h, p.sMM.n)
                v.tensor_copy(Eb[0:48, 256:384], p.psT[0:48, 0:128])
                v.tensor_copy(Eb[:, 384:512], p.psT[:, 128:256])
                v.tensor_copy(Eb[0:44, 512:640], p.psT[0:44, 256:384])
                p.dinc(v, p.sEV)

            @p.raw_section("tensor")
            def _(t, qi=qi, g4=g4):
                t.wait_ge(p.sEV.h, p.sEV.n)
                vd = g4 * 64
                t.matmul(p.psA[:, 0:64], Eb[0:48, 256:384],
                         Eb[0:48, 640 + vd:640 + vd + 64], start=True, stop=False)
                t.matmul(p.psA[:, 0:64], Eb[:, 384:512],
                         Eb[:, 4096 + qi * 256 + vd:4096 + qi * 256 + vd + 64],
                         start=False, stop=False)
                i = t.matmul(p.psA[:, 0:64], Eb[0:44, 512:640],
                             Eb[0:44, 896 + vd:896 + vd + 64], start=False, stop=True)
                p.sMM.inc(i)

            @p.raw_section("vector")
            def _(v, qi=qi, h=h):
                v.wait_ge(p.sMM.h, p.sMM.n)
                v.tensor_scalar_mul(p.A[:, qi * 1024 + h * 64:qi * 1024 + h * 64 + 64],
                                    p.psA[:, 0:64], rinv)
                v.drain()
                p.dinc(v, p.sEV)
                p.T.inc(v.drain())

    emit_transpose(p, dst=p.A, dst_off=XN, src=p.A, src_off=0)
    emit_weight_load(p, WoT, p.W, 0, D, D)
    emit_proj_residual(p, src_fm=(p.A, XN), w=(p.W, 0), n_in=D, n_out=D,
                       wait_dw=p.dW.n)


def emit_l3_ffn(p):
    w1 = p.din("L3_ffn_w1", [8, D, 4096])
    b1 = p.din("L3_ffn_b1", [8, 4096])
    w2 = p.din("L3_ffn_w2", [8, 4096, D])
    b2 = p.din("L3_ffn_b2", [8, D])
    gw = p.din("L3_ffn_gate_w", [D, 8])
    gb = p.din("L3_ffn_gate_b", [8])

    XN = 16384
    emit_ln(p, dst=p.A, dst_off=0)
    emit_transpose(p, dst=p.A, dst_off=XN, src=p.A, src_off=0)

    emit_weight_load(p, gw, p.W, 0, D, 8)
    emit_row_load(p, gb, 0, 8)
    emit_proj_tm(p, dst=p.E, dst_off=0, src_fm=p.A, src_off=XN, w_sb=p.W, w_off=0,
                 n_in=D, n_out=8, bias_off=0, wait_dw=p.dW.n, evac="vector",
                 dst_stride=8)

    m1 = p.ST[:, 384:400]
    CW = 256

    @p.section("vector")
    def _(v):
        lg = p.E[:, 0:128].rearrange("p (t e) -> p t e", e=8)
        cw = p.ST[:, 256:384].rearrange("p (t e) -> p t e", e=8)
        v.tensor_reduce(out=m1, in_=lg, axis=X, op=OP.max)
        v.drain()
        v.tensor_tensor(out=cw, in0=lg, in1=m1.unsqueeze(2).broadcast_to([128, 16, 8]),
                        op=OP.is_equal)
        return v.drain()

    for e in range(8):
        emit_moe_expert(p, e, w1, b1, w2, b2, xn_off=XN,
                        cw_col=lambda ti, e=e: CW + ti * 8 + e,
                        act=AF.Relu, add_xf=False)


def stage_host(inputs):
    H = dict(host_constants())
    x = np.ascontiguousarray(np.asarray(inputs["L0_attn_x"], dtype=np.float32))
    H["xin"] = [np.ascontiguousarray(x[c]) for c in range(NCORES)]
    for k, v in inputs.items():
        if k != "L0_attn_x":
            H[k] = np.ascontiguousarray(np.asarray(v, dtype=np.float32))
    H["L0_ffn_router_wT"] = np.ascontiguousarray(H["L0_ffn_router_w"].T)
    H["c_ident_f32"] = np.eye(128, dtype=np.float32)
    H["c_ones_col"] = np.ones((128, 1), dtype=np.float32)
    H["L1_attn_in_proj_weightT"] = np.ascontiguousarray(H["L1_attn_in_proj_weight"].T)
    H["L1_attn_out_wT"] = np.ascontiguousarray(H["L1_attn_out_w"].T)
    H["L1_ffn_W1T"] = np.ascontiguousarray(H["L1_ffn_W1"].T)
    H["L1_ffn_W2T"] = np.ascontiguousarray(H["L1_ffn_W2"].T)
    rpk = H["L1_attn_rel_pos_keys"]
    H["rho_c"] = [np.ascontiguousarray(np.vstack([rpk[np.arange(8) - c + 16, :64].T]*2)) for c in range(NCORES)]
    H["L2_attn_dist_scale2d"] = np.asarray(H["L2_attn_dist_scale"], dtype=np.float32).reshape(1, 1)
    a128 = np.arange(128); b512 = np.arange(512)
    dist = np.empty((28, 128, 512), dtype=np.float32)
    for r in range(28):
        dist[r] = np.abs((r * 128 - 1536) + a128[:, None] - b512[None, :]).astype(np.float32)
    H["c_dist"] = dist
    H["L2_ffn_gammaP"] = np.ascontiguousarray(
        H["L2_ffn_gamma"].reshape(8, 8, 128).transpose(2, 0, 1).reshape(128, 64))
    H["L2_ffn_betaP"] = np.ascontiguousarray(
        H["L2_ffn_beta"].reshape(8, 8, 128).transpose(2, 0, 1).reshape(128, 64))
    H["L3_attn_WqT"] = np.ascontiguousarray(H["L3_attn_Wq"].T)
    H["L3_attn_WkT"] = np.ascontiguousarray(H["L3_attn_Wk"].T)
    H["L3_attn_WvT"] = np.ascontiguousarray(H["L3_attn_Wv"].T)
    H["L3_attn_WoT"] = np.ascontiguousarray(H["L3_attn_Wo"].T)
    a = np.arange(128)[:, None]
    j = np.arange(176)[None, :]
    rel = j - 48 - a
    base_valid = (rel >= -45) & (rel <= 0)
    m1v = np.where(base_valid, 0.0, NEG).astype(np.float32)
    m0v = np.where(base_valid & (j >= 48), 0.0, NEG).astype(np.float32)
    H["c_l3_bandmask"] = np.stack([m0v, m1v]).astype(BF)
    sidx = np.arange(45, 2048, 46)
    sm = np.empty((NT, 128, 44), dtype=np.float32)
    for qi in range(NT):
        q = qi * 128 + np.arange(128)[:, None]
        sm[qi] = np.where(sidx[None, :] <= q - 46, 0.0, NEG)
    H["c_l3_stridemask"] = sm.astype(BF)
    return H


def run_launch(p, H):
    in_maps = []
    for c in range(NCORES):
        m = {}
        for name in p.inputs:
            v = H[name]
            m[name] = v[c] if isinstance(v, list) else v
        in_maps.append(m)
    res = run_bass_kernel_spmd(p.nc, in_maps, core_ids=list(range(NCORES)))
    return res.results


LAYER_EMITTERS = [emit_l0_attn, emit_l0_ffn, emit_l1_attn, emit_l1_ffn,
                  emit_l2_attn, emit_l2_ffn, emit_l3_attn, emit_l3_ffn]


def build_layer_program(emitters):
    p = Prog()
    p.setup()
    emit_load_consts(p)
    emit_load_x(p)
    for em in emitters:
        em(p)
    emit_store_x(p)
    p.ctx.close()
    return p


def kernel(**inputs):
    H = stage_host(inputs)
    p = build_layer_program(LAYER_EMITTERS)
    results = run_launch(p, H)
    return np.stack([results[c]["xout"] for c in range(NCORES)], axis=0).astype(np.float32)


# version 04 (final)
# speedup vs baseline: 1.8174x; 1.0168x over previous
import numpy as np
import ml_dtypes
from contextlib import ExitStack

import concourse.bass as bass
import concourse.mybir as mybir
from concourse.bass_utils import run_bass_kernel_spmd

F32 = mybir.dt.float32
BF16 = mybir.dt.bfloat16
AF = mybir.ActivationFunctionType
OP = mybir.AluOpType
BF = ml_dtypes.bfloat16

NCORES = 8
T = 2048
D = 1024
NT = 16
NEG = -1.0e30
X = mybir.AxisListType.X


def host_constants():
    c = {}
    c["c_ident_bf"] = np.eye(128, dtype=np.float32).astype(BF)
    c["c_ones_row"] = np.ones((1, 128), dtype=BF)
    half = 32
    ang = (np.arange(T, dtype=np.float32)[:, None] *
           (1.0 / (10000.0 ** (np.arange(half, dtype=np.float32) / np.float32(half)))
            ).astype(np.float32)[None, :])
    c["c_rope_sin"] = np.sin(ang).astype(np.float32)
    c["c_rope_cos"] = np.cos(ang).astype(np.float32)
    a = np.arange(128)
    c["c_causal128"] = np.where(a[:, None] >= a[None, :], 0.0, NEG).astype(np.float32).astype(BF)
    return c


class Sem:
    def __init__(self, prog, name):
        self.h = prog.ctx.enter_context(prog.nc.semaphore(name))
        self.n = 0

    def inc(self, inst, k=1):
        inst.then_inc(self.h, k)
        self.n += k
        return self.n

    def wait(self, eng, val=None):
        eng.wait_ge(self.h, self.n if val is None else val)


class Prog:

    def __init__(self):
        self.nc = bass.Bass("TRN2", target_bir_lowering=False)
        self.ctx = ExitStack()
        self.inputs = {}
        self.outputs = {}
        self._sb = {}

    def din(self, name, shape, dtype=F32):
        if name not in self.inputs:
            self.inputs[name] = self.nc.dram_tensor(name, list(shape), dtype,
                                                    kind="ExternalInput").ap()
        return self.inputs[name]

    def dout(self, name, shape, dtype=F32):
        if name not in self.outputs:
            self.outputs[name] = self.nc.dram_tensor(name, list(shape), dtype,
                                                     kind="ExternalOutput").ap()
        return self.outputs[name]

    def dint(self, name, shape, dtype=F32, addr_space="Local"):
        key = "int_" + name
        if key not in self._sb:
            self._sb[key] = self.nc.dram_tensor(name, list(shape), dtype,
                                                kind="Internal", addr_space=addr_space).ap()
        return self._sb[key]

    def sb(self, name, shape, dtype):
        if name not in self._sb:
            self._sb[name] = self.ctx.enter_context(
                self.nc.sbuf_tensor(name, list(shape), dtype))
        return self._sb[name]

    def ps(self, name, shape, dtype=F32):
        if name not in self._sb:
            self._sb[name] = self.ctx.enter_context(
                self.nc.psum_tensor(name, list(shape), dtype))
        return self._sb[name]

    def setup(self):
        self.block = self.ctx.enter_context(self.nc.Block())
        self.T = Sem(self, "Tl")
        self.dW = Sem(self, "dW")
        self.sMM = Sem(self, "sMM")
        self.sEV = Sem(self, "sEV")
        self.sAC = Sem(self, "sAC")
        self.sGP = Sem(self, "sGP")
        self.sCC = Sem(self, "sCC")
        self.xres = self.sb("xres", [128, NT * D], F32)
        self.A = self.sb("arenaA", [128, 32768], BF16)
        self.B = self.sb("arenaB", [128, 16384], BF16)
        self.E = self.sb("arenaE", [128, 4096], F32)
        self.ST = self.sb("stats", [128, 512], F32)
        self.W = self.sb("wstream", [128, 8192], BF16)
        self.Rb = self.sb("rowbuf", [1, 2048], BF16)
        self.ident = self.sb("identbf", [128, 128], BF16)
        self.onesb = self.sb("onesrow", [1, 128], BF16)
        self.CS = self.sb("cossin", [128, 1024], F32)
        self.psS = self.ps("psS", [128, 2048], F32)
        self.psA = self.ps("psA", [128, 512], F32)
        self.psB = self.ps("psB", [128, 512], F32)
        self.psC = self.ps("psC", [128, 512], F32)
        self.psT = self.ps("psT", [128, 1024], BF16)

    def wait_all(self, eng):
        eng.wait_ge(self.T.h, self.T.n)

    def dinc(self, eng, sem, k=1):
        sem.inc(eng.drain(), k)

    def section(self, engname):
        def deco(fn):
            @getattr(self.block, engname)
            def _(eng):
                self.wait_all(eng)
                last = fn(eng)
                if last is not None:
                    if engname in ("sync", "gpsimd"):
                        self.T.inc(last, 16)
                    elif engname == "tensor":
                        self.T.inc(last, 1)
                    else:
                        self.dinc(eng, self.T)
        return deco

    def raw_section(self, engname):
        def deco(fn):
            @getattr(self.block, engname)
            def _(eng):
                fn(eng)
        return deco


def emit_load_consts(p):
    ib = p.din("c_ident_bf", [128, 128], BF16)
    orow = p.din("c_ones_row", [1, 128], BF16)

    @p.section("sync")
    def _(s):
        i = s.dma_start(out=p.ident[:], in_=ib[:])
        p.T.inc(i, 16)
        return s.dma_start(out=p.onesb[:], in_=orow[:])


def emit_load_x(p, src_name="xin"):
    xin = p.din(src_name, [T, D], F32)

    @p.section("sync")
    def _(s):
        return s.dma_start(out=p.xres[:].rearrange("p (t d) -> p t d", d=D),
                           in_=xin.rearrange("(t p) d -> p t d", p=128))


def emit_store_x(p, dst_name="xout"):
    xout = p.dout(dst_name, [T, D], F32)

    @p.raw_section("gpsimd")
    def _(g):
        p.wait_all(g)
        i = g.dma_start(out=xout.rearrange("(t p) d -> p t d", p=128),
                        in_=p.xres[:].rearrange("p (t d) -> p t d", d=D))
        p.T.inc(i, 16)
        g.wait_ge(p.T.h, p.T.n)


def emit_ln(p, dst, dst_off, width=D, src=None, src_off=0, eps=1e-5,
            ntiles=NT, src_stride=None, dst_stride=None, stat_col=0,
            scratch=None, affine=None):
    src = src if src is not None else p.xres
    src_stride = src_stride if src_stride is not None else width
    dst_stride = dst_stride if dst_stride is not None else width
    scratch = scratch if scratch is not None else p.E[:, 0:width]
    mean = p.ST[:, stat_col:stat_col + ntiles]
    rstd = p.ST[:, stat_col + ntiles:stat_col + 2 * ntiles]
    sumsq = p.ST[:, stat_col + 2 * ntiles:stat_col + 3 * ntiles]

    @p.section("vector")
    def _(v):
        for ti in range(ntiles):
            st = src[:, src_off + ti * src_stride: src_off + ti * src_stride + width]
            v.tensor_reduce(out=mean[:, ti:ti + 1], in_=st, axis=X, op=OP.add)
            v.tensor_mul(scratch, st, st)
            v.tensor_reduce(out=sumsq[:, ti:ti + 1], in_=scratch, axis=X, op=OP.add)
        v.tensor_scalar_mul(mean[:], mean[:], 1.0 / width)
        v.tensor_scalar_mul(sumsq[:], sumsq[:], 1.0 / width)
        v.drain()
        v.tensor_tensor(out=rstd[:], in0=mean[:], in1=mean[:], op=OP.mult)
        v.drain()
        v.tensor_sub(rstd[:], sumsq[:], rstd[:])
        v.drain()
        return v.tensor_scalar_add(rstd[:], rstd[:], float(eps))

    @p.section("scalar")
    def _(s):
        return s.sqrt(rstd[:], rstd[:])

    @p.section("vector")
    def _(v):
        v.reciprocal(rstd[:], rstd[:])
        v.drain()
        last = None
        for ti in range(ntiles):
            st = src[:, src_off + ti * src_stride: src_off + ti * src_stride + width]
            dt_ = dst[:, dst_off + ti * dst_stride: dst_off + ti * dst_stride + width]
            if affine is None:
                last = v.tensor_scalar(out=dt_, in0=st, scalar1=mean[:, ti:ti + 1],
                                       scalar2=rstd[:, ti:ti + 1],
                                       op0=OP.subtract, op1=OP.mult)
            else:
                g_ap, b_ap = affine
                v.tensor_scalar(out=scratch, in0=st, scalar1=mean[:, ti:ti + 1],
                                scalar2=rstd[:, ti:ti + 1],
                                op0=OP.subtract, op1=OP.mult)
                v.tensor_mul(scratch, scratch, g_ap)
                last = v.tensor_tensor(out=dt_, in0=scratch, in1=b_ap, op=OP.add)
        return last


def emit_transpose(p, dst, dst_off, src, src_off, ntiles=NT, width=D,
                   src_stride=None, dst_stride=None):
    src_stride = src_stride if src_stride is not None else width
    dst_stride = dst_stride if dst_stride is not None else ntiles * 128
    nch = width // 128
    groups = []
    for c in range(nch):
        for g0 in range(0, ntiles, 8):
            groups.append((c, g0, min(8, ntiles - g0)))

    @p.raw_section("tensor")
    def _(t):
        p.wait_all(t)
        ev0 = p.sEV.n
        for gi, (c, g0, gl) in enumerate(groups):
            if gi >= 1:
                t.wait_ge(p.sEV.h, ev0 + gi)
            last = None
            for k in range(gl):
                ti = g0 + k
                sc = src_off + ti * src_stride + c * 128
                last = t.transpose(p.psT[:, k * 128:(k + 1) * 128],
                                   src[:, sc:sc + 128], p.ident[:])
            p.sMM.inc(last)

    @p.raw_section("vector")
    def _(v):
        base_mm = p.sMM.n - len(groups)
        last = None
        for gi, (c, g0, gl) in enumerate(groups):
            v.wait_ge(p.sMM.h, base_mm + gi + 1)
            dc = dst_off + c * dst_stride + g0 * 128
            last = v.tensor_copy(dst[:, dc:dc + gl * 128], p.psT[:, 0:gl * 128])
            p.sEV.inc(last)
        p.T.inc(v.drain())


def emit_weight_load(p, w_dram, dst, dst_off, n_in, n_out, in_off=0, out_off=0):
    @p.raw_section("gpsimd")
    def _(g):
        g.wait_ge(p.T.h, p.T.n)
        src = w_dram[in_off:in_off + n_in, out_off:out_off + n_out]
        i = g.dma_start(out=dst[:, dst_off:dst_off + (n_in // 128) * n_out]
                        .rearrange("p (c o) -> p c o", o=n_out),
                        in_=src.rearrange("(c p) o -> p c o", p=128))
        p.dW.inc(i, 16)


def emit_row_load(p, dram_vec, dst_off, n, src_off=0):
    @p.raw_section("gpsimd")
    def _(g):
        g.wait_ge(p.T.h, p.T.n)
        i = g.dma_start(out=p.Rb[0:1, dst_off:dst_off + n],
                        in_=dram_vec[src_off:src_off + n].unsqueeze(0))
        p.dW.inc(i, 16)


def emit_proj_tm(p, dst, dst_off, src_fm, src_off, w_sb, w_off, n_in, n_out,
                 bias_off=None, act=None, scale=1.0, ntiles=NT,
                 src_stride=None, dst_stride=None, wait_dw=None, evac="scalar"):
    src_stride = src_stride if src_stride is not None else ntiles * 128
    dst_stride = dst_stride if dst_stride is not None else n_out
    nkc = n_in // 128
    nog = (n_out + 511) // 512
    banks = [p.psA, p.psB]
    njobs = ntiles * nog
    sem = p.sEV if evac == "vector" else p.sAC

    @p.raw_section("tensor")
    def _(t):
        p.wait_all(t)
        if wait_dw is not None:
            t.wait_ge(p.dW.h, wait_dw)
        ev0 = sem.n
        job = 0
        for ti in range(ntiles):
            for og in range(nog):
                ow = min(512, n_out - og * 512)
                ps = banks[job % 2]
                if job >= 2:
                    t.wait_ge(sem.h, ev0 + job - 1)
                last = None
                for kc in range(nkc):
                    last = t.matmul(ps[:, 0:ow],
                                    src_fm[:, src_off + kc * src_stride + ti * 128:
                                           src_off + kc * src_stride + ti * 128 + 128],
                                    w_sb[:, w_off + kc * n_out + og * 512:
                                         w_off + kc * n_out + og * 512 + ow],
                                    start=(kc == 0),
                                    stop=(kc == nkc - 1 and bias_off is None))
                if bias_off is not None:
                    last = t.matmul(ps[:, 0:ow], p.onesb[:],
                                    p.Rb[0:1, bias_off + og * 512:bias_off + og * 512 + ow],
                                    start=False, stop=True)
                p.sMM.inc(last)
                job += 1

    engname = "vector" if evac == "vector" else "scalar"

    @p.raw_section(engname)
    def _(e):
        base_mm = p.sMM.n - njobs
        job = 0
        last = None
        for ti in range(ntiles):
            for og in range(nog):
                ow = min(512, n_out - og * 512)
                ps = banks[job % 2]
                e.wait_ge(p.sMM.h, base_mm + job + 1)
                dcol = dst_off + ti * dst_stride + og * 512
                if evac == "vector":
                    if scale == 1.0:
                        last = e.tensor_copy(dst[:, dcol:dcol + ow], ps[:, 0:ow])
                    else:
                        last = e.tensor_scalar_mul(dst[:, dcol:dcol + ow], ps[:, 0:ow], scale)
                else:
                    last = e.activation(out=dst[:, dcol:dcol + ow], in_=ps[:, 0:ow],
                                        func=(act or AF.Copy), scale=scale)
                sem.inc(last)
                job += 1
        p.T.inc(e.drain())


def emit_proj_residual(p, src_fm, w, n_in, n_out, wait_dw, bias_off=None, ntiles=NT,
                       dst=None):
    sbuf, soff = src_fm
    wbuf, woff = w
    if dst is None:
        dbuf, doff, dstride = p.xres, 0, D
    else:
        dbuf, doff, dstride = dst
    nkc = n_in // 128
    banks = [p.psA, p.psB]
    nog = n_out // 512
    njobs = ntiles * nog

    @p.raw_section("tensor")
    def _(t):
        p.wait_all(t)
        t.wait_ge(p.dW.h, wait_dw)
        ev0 = p.sEV.n
        job = 0
        for ti in range(ntiles):
            for og in range(nog):
                ps = banks[job % 2]
                if job >= 2:
                    t.wait_ge(p.sEV.h, ev0 + job - 1)
                last = None
                for kc in range(nkc):
                    last = t.matmul(ps[:], sbuf[:, soff + kc * (ntiles * 128) + ti * 128:
                                                soff + kc * (ntiles * 128) + ti * 128 + 128],
                                    wbuf[:, woff + kc * n_out + og * 512:
                                         woff + kc * n_out + og * 512 + 512],
                                    start=(kc == 0),
                                    stop=(kc == nkc - 1 and bias_off is None))
                if bias_off is not None:
                    last = t.matmul(ps[:], p.onesb[:],
                                    p.Rb[0:1, bias_off + og * 512:bias_off + og * 512 + 512],
                                    start=False, stop=True)
                p.sMM.inc(last)
                job += 1

    @p.raw_section("vector")
    def _(v):
        base = p.sMM.n - njobs
        job = 0
        last = None
        for ti in range(ntiles):
            for og in range(nog):
                ps = banks[job % 2]
                v.wait_ge(p.sMM.h, base + job + 1)
                xs = dbuf[:, doff + ti * dstride + og * 512: doff + ti * dstride + og * 512 + 512]
                last = v.tensor_add(xs, xs, ps[:])
                p.sEV.inc(last)
                job += 1
        p.T.inc(v.drain())


def emit_rope_quarter(p, src, src_off, dst, dst_off, width=256):
    scr = p.ST[:, 112:368]
    nh = width // 64

    @p.section("vector")
    def _(v):
        last = None
        for ti in range(NT):
            q = src[:, src_off + ti * width: src_off + (ti + 1) * width]
            qh = q.rearrange("p (h d) -> p h d", d=64)
            q1 = qh[:, :, 0:32]
            q2 = qh[:, :, 32:64]
            o = dst[:, dst_off + ti * width: dst_off + (ti + 1) * width]
            oh = o.rearrange("p (h d) -> p h d", d=64)
            o1 = oh[:, :, 0:32]
            o2 = oh[:, :, 32:64]
            cos = p.CS[:, ti * 32:(ti + 1) * 32].unsqueeze(1).broadcast_to([128, nh, 32])
            sin = p.CS[:, 512 + ti * 32:512 + (ti + 1) * 32].unsqueeze(1).broadcast_to([128, nh, 32])
            t1 = scr[:, 0:nh * 32].rearrange("p (h d) -> p h d", d=32)
            t2 = scr[:, 128:128 + nh * 32].rearrange("p (h d) -> p h d", d=32)
            v.tensor_tensor(out=t1, in0=q1, in1=cos, op=OP.mult)
            v.tensor_tensor(out=t2, in0=q2, in1=sin, op=OP.mult)
            v.tensor_tensor(out=o1, in0=t1, in1=t2, op=OP.subtract)
            v.tensor_tensor(out=t1, in0=q1, in1=sin, op=OP.mult)
            v.tensor_tensor(out=t2, in0=q2, in1=cos, op=OP.mult)
            last = v.tensor_tensor(out=o2, in0=t1, in1=t2, op=OP.add)
        return last


def emit_dump(p, name, buf, off, cols, dtype=BF16):
    d = p.dout(name, [128, cols], dtype)

    @p.raw_section("gpsimd")
    def _(g):
        p.wait_all(g)
        i = g.dma_start(out=d[:], in_=buf[:, off:off + cols])
        p.T.inc(i, 16)

def emit_l0_attn(p):
    Wq = p.din("L0_attn_W_q", [D, D])
    Wdkv = p.din("L0_attn_W_dkv", [D, 256])
    Wuk = p.din("L0_attn_W_uk", [256, D])
    Wuv = p.din("L0_attn_W_uv", [256, D])
    Wo = p.din("L0_attn_W_o", [D, D])
    lng = p.din("L0_attn_ln_gamma", [256])
    lnb = p.din("L0_attn_ln_beta", [256])
    cosd = p.din("c_rope_cos", [T, 32])
    sind = p.din("c_rope_sin", [T, 32])
    causd = p.din("c_causal128", [128, 128], BF16)

    XN = 16384
    COS, SIN, CAUS = 3072, 3584, 4096
    GAM = 4352
    emit_ln(p, dst=p.A, dst_off=0)
    emit_transpose(p, dst=p.A, dst_off=XN, src=p.A, src_off=0)

    emit_weight_load(p, Wdkv, p.W, 0, D, 256)
    emit_proj_tm(p, dst=p.E, dst_off=0, src_fm=p.A, src_off=XN, w_sb=p.W, w_off=0,
                 n_in=D, n_out=256, wait_dw=p.dW.n, evac="vector", dst_stride=256)

    mean = p.ST[:, 64:80]
    rstd = p.ST[:, 80:96]
    sumsq = p.ST[:, 96:112]
    scr = p.ST[:, 112:368]

    @p.section("vector")
    def _(v):
        for ti in range(NT):
            st = p.E[:, ti * 256:(ti + 1) * 256]
            v.tensor_reduce(out=mean[:, ti:ti + 1], in_=st, axis=X, op=OP.add)
            v.tensor_mul(scr, st, st)
            v.tensor_reduce(out=sumsq[:, ti:ti + 1], in_=scr, axis=X, op=OP.add)
        v.tensor_scalar_mul(mean[:], mean[:], 1.0 / 256)
        v.tensor_scalar_mul(sumsq[:], sumsq[:], 1.0 / 256)
        v.drain()
        v.tensor_tensor(out=rstd[:], in0=mean[:], in1=mean[:], op=OP.mult)
        v.drain()
        v.tensor_sub(rstd[:], sumsq[:], rstd[:])
        v.drain()
        return v.tensor_scalar_add(rstd[:], rstd[:], 1e-5)

    @p.section("scalar")
    def _(s):
        return s.sqrt(rstd[:], rstd[:])

    @p.section("vector")
    def _(v):
        v.reciprocal(rstd[:], rstd[:])
        v.drain()
        last = None
        for ti in range(NT):
            last = v.tensor_scalar(out=p.B[:, ti * 256:(ti + 1) * 256],
                                   in0=p.E[:, ti * 256:(ti + 1) * 256],
                                   scalar1=mean[:, ti:ti + 1], scalar2=rstd[:, ti:ti + 1],
                                   op0=OP.subtract, op1=OP.mult)
        return last

    emit_transpose(p, dst=p.B, dst_off=4096, src=p.B, src_off=0, width=256)
    LATF = 4096

    gam = p.ST[:, 374:376]

    BET = 4224

    @p.raw_section("gpsimd")
    def _(g):
        g.wait_ge(p.T.h, p.T.n)
        i = g.dma_start(out=gam, in_=lng.rearrange("(c p) -> p c", p=128),
                        allow_slow_non_contiguous=True)
        p.dW.inc(i, 16)
        i = g.dma_start(out=p.W[:, BET:BET + 2], in_=lnb.rearrange("(c p) -> p c", p=128),
                        allow_slow_non_contiguous=True)
        p.dW.inc(i, 16)
        i = g.dma_start(out=p.CS[:, 0:512].rearrange("p (t f) -> p t f", f=32),
                        in_=cosd.rearrange("(t p) f -> p t f", p=128))
        p.dW.inc(i, 16)
        i = g.dma_start(out=p.CS[:, 512:1024].rearrange("p (t f) -> p t f", f=32),
                        in_=sind.rearrange("(t p) f -> p t f", p=128))
        p.dW.inc(i, 16)
        i = g.dma_start(out=p.W[:, CAUS:CAUS + 128], in_=causd[:])
        p.dW.inc(i, 16)

    bet = p.W[:, BET:BET + 2]

    for qq in range(4):
        hsl = qq * 256
        emit_weight_load(p, Wq, p.W, 0, D, 256, out_off=hsl)
        emit_weight_load(p, Wuk, p.W, 2048, 256, 256, out_off=hsl)
        emit_weight_load(p, Wuv, p.W, 2560, 256, 256, out_off=hsl)
        dw_after = p.dW.n

        @p.raw_section("tensor")
        def _(t, dw_after=dw_after):
            t.wait_ge(p.dW.h, dw_after)
            for j, woff in enumerate((2048, 2560)):
                last = None
                for c in range(2):
                    last = t.matmul(p.psC[0:1, j * 256:(j + 1) * 256],
                                    bet[:, c:c + 1],
                                    p.W[:, woff + c * 256:woff + (c + 1) * 256],
                                    start=(c == 0), stop=(c == 1))
                p.sMM.inc(last)

        @p.raw_section("vector")
        def _(v):
            v.wait_ge(p.sMM.h, p.sMM.n)
            v.tensor_copy(p.Rb[0:1, 0:512], p.psC[0:1, 0:512])
            for woff in (2048, 2560):
                for c in range(2):
                    v.tensor_scalar_mul(p.W[:, woff + c * 256:woff + (c + 1) * 256],
                                        p.W[:, woff + c * 256:woff + (c + 1) * 256],
                                        gam[:, c:c + 1])
            p.dinc(v, p.T)

        emit_proj_tm(p, dst=p.E, dst_off=0, src_fm=p.A, src_off=XN, w_sb=p.W, w_off=0,
                     n_in=D, n_out=256, scale=0.125, wait_dw=dw_after,
                     dst_stride=256, evac="vector")
        emit_rope_quarter(p, src=p.E, src_off=0, dst=p.A, dst_off=0)
        emit_transpose(p, dst=p.A, dst_off=4096, src=p.A, src_off=0, width=256)
        emit_proj_tm(p, dst=p.E, dst_off=0, src_fm=p.B, src_off=LATF, w_sb=p.W,
                     w_off=2048, n_in=256, n_out=256, bias_off=0, dst_stride=256,
                     evac="vector")
        emit_rope_quarter(p, src=p.E, src_off=0, dst=p.A, dst_off=8192)
        emit_transpose(p, dst=p.A, dst_off=12288, src=p.A, src_off=8192, width=256)
        emit_proj_tm(p, dst=p.B, dst_off=8192, src_fm=p.B, src_off=LATF, w_sb=p.W,
                     w_off=2560, n_in=256, n_out=256, bias_off=256, dst_stride=256)
        emit_causal_attn_quarter(p, qfm=(p.A, 4096), kfm=(p.A, 12288),
                                 vtm=(p.B, 8192), dst=(p.B, 0),
                                 caus=(p.W, CAUS))
        emit_transpose(p, dst=p.B, dst_off=12288, src=p.B, src_off=0, width=256)
        emit_weight_load(p, Wo, p.W, 4608, 256, D, in_off=256 * qq)
        emit_proj_residual(p, src_fm=(p.B, 12288), w=(p.W, 4608), n_in=256, n_out=D,
                           wait_dw=p.dW.n)


def emit_causal_attn_quarter(p, qfm, kfm, vtm, dst, caus):
    qbuf, qoff = qfm
    kbuf, koff = kfm
    vbuf, voff = vtm
    dbuf, doff = dst
    cbuf, coff = caus
    P_off = 12288
    PT_off = 14336
    nmax = p.ST[:, 368:369]
    sums = p.ST[:, 369:370]
    rinv = p.ST[:, 373:374]

    iters = []
    for h in range(4):
        for qi in range(NT):
            nk = (qi + 1) * 128
            iters.append((h, qi, nk, (nk + 511) // 512, (qi + 8) // 8))
    NIT = len(iters)
    t_inc = [1 + ngr + 1 for (_, _, _, _, ngr) in iters]
    v_inc = [1 + 1 + ngr + 1 for (_, _, _, _, ngr) in iters]
    t_pre = [0]
    v_pre = [0]
    for a, b in zip(t_inc, v_inc):
        t_pre.append(t_pre[-1] + a)
        v_pre.append(v_pre[-1] + b)
    mm0 = p.sMM.n
    ev0 = p.sEV.n
    ac0 = p.sAC.n

    @p.raw_section("tensor")
    def _(t):
        p.wait_all(t)
        for k, (h, qi, nk, nb, ngr) in enumerate(iters):
            hd = h * 64
            ch = hd // 128
            hin = hd % 128
            if k > 0:
                t.wait_ge(p.sAC.h, ac0 + k)
            for b in range(nb):
                kw = min(512, nk - b * 512)
                t.matmul(p.psS[:, b * 512:b * 512 + kw],
                         qbuf[hin:hin + 64, qoff + ch * 2048 + qi * 128:
                              qoff + ch * 2048 + qi * 128 + 128],
                         kbuf[hin:hin + 64, koff + ch * 2048 + b * 512:
                              koff + ch * 2048 + b * 512 + kw],
                         start=True, stop=(b != nb - 1))
            i = t.matmul(p.psS[:, nk - 128:nk], p.ident[:],
                         cbuf[:, coff:coff + 128], start=False, stop=True)
            i.then_inc(p.sMM.h, 1)
            t.wait_ge(p.sAC.h, ac0 + k + 1)
            if k > 0:
                t.wait_ge(p.sEV.h, ev0 + v_pre[k] - 1)
            nkb = qi + 1
            for gi, g0 in enumerate(range(0, nkb, 8)):
                gl = min(8, nkb - g0)
                if gi > 0:
                    t.wait_ge(p.sEV.h, ev0 + v_pre[k] + 2 + gi)
                for kk in range(gl):
                    i = t.transpose(p.psT[:, kk * 128:(kk + 1) * 128],
                                    p.B[:, P_off + (g0 + kk) * 128:
                                        P_off + (g0 + kk) * 128 + 128],
                                    p.ident[:])
                i.then_inc(p.sMM.h, 1)
            t.wait_ge(p.sEV.h, ev0 + v_pre[k] + 2 + ngr)
            for kb in range(nkb):
                i = t.matmul(p.psA[:, 0:64],
                             p.B[:, PT_off + kb * 128:PT_off + kb * 128 + 128],
                             vbuf[:, voff + kb * 256 + hd:voff + kb * 256 + hd + 64],
                             start=(kb == 0), stop=(kb == nkb - 1))
            i.then_inc(p.sMM.h, 1)
    p.sMM.n += t_pre[NIT]

    @p.raw_section("vector")
    def _(v):
        for k, (h, qi, nk, nb, ngr) in enumerate(iters):
            hd = h * 64
            v.wait_ge(p.sMM.h, mm0 + t_pre[k] + 1)
            v.reduce_max(out=nmax, in_=p.psS[:, 0:nk], axis=X, negate=True)
            i = v.drain()
            i.then_inc(p.sEV.h, 1)
            v.wait_ge(p.sAC.h, ac0 + k + 1)
            i = v.reciprocal(rinv, sums)
            i.then_inc(p.sEV.h, 1)
            nkb = qi + 1
            for gi, g0 in enumerate(range(0, nkb, 8)):
                gl = min(8, nkb - g0)
                v.wait_ge(p.sMM.h, mm0 + t_pre[k] + 1 + gi + 1)
                i = v.tensor_copy(p.B[:, PT_off + g0 * 128:PT_off + g0 * 128 + gl * 128],
                                  p.psT[:, 0:gl * 128])
                if gi == ngr - 1:
                    i = v.drain()
                i.then_inc(p.sEV.h, 1)
            v.wait_ge(p.sMM.h, mm0 + t_pre[k + 1])
            i = v.tensor_scalar_mul(dbuf[:, doff + qi * 256 + hd:doff + qi * 256 + hd + 64],
                                    p.psA[:, 0:64], rinv)
            i.then_inc(p.sEV.h, 1)
        i = v.drain()
        p.T.inc(i)
    p.sEV.n += v_pre[NIT]

    @p.raw_section("scalar")
    def _(s):
        for k, (h, qi, nk, nb, ngr) in enumerate(iters):
            s.wait_ge(p.sEV.h, ev0 + v_pre[k] + 1)
            if k > 0:
                s.wait_ge(p.sMM.h, mm0 + t_pre[k] - 1)
            s.activation(out=p.B[:, P_off:P_off + nk], in_=p.psS[:, 0:nk],
                         func=AF.Exp, bias=nmax, accum_out=sums)
            i = s.drain()
            i.then_inc(p.sAC.h, 1)
    p.sAC.n += NIT


def emit_proj_residual(p, src_fm, w, n_in, n_out, wait_dw, bias_off=None, ntiles=NT,
                       dst=None):
    sbuf, soff = src_fm
    wbuf, woff = w
    if dst is None:
        dbuf, doff, dstride = p.xres, 0, D
    else:
        dbuf, doff, dstride = dst
    nkc = n_in // 128
    banks = [p.psA, p.psB]
    nog = n_out // 512
    njobs = ntiles * nog

    @p.raw_section("tensor")
    def _(t):
        p.wait_all(t)
        t.wait_ge(p.dW.h, wait_dw)
        ev0 = p.sEV.n
        job = 0
        for ti in range(ntiles):
            for og in range(nog):
                ps = banks[job % 2]
                if job >= 2:
                    t.wait_ge(p.sEV.h, ev0 + job - 1)
                last = None
                for kc in range(nkc):
                    last = t.matmul(ps[:], sbuf[:, soff + kc * (ntiles * 128) + ti * 128:
                                                soff + kc * (ntiles * 128) + ti * 128 + 128],
                                    wbuf[:, woff + kc * n_out + og * 512:
                                         woff + kc * n_out + og * 512 + 512],
                                    start=(kc == 0),
                                    stop=(kc == nkc - 1 and bias_off is None))
                if bias_off is not None:
                    last = t.matmul(ps[:], p.onesb[:],
                                    p.Rb[0:1, bias_off + og * 512:bias_off + og * 512 + 512],
                                    start=False, stop=True)
                p.sMM.inc(last)
                job += 1

    @p.raw_section("vector")
    def _(v):
        base = p.sMM.n - njobs
        job = 0
        last = None
        for ti in range(ntiles):
            for og in range(nog):
                ps = banks[job % 2]
                v.wait_ge(p.sMM.h, base + job + 1)
                xs = dbuf[:, doff + ti * dstride + og * 512: doff + ti * dstride + og * 512 + 512]
                last = v.tensor_add(xs, xs, ps[:])
                p.sEV.inc(last)
                job += 1
        p.T.inc(v.drain())


def emit_rope_quarter(p, src, src_off, dst, dst_off, width=256):
    scr = p.ST[:, 112:368]
    nh = width // 64

    @p.section("vector")
    def _(v):
        last = None
        for ti in range(NT):
            q = src[:, src_off + ti * width: src_off + (ti + 1) * width]
            qh = q.rearrange("p (h d) -> p h d", d=64)
            q1 = qh[:, :, 0:32]
            q2 = qh[:, :, 32:64]
            o = dst[:, dst_off + ti * width: dst_off + (ti + 1) * width]
            oh = o.rearrange("p (h d) -> p h d", d=64)
            o1 = oh[:, :, 0:32]
            o2 = oh[:, :, 32:64]
            cos = p.CS[:, ti * 32:(ti + 1) * 32].unsqueeze(1).broadcast_to([128, nh, 32])
            sin = p.CS[:, 512 + ti * 32:512 + (ti + 1) * 32].unsqueeze(1).broadcast_to([128, nh, 32])
            t1 = scr[:, 0:nh * 32].rearrange("p (h d) -> p h d", d=32)
            t2 = scr[:, 128:128 + nh * 32].rearrange("p (h d) -> p h d", d=32)
            v.tensor_tensor(out=t1, in0=q1, in1=cos, op=OP.mult)
            v.tensor_tensor(out=t2, in0=q2, in1=sin, op=OP.mult)
            v.tensor_tensor(out=o1, in0=t1, in1=t2, op=OP.subtract)
            v.tensor_tensor(out=t1, in0=q1, in1=sin, op=OP.mult)
            v.tensor_tensor(out=t2, in0=q2, in1=cos, op=OP.mult)
            last = v.tensor_tensor(out=o2, in0=t1, in1=t2, op=OP.add)
        return last


def emit_dump(p, name, buf, off, cols, dtype=BF16):
    d = p.dout(name, [128, cols], dtype)

    @p.raw_section("gpsimd")
    def _(g):
        p.wait_all(g)
        i = g.dma_start(out=d[:], in_=buf[:, off:off + cols])
        p.T.inc(i, 16)

def emit_l0_attn(p):
    Wq = p.din("L0_attn_W_q", [D, D])
    Wdkv = p.din("L0_attn_W_dkv", [D, 256])
    Wuk = p.din("L0_attn_W_uk", [256, D])
    Wuv = p.din("L0_attn_W_uv", [256, D])
    Wo = p.din("L0_attn_W_o", [D, D])
    lng = p.din("L0_attn_ln_gamma", [256])
    lnb = p.din("L0_attn_ln_beta", [256])
    cosd = p.din("c_rope_cos", [T, 32])
    sind = p.din("c_rope_sin", [T, 32])
    causd = p.din("c_causal128", [128, 128], BF16)

    XN = 16384
    COS, SIN, CAUS = 3072, 3584, 4096
    GAM = 4352
    emit_ln(p, dst=p.A, dst_off=0)
    emit_transpose(p, dst=p.A, dst_off=XN, src=p.A, src_off=0)

    emit_weight_load(p, Wdkv, p.W, 0, D, 256)
    emit_proj_tm(p, dst=p.E, dst_off=0, src_fm=p.A, src_off=XN, w_sb=p.W, w_off=0,
                 n_in=D, n_out=256, wait_dw=p.dW.n, evac="vector", dst_stride=256)

    mean = p.ST[:, 64:80]
    rstd = p.ST[:, 80:96]
    sumsq = p.ST[:, 96:112]
    scr = p.ST[:, 112:368]

    @p.section("vector")
    def _(v):
        for ti in range(NT):
            st = p.E[:, ti * 256:(ti + 1) * 256]
            v.tensor_reduce(out=mean[:, ti:ti + 1], in_=st, axis=X, op=OP.add)
            v.tensor_mul(scr, st, st)
            v.tensor_reduce(out=sumsq[:, ti:ti + 1], in_=scr, axis=X, op=OP.add)
        v.tensor_scalar_mul(mean[:], mean[:], 1.0 / 256)
        v.tensor_scalar_mul(sumsq[:], sumsq[:], 1.0 / 256)
        v.drain()
        v.tensor_tensor(out=rstd[:], in0=mean[:], in1=mean[:], op=OP.mult)
        v.drain()
        v.tensor_sub(rstd[:], sumsq[:], rstd[:])
        v.drain()
        return v.tensor_scalar_add(rstd[:], rstd[:], 1e-5)

    @p.section("scalar")
    def _(s):
        return s.sqrt(rstd[:], rstd[:])

    @p.section("vector")
    def _(v):
        v.reciprocal(rstd[:], rstd[:])
        v.drain()
        last = None
        for ti in range(NT):
            last = v.tensor_scalar(out=p.B[:, ti * 256:(ti + 1) * 256],
                                   in0=p.E[:, ti * 256:(ti + 1) * 256],
                                   scalar1=mean[:, ti:ti + 1], scalar2=rstd[:, ti:ti + 1],
                                   op0=OP.subtract, op1=OP.mult)
        return last

    emit_transpose(p, dst=p.B, dst_off=4096, src=p.B, src_off=0, width=256)
    LATF = 4096

    gam = p.ST[:, 374:376]

    BET = 4224

    @p.raw_section("gpsimd")
    def _(g):
        g.wait_ge(p.T.h, p.T.n)
        i = g.dma_start(out=gam, in_=lng.rearrange("(c p) -> p c", p=128),
                        allow_slow_non_contiguous=True)
        p.dW.inc(i, 16)
        i = g.dma_start(out=p.W[:, BET:BET + 2], in_=lnb.rearrange("(c p) -> p c", p=128),
                        allow_slow_non_contiguous=True)
        p.dW.inc(i, 16)
        i = g.dma_start(out=p.CS[:, 0:512].rearrange("p (t f) -> p t f", f=32),
                        in_=cosd.rearrange("(t p) f -> p t f", p=128))
        p.dW.inc(i, 16)
        i = g.dma_start(out=p.CS[:, 512:1024].rearrange("p (t f) -> p t f", f=32),
                        in_=sind.rearrange("(t p) f -> p t f", p=128))
        p.dW.inc(i, 16)
        i = g.dma_start(out=p.W[:, CAUS:CAUS + 128], in_=causd[:])
        p.dW.inc(i, 16)

    bet = p.W[:, BET:BET + 2]

    for qq in range(4):
        hsl = qq * 256
        emit_weight_load(p, Wq, p.W, 0, D, 256, out_off=hsl)
        emit_weight_load(p, Wuk, p.W, 2048, 256, 256, out_off=hsl)
        emit_weight_load(p, Wuv, p.W, 2560, 256, 256, out_off=hsl)
        dw_after = p.dW.n

        @p.raw_section("tensor")
        def _(t, dw_after=dw_after):
            t.wait_ge(p.dW.h, dw_after)
            for j, woff in enumerate((2048, 2560)):
                last = None
                for c in range(2):
                    last = t.matmul(p.psC[0:1, j * 256:(j + 1) * 256],
                                    bet[:, c:c + 1],
                                    p.W[:, woff + c * 256:woff + (c + 1) * 256],
                                    start=(c == 0), stop=(c == 1))
                p.sMM.inc(last)

        @p.raw_section("vector")
        def _(v):
            v.wait_ge(p.sMM.h, p.sMM.n)
            v.tensor_copy(p.Rb[0:1, 0:512], p.psC[0:1, 0:512])
            for woff in (2048, 2560):
                for c in range(2):
                    v.tensor_scalar_mul(p.W[:, woff + c * 256:woff + (c + 1) * 256],
                                        p.W[:, woff + c * 256:woff + (c + 1) * 256],
                                        gam[:, c:c + 1])
            p.dinc(v, p.T)

        emit_proj_tm(p, dst=p.E, dst_off=0, src_fm=p.A, src_off=XN, w_sb=p.W, w_off=0,
                     n_in=D, n_out=256, scale=0.125, wait_dw=dw_after,
                     dst_stride=256, evac="vector")
        emit_rope_quarter(p, src=p.E, src_off=0, dst=p.A, dst_off=0)
        emit_transpose(p, dst=p.A, dst_off=4096, src=p.A, src_off=0, width=256)
        emit_proj_tm(p, dst=p.E, dst_off=0, src_fm=p.B, src_off=LATF, w_sb=p.W,
                     w_off=2048, n_in=256, n_out=256, bias_off=0, dst_stride=256,
                     evac="vector")
        emit_rope_quarter(p, src=p.E, src_off=0, dst=p.A, dst_off=8192)
        emit_transpose(p, dst=p.A, dst_off=12288, src=p.A, src_off=8192, width=256)
        emit_proj_tm(p, dst=p.B, dst_off=8192, src_fm=p.B, src_off=LATF, w_sb=p.W,
                     w_off=2560, n_in=256, n_out=256, bias_off=256, dst_stride=256)
        emit_causal_attn_quarter(p, qfm=(p.A, 4096), kfm=(p.A, 12288),
                                 vtm=(p.B, 8192), dst=(p.B, 0),
                                 caus=(p.W, CAUS))
        emit_transpose(p, dst=p.B, dst_off=12288, src=p.B, src_off=0, width=256)
        emit_weight_load(p, Wo, p.W, 4608, 256, D, in_off=256 * qq)
        emit_proj_residual(p, src_fm=(p.B, 12288), w=(p.W, 4608), n_in=256, n_out=D,
                           wait_dw=p.dW.n)


def emit_causal_attn_quarter(p, qfm, kfm, vtm, dst, caus):
    qbuf, qoff = qfm
    kbuf, koff = kfm
    vbuf, voff = vtm
    dbuf, doff = dst
    cbuf, coff = caus
    P_off = 12288
    PT_off = 14336
    nmax = p.ST[:, 368:369]
    sums = p.ST[:, 369:373]
    rinv = p.ST[:, 373:374]

    for h in range(4):
        hd = h * 64
        ch = hd // 128
        hin = hd % 128
        for qi in range(NT):
            nk = (qi + 1) * 128
            nb = (nk + 511) // 512

            @p.raw_section("tensor")
            def _(t, qi=qi, nk=nk, nb=nb, ch=ch, hin=hin):
                p.wait_all(t)
                for b in range(nb):
                    kw = min(512, nk - b * 512)
                    t.matmul(p.psS[:, b * 512:b * 512 + kw],
                             qbuf[hin:hin + 64, qoff + ch * 2048 + qi * 128:
                                  qoff + ch * 2048 + qi * 128 + 128],
                             kbuf[hin:hin + 64, koff + ch * 2048 + b * 512:
                                  koff + ch * 2048 + b * 512 + kw],
                             start=True, stop=(b != nb - 1))
                last = t.matmul(p.psS[:, nk - 128:nk], p.ident[:],
                                cbuf[:, coff:coff + 128], start=False, stop=True)
                p.sMM.inc(last)

            @p.raw_section("vector")
            def _(v, nk=nk):
                v.wait_ge(p.sMM.h, p.sMM.n)
                v.reduce_max(out=nmax, in_=p.psS[:, 0:nk], axis=X, negate=True)
                p.dinc(v, p.sEV)

            @p.raw_section("scalar")
            def _(s, nk=nk, nb=nb):
                s.wait_ge(p.sEV.h, p.sEV.n)
                last = None
                for b in range(nb):
                    kw = min(512, nk - b * 512)
                    last = s.activation(out=p.B[:, P_off + b * 512:P_off + b * 512 + kw],
                                        in_=p.psS[:, b * 512:b * 512 + kw], func=AF.Exp,
                                        bias=nmax, accum_out=sums[:, b:b + 1])
                p.dinc(s, p.sAC)

            @p.raw_section("vector")
            def _(v, nb=nb):
                v.wait_ge(p.sAC.h, p.sAC.n)
                v.tensor_reduce(out=rinv, in_=sums[:, 0:nb], axis=X, op=OP.add)
                v.drain()
                v.reciprocal(rinv, rinv)
                p.dinc(v, p.sEV)

            nkb = qi + 1
            ngr = (nkb + 7) // 8

            @p.raw_section("tensor")
            def _(t, nkb=nkb, ngr=ngr):
                t.wait_ge(p.sAC.h, p.sAC.n)
                ev0 = p.sEV.n
                for gi, g0 in enumerate(range(0, nkb, 8)):
                    gl = min(8, nkb - g0)
                    if gi > 0:
                        t.wait_ge(p.sEV.h, ev0 + gi)
                    last = None
                    for k in range(gl):
                        last = t.transpose(p.psT[:, k * 128:(k + 1) * 128],
                                           p.B[:, P_off + (g0 + k) * 128:
                                               P_off + (g0 + k) * 128 + 128],
                                           p.ident[:])
                    p.sMM.inc(last)

            @p.raw_section("vector")
            def _(v, nkb=nkb, ngr=ngr):
                base = p.sMM.n - ngr
                for gi, g0 in enumerate(range(0, nkb, 8)):
                    gl = min(8, nkb - g0)
                    v.wait_ge(p.sMM.h, base + gi + 1)
                    v.tensor_copy(p.B[:, PT_off + g0 * 128:PT_off + g0 * 128 + gl * 128],
                                  p.psT[:, 0:gl * 128])
                    p.dinc(v, p.sEV)

            @p.raw_section("tensor")
            def _(t, nkb=nkb, hd=hd):
                t.wait_ge(p.sEV.h, p.sEV.n)
                last = None
                for kb in range(nkb):
                    last = t.matmul(p.psA[:, 0:64],
                                    p.B[:, PT_off + kb * 128:PT_off + kb * 128 + 128],
                                    vbuf[:, voff + kb * 256 + hd:voff + kb * 256 + hd + 64],
                                    start=(kb == 0), stop=(kb == nkb - 1))
                p.sMM.inc(last)

            @p.raw_section("vector")
            def _(v, qi=qi, hd=hd):
                v.wait_ge(p.sMM.h, p.sMM.n)
                v.tensor_scalar_mul(dbuf[:, doff + qi * 256 + hd:doff + qi * 256 + hd + 64],
                                    p.psA[:, 0:64], rinv)
                v.drain()
                p.dinc(v, p.sEV)
                p.T.inc(v.drain())


def emit_l0_ffn(p):
    rT = p.din("L0_ffn_router_wT", [D, 8])
    w1 = p.din("L0_ffn_w1", [8, D, 4096])
    b1 = p.din("L0_ffn_b1", [8, 4096])
    w2 = p.din("L0_ffn_w2", [8, 4096, D])
    b2 = p.din("L0_ffn_b2", [8, D])
    idf = p.din("c_ident_f32", [128, 128], F32)

    XN = 16384
    CAP = 513
    emit_ln(p, dst=p.A, dst_off=0)
    emit_transpose(p, dst=p.A, dst_off=XN, src=p.A, src_off=0)

    emit_weight_load(p, rT, p.W, 0, D, 8)
    @p.raw_section("gpsimd")
    def _(g):
        g.wait_ge(p.T.h, p.T.n)
        i = g.dma_start(out=p.CS[:, 0:128], in_=idf[:])
        p.dW.inc(i, 16)
    IDF = 0

    emit_proj_tm(p, dst=p.E, dst_off=0, src_fm=p.A, src_off=XN, w_sb=p.W, w_off=0,
                 n_in=D, n_out=8, wait_dw=p.dW.n, evac="vector", dst_stride=8)

    @p.section("vector")
    def _(v):
        lg = p.E[:, 0:128].rearrange("p (t e) -> p t e", e=8)
        m = p.ST[:, 128:144]
        v.tensor_reduce(out=m, in_=lg, axis=X, op=OP.max)
        v.drain()
        v.tensor_tensor(out=lg, in0=lg, in1=m.unsqueeze(2).broadcast_to([128, 16, 8]),
                        op=OP.subtract)
        return v.drain()

    @p.section("scalar")
    def _(s):
        return s.activation(out=p.E[:, 0:128], in_=p.E[:, 0:128], func=AF.Exp)

    @p.section("vector")
    def _(v):
        lg = p.E[:, 0:128].rearrange("p (t e) -> p t e", e=8)
        sm = p.ST[:, 144:160]
        v.tensor_reduce(out=sm, in_=lg, axis=X, op=OP.add)
        v.drain()
        v.reciprocal(sm, sm)
        v.drain()
        v.tensor_tensor(out=lg, in0=lg, in1=sm.unsqueeze(2).broadcast_to([128, 16, 8]),
                        op=OP.mult)
        return v.drain()

    @p.raw_section("tensor")
    def _(t):
        p.wait_all(t)
        ev0 = p.sEV.n
        for ti in range(NT):
            if ti >= 1:
                t.wait_ge(p.sEV.h, ev0 + ti)
            i = t.transpose(p.psC[0:8, 0:128],
                            p.E[:, ti * 8:(ti + 1) * 8], p.CS[:, IDF:IDF + 128])
            p.sMM.inc(i)

    @p.raw_section("vector")
    def _(v):
        base = p.sMM.n - NT
        for ti in range(NT):
            v.wait_ge(p.sMM.h, base + ti + 1)
            v.tensor_copy(p.E[0:8, 2048 + ti * 128:2048 + (ti + 1) * 128],
                          p.psC[0:8, 0:128])
            p.dinc(v, p.sEV)
        p.T.inc(v.drain())

    SCF = 2048

    a = p.ST[0:8, 160:161]
    bb = p.ST[0:8, 161:162]
    mid = p.ST[0:8, 162:163]
    cnt = p.ST[0:8, 163:164]
    sel = p.ST[0:8, 164:165]
    d1 = p.ST[0:8, 165:166]
    d2 = p.ST[0:8, 166:167]
    cmpb = p.B[0:8, 0:2048]

    @p.section("vector")
    def _(v):
        v.memset(a, 0.0)
        v.memset(bb, 1.0)
        v.drain()
        for it in range(40):
            v.tensor_tensor(out=mid, in0=a, in1=bb, op=OP.add)
            v.drain()
            v.tensor_scalar_mul(mid, mid, 0.5)
            v.drain()
            v.tensor_single_scalar(out=cmpb, in_=p.E[0:8, SCF:SCF + 2048], scalar=mid,
                                   op=OP.is_ge)
            v.tensor_reduce(out=cnt, in_=cmpb, axis=X, op=OP.add)
            v.drain()
            v.tensor_single_scalar(out=sel, in_=cnt, scalar=float(CAP) - 0.5,
                                   op=OP.is_ge)
            v.drain()
            v.tensor_sub(d1, mid, a)
            v.tensor_sub(d2, bb, mid)
            v.drain()
            v.scalar_tensor_tensor(out=a, in0=d1, scalar=sel, in1=a,
                                   op0=OP.mult, op1=OP.add)
            v.scalar_tensor_tensor(out=bb, in0=d2, scalar=sel, in1=mid,
                                   op0=OP.mult, op1=OP.add)
            v.drain()
        v.tensor_single_scalar(out=cmpb, in_=p.E[0:8, SCF:SCF + 2048], scalar=a,
                               op=OP.is_ge)
        v.drain()
        v.tensor_tensor(out=p.E[0:8, SCF:SCF + 2048], in0=p.E[0:8, SCF:SCF + 2048],
                        in1=cmpb, op=OP.mult)
        return v.drain()

    @p.raw_section("tensor")
    def _(t):
        p.wait_all(t)
        ev0 = p.sEV.n
        for ti in range(NT):
            if ti >= 1:
                t.wait_ge(p.sEV.h, ev0 + ti)
            i = t.transpose(p.psC[:, 0:8],
                            p.E[0:8, SCF + ti * 128:SCF + (ti + 1) * 128],
                            p.CS[0:8, IDF:IDF + 8])
            p.sMM.inc(i)

    @p.raw_section("vector")
    def _(v):
        base = p.sMM.n - NT
        for ti in range(NT):
            v.wait_ge(p.sMM.h, base + ti + 1)
            v.tensor_copy(p.ST[:, 256 + ti * 8:256 + (ti + 1) * 8], p.psC[:, 0:8])
            p.dinc(v, p.sEV)
        p.T.inc(v.drain())

    CW = 256

    for e in range(8):
        emit_moe_expert(p, e, w1, b1, w2, b2, xn_off=XN, cw_col=lambda ti, e=e: CW + ti * 8 + e,
                        act=AF.Gelu, add_xf=False)


def emit_moe_expert(p, e, w1, b1, w2, b2, xn_off, cw_col, act, add_xf,
                    h_scale=1.0, xn_buf=None, h_off=0, h_buf=None,
                    out_dst=None, premul=None, premul_dst=None):
    xn_buf = xn_buf if xn_buf is not None else p.A
    h_buf = h_buf if h_buf is not None else p.B
    w1e = w1[e] if e is not None else w1
    w2e = w2[e] if e is not None else w2
    b1e = b1[e] if e is not None else b1
    b2e = b2[e] if e is not None else b2

    @p.raw_section("gpsimd")
    def _(g):
        g.wait_ge(p.T.h, p.T.n)
        i = g.dma_start(out=p.CS[:, 128:160], in_=b1e.rearrange("(c p) -> p c", p=128),
                        allow_slow_non_contiguous=True)
        p.dW.inc(i, 16)
        i = g.dma_start(out=p.Rb[0:1, 0:1024], in_=b2e.unsqueeze(0))
        p.dW.inc(i, 16)
    bias_dw = p.dW.n

    if premul is not None:
        gcol, bcol = premul
        nbuf, noff = premul_dst

        @p.section("vector")
        def _(v):
            last = None
            for dc in range(8):
                last = v.tensor_scalar(out=nbuf[:, noff + dc * 2048:noff + (dc + 1) * 2048],
                                       in0=xn_buf[:, xn_off + dc * (NT * 128):
                                                  xn_off + (dc + 1) * (NT * 128)],
                                       scalar1=gcol(dc), scalar2=bcol(dc),
                                       op0=OP.mult, op1=OP.add)
            return last
        rhs_buf, rhs_off = nbuf, noff
    else:
        rhs_buf, rhs_off = xn_buf, xn_off

    w1_banks = [p.psS[:, 0:512], p.psS[:, 512:1024], p.psS[:, 1024:1536],
                p.psS[:, 1536:2048]]
    w2_pairs = [(p.psA[:], p.psB[:]), (p.psC[:], p.psS[:, 0:512])]

    for g in range(8):
        @p.raw_section("gpsimd")
        def _(gg, g=g):
            if g == 0:
                gg.wait_ge(p.T.h, p.T.n)
            else:
                gg.wait_ge(p.sAC.h, p.sAC.n)
            i = gg.dma_start(out=p.W[:, 0:4096].rearrange("p (c m) -> p c m", m=512),
                             in_=w1e[:, g * 512:(g + 1) * 512]
                             .rearrange("(c p) m -> p c m", p=128))
            p.dW.inc(i, 16)
        w1_dw = p.dW.n

        @p.raw_section("gpsimd")
        def _(gg, g=g):
            if g > 0:
                gg.wait_ge(p.sEV.h, p.sEV.n)
            i = gg.dma_start(out=p.W[:, 4096:8192]
                             .rearrange("p (c m) -> p c m", m=1024),
                             in_=w2e[g * 512:(g + 1) * 512, :]
                             .rearrange("(c p) m -> p c m", p=128))
            p.dW.inc(i, 16)
        w2_dw = p.dW.n

        ac0 = p.sAC.n

        @p.raw_section("tensor")
        def _(t, g=g, ac0=ac0):
            t.wait_ge(p.dW.h, w1_dw)
            if g == 0:
                p.wait_all(t)
            t.wait_ge(p.sEV.h, p.sEV.n)
            for hc in range(4):
                for dc in range(8):
                    for tg in range(4):
                        if dc == 0 and hc >= 1:
                            t.wait_ge(p.sAC.h, ac0 + 4 * (hc - 1) + tg + 1)
                        last = t.matmul(
                            w1_banks[tg],
                            p.W[:, dc * 512 + hc * 128:dc * 512 + hc * 128 + 128],
                            rhs_buf[:, rhs_off + dc * (NT * 128) + tg * 512:
                                    rhs_off + dc * (NT * 128) + tg * 512 + 512],
                            start=(dc == 0), stop=(dc == 7))
                        if dc == 7:
                            p.sMM.inc(last)

        @p.raw_section("scalar")
        def _(s, g=g):
            base = p.sMM.n - 16
            if g == 0:
                s.wait_ge(p.dW.h, bias_dw)
            job = 0
            for hc in range(4):
                for tg in range(4):
                    s.wait_ge(p.sMM.h, base + job + 1)
                    s.activation(out=h_buf[:, h_off + hc * 2048 + tg * 512:
                                           h_off + hc * 2048 + tg * 512 + 512],
                                 in_=w1_banks[tg], func=act,
                                 bias=p.CS[:, 128 + g * 4 + hc:129 + g * 4 + hc],
                                 scale=h_scale)
                    p.dinc(s, p.sAC)
                    job += 1

        ev0 = p.sEV.n
        ac_all = p.sAC.n

        @p.raw_section("tensor")
        def _(t, g=g, ev0=ev0, ac_all=ac_all):
            t.wait_ge(p.dW.h, w2_dw)
            t.wait_ge(p.sAC.h, ac_all)
            for ti in range(16):
                pa, pb = w2_pairs[ti % 2]
                if ti >= 2:
                    t.wait_ge(p.sEV.h, ev0 + 2 * (ti - 1))
                last = None
                for hc in range(4):
                    for oh in range(2):
                        ps = pa if oh == 0 else pb
                        last = t.matmul(ps,
                                        h_buf[:, h_off + hc * 2048 + ti * 128:
                                              h_off + hc * 2048 + ti * 128 + 128],
                                        p.W[:, 4096 + hc * 1024 + oh * 512:
                                            4096 + hc * 1024 + oh * 512 + 512],
                                        start=(hc == 0), stop=(hc == 3 and g != 7))
                if g == 7:
                    for oh in range(2):
                        ps = pa if oh == 0 else pb
                        last = t.matmul(ps, p.onesb[:],
                                        p.Rb[0:1, oh * 512:oh * 512 + 512],
                                        start=False, stop=True)
                p.sMM.inc(last)

        @p.raw_section("vector")
        def _(v, g=g):
            base = p.sMM.n - 16
            for ti in range(16):
                pa, pb = w2_pairs[ti % 2]
                v.wait_ge(p.sMM.h, base + ti + 1)
                for oh in range(2):
                    ps = pa if oh == 0 else pb
                    psap = ps
                    if out_dst is None:
                        xs = p.xres[:, ti * D + oh * 512: ti * D + oh * 512 + 512]
                    else:
                        obuf, ooff = out_dst
                        xs = obuf[:, ooff + ti * D + oh * 512:ooff + ti * D + oh * 512 + 512]
                    if cw_col is None:
                        v.tensor_add(xs, xs, psap)
                    else:
                        v.scalar_tensor_tensor(out=xs, in0=psap,
                                               scalar=p.ST[:, cw_col(ti):cw_col(ti) + 1],
                                               in1=xs, op0=OP.mult, op1=OP.add)
                p.dinc(v, p.sEV, 2)
            if g == 7:
                p.T.inc(v.drain())


def emit_l1_attn(p, core_id_dram_names=True):
    WT = p.din("L1_attn_in_proj_weightT", [D, 3 * D])
    bias = p.din("L1_attn_in_proj_bias", [3 * D])
    rho = p.din("rho_c", [128, 8])
    outWT = p.din("L1_attn_out_wT", [D, D])
    outb = p.din("L1_attn_out_b", [D])

    kv_in = p.dint("l1_kv_in", [2 * T, D], BF16)
    kv_out = p.dint("l1_kv_out", [NCORES * 2 * T, D], BF16)

    XN = 16384
    emit_ln(p, dst=p.A, dst_off=0)
    emit_transpose(p, dst=p.A, dst_off=XN, src=p.A, src_off=0)

    for j, (wo, boff) in enumerate(((D, D), (2 * D, 2 * D))):
        emit_weight_load(p, WT, p.W, 0, D, D, out_off=wo)
        emit_row_load(p, bias, 0, D, src_off=boff)
        emit_proj_tm(p, dst=p.B, dst_off=0, src_fm=p.A, src_off=XN, w_sb=p.W, w_off=0,
                     n_in=D, n_out=D, bias_off=0, wait_dw=p.dW.n)

        @p.section("gpsimd")
        def _(g, j=j):
            return g.dma_start(out=kv_in[j * T:(j + 1) * T, :].rearrange("(t p) d -> p t d", p=128),
                               in_=p.B[:].rearrange("p (t d) -> p t d", d=D))

    @p.raw_section("gpsimd")
    def _(g):
        p.wait_all(g)
        g.collective_compute("AllGather", OP.bypass, replica_groups=[list(range(NCORES))],
                             ins=[kv_in[:]], outs=[kv_out[:]]).then_inc(p.sCC.h)
        p.sCC.n += 1

    emit_weight_load(p, WT, p.W, 0, D, D, out_off=0)
    emit_row_load(p, bias, 0, D, src_off=0)
    emit_proj_tm(p, dst=p.B, dst_off=0, src_fm=p.A, src_off=XN, w_sb=p.W, w_off=0,
                 n_in=D, n_out=D, bias_off=0, scale=0.125, wait_dw=p.dW.n)
    emit_transpose(p, dst=p.A, dst_off=0, src=p.B, src_off=0)

    @p.raw_section("gpsimd")
    def _(g):
        g.wait_ge(p.T.h, p.T.n)
        i = g.dma_start(out=p.W[0:128, 0:8], in_=rho[:])
        p.dW.inc(i, 16)
    rel_dw = p.dW.n

    REL = 16384

    @p.raw_section("tensor")
    def _(t):
        p.wait_all(t)
        t.wait_ge(p.dW.h, rel_dw)
        ev0 = p.sEV.n
        job = 0
        for ti in range(NT):
            for h in range(16):
                ch, hin = (h * 64) // 128, (h * 64) % 128
                if job >= 2:
                    t.wait_ge(p.sEV.h, ev0 + job - 1)
                i = t.matmul(p.psA[0:128, 0:8],
                             p.A[hin:hin + 64, ch * 2048 + ti * 128: ch * 2048 + ti * 128 + 128],
                             p.W[hin:hin + 64, 0:8], start=True, stop=True)
                p.sMM.inc(i)
                job += 1

    @p.raw_section("vector")
    def _(v):
        base = p.sMM.n - NT * 16
        job = 0
        for ti in range(NT):
            for h in range(16):
                v.wait_ge(p.sMM.h, base + job + 1)
                v.tensor_copy(p.A[:, REL + ti * 128 + h * 8: REL + ti * 128 + h * 8 + 8],
                              p.psA[:, 0:8])
                p.dinc(v, p.sEV)
                job += 1
        p.T.inc(v.drain())

    for ti in range(NT):
        @p.raw_section("gpsimd")
        def _(g, ti=ti):
            if ti == 0:
                p.wait_all(g)
                g.wait_ge(p.sCC.h, p.sCC.n)
            else:
                g.wait_ge(p.sEV.h, p.sEV.n)
            for s in range(NCORES):
                i = g.dma_start(out=p.W[:, s * 1024:(s + 1) * 1024],
                                in_=kv_out[(s * 2) * T + ti * 128:(s * 2) * T + ti * 128 + 128, :])
                p.dW.inc(i, 16)

        @p.raw_section("vector")
        def _(v, ti=ti):
            v.wait_ge(p.dW.h, p.dW.n)
            for s in range(NCORES):
                v.tensor_tensor(out=p.E[:, 1024:2048], in0=p.B[:, ti * 1024:(ti + 1) * 1024],
                                in1=p.W[:, s * 1024:(s + 1) * 1024], op=OP.mult)
                v.tensor_reduce(out=p.E[:, 0:128].rearrange("p (h s) -> p h s", s=8)[:, :, s:s + 1].squeeze(2),
                                in_=p.E[:, 1024:2048].rearrange("p (h d) -> p h d", d=64),
                                axis=X, op=OP.add)
            v.drain()
            v.tensor_tensor(out=p.E[:, 0:128], in0=p.E[:, 0:128],
                            in1=p.A[:, REL + ti * 128:REL + (ti + 1) * 128], op=OP.add)
            sview = p.E[:, 0:128].rearrange("p (h s) -> p h s", s=8)
            m = p.ST[:, 384:400]
            v.drain()
            v.tensor_reduce(out=m, in_=sview, axis=X, op=OP.max)
            v.drain()
            v.tensor_tensor(out=sview, in0=sview, in1=m.unsqueeze(2).broadcast_to([128, 16, 8]),
                            op=OP.subtract)
            p.dinc(v, p.sEV)

        @p.raw_section("scalar")
        def _(s_, ti=ti):
            s_.wait_ge(p.sEV.h, p.sEV.n)
            s_.activation(out=p.E[:, 0:128], in_=p.E[:, 0:128], func=AF.Exp)
            p.dinc(s_, p.sAC)

        @p.raw_section("gpsimd")
        def _(g, ti=ti):
            g.wait_ge(p.sAC.h, p.sAC.n)
            for s in range(NCORES):
                i = g.dma_start(out=p.W[:, s * 1024:(s + 1) * 1024],
                                in_=kv_out[(s * 2 + 1) * T + ti * 128:(s * 2 + 1) * T + ti * 128 + 128, :])
                p.dW.inc(i, 16)

        @p.raw_section("vector")
        def _(v, ti=ti):
            v.wait_ge(p.sAC.h, p.sAC.n)
            sview = p.E[:, 0:128].rearrange("p (h s) -> p h s", s=8)
            sm = p.ST[:, 400:416]
            v.tensor_reduce(out=sm, in_=sview, axis=X, op=OP.add)
            v.drain()
            v.reciprocal(sm, sm)
            v.drain()
            v.tensor_tensor(out=sview, in0=sview, in1=sm.unsqueeze(2).broadcast_to([128, 16, 8]),
                            op=OP.mult)
            v.drain()
            v.wait_ge(p.dW.h, p.dW.n)
            for s in range(NCORES):
                pb = p.E[:, 0:128].rearrange("p (h s) -> p h s", s=8)[:, :, s:s + 1] \
                    .broadcast_to([128, 16, 64])
                vv = p.W[:, s * 1024:(s + 1) * 1024].rearrange("p (h d) -> p h d", d=64)
                if s == 0:
                    v.tensor_tensor(out=p.E[:, 2048:3072].rearrange("p (h d) -> p h d", d=64),
                                    in0=vv, in1=pb, op=OP.mult)
                else:
                    v.tensor_tensor(out=p.E[:, 1024:2048].rearrange("p (h d) -> p h d", d=64),
                                    in0=vv, in1=pb, op=OP.mult)
                    v.tensor_add(p.E[:, 2048:3072], p.E[:, 2048:3072], p.E[:, 1024:2048])
            v.drain()
            v.tensor_copy(p.A[:, ti * 1024:(ti + 1) * 1024], p.E[:, 2048:3072])
            p.dinc(v, p.sEV)
            p.T.inc(v.drain())

    emit_transpose(p, dst=p.A, dst_off=XN, src=p.A, src_off=0)
    emit_weight_load(p, outWT, p.W, 0, D, D)
    emit_row_load(p, outb, 0, D)
    emit_proj_residual(p, src_fm=(p.A, XN), w=(p.W, 0), n_in=D, n_out=D,
                       wait_dw=p.dW.n, bias_off=0)


def emit_quantize(p, wT_dram, n_in, n_out, q_dram, onescol, eps=1e-5):
    nch = n_in // 128
    dbl = (2 * n_out <= 4096)
    acc = p.ST[:, 416:417]
    tmp = p.ST[:, 417:418]
    thr = p.ST[:, 418:419]
    thr_d = p.dint("q_thr", [1, 1], F32)

    @p.section("vector")
    def _(v):
        v.memset(acc, 0.0)
        return v.drain()

    for c in range(nch):
        bo = (c % 2) * n_out if dbl else 0

        @p.raw_section("gpsimd")
        def _(g, c=c, bo=bo):
            if c == 0:
                g.wait_ge(p.T.h, p.T.n)
            else:
                g.wait_ge(p.sEV.h, p.sEV.n - (1 if (dbl and c >= 2) else 0))
            i = g.dma_start(out=p.E[:, bo:bo + n_out],
                            in_=wT_dram[c * 128:(c + 1) * 128, :])
            p.dW.inc(i, 16)

        @p.raw_section("vector")
        def _(v, c=c, bo=bo):
            v.wait_ge(p.dW.h, p.dW.n)
            v.tensor_reduce(out=tmp, in_=p.E[:, bo:bo + n_out],
                            axis=X, op=OP.add, apply_absolute_value=True)
            v.drain()
            v.tensor_add(acc, acc, tmp)
            p.dinc(v, p.sEV)

    @p.raw_section("tensor")
    def _(t):
        t.wait_ge(p.sEV.h, p.sEV.n)
        i = t.matmul(p.psA[0:1, 0:1], acc, onescol[:], start=True, stop=True)
        p.sMM.inc(i)

    @p.raw_section("vector")
    def _(v):
        v.wait_ge(p.sMM.h, p.sMM.n)
        v.tensor_scalar(out=thr[0:1], in0=p.psA[0:1, 0:1],
                        scalar1=0.5 / (n_in * n_out), scalar2=0.5 * eps,
                        op0=OP.mult, op1=OP.add)
        p.dinc(v, p.sEV)

    @p.raw_section("gpsimd")
    def _(g):
        g.wait_ge(p.sEV.h, p.sEV.n)
        i = g.dma_start(out=thr_d[:], in_=thr[0:1])
        p.dW.inc(i, 16)
        g.wait_ge(p.dW.h, p.dW.n)
        i = g.dma_start(out=thr, in_=thr_d.broadcast_to([128, 1]))
        p.dW.inc(i, 16)

    @p.section("vector")
    def _(v):
        v.wait_ge(p.dW.h, p.dW.n)
        return v.drain()

    for c in range(nch):
        bo = (c % 2) * n_out if dbl else 0

        @p.raw_section("gpsimd")
        def _(g, c=c, bo=bo):
            if c == 0:
                g.wait_ge(p.T.h, p.T.n)
            else:
                g.wait_ge(p.sEV.h, p.sEV.n - (1 if (dbl and c >= 2) else 0))
            i = g.dma_start(out=p.E[:, bo:bo + n_out],
                            in_=wT_dram[c * 128:(c + 1) * 128, :])
            p.dW.inc(i, 16)

        @p.raw_section("vector")
        def _(v, c=c, bo=bo):
            v.wait_ge(p.dW.h, p.dW.n)
            e = p.E[:, bo:bo + n_out]
            v.tensor_single_scalar(out=p.B[:, bo:bo + n_out], in_=e, scalar=thr, op=OP.is_ge)
            v.tensor_scalar(out=p.B[:, 8192 + bo:8192 + bo + n_out], in0=e,
                            scalar1=-1.0, scalar2=thr, op0=OP.mult, op1=OP.is_ge)
            v.drain()
            v.tensor_sub(p.B[:, bo:bo + n_out], p.B[:, bo:bo + n_out],
                         p.B[:, 8192 + bo:8192 + bo + n_out])
            p.dinc(v, p.sEV)

        @p.raw_section("gpsimd")
        def _(g, c=c, bo=bo):
            g.wait_ge(p.sEV.h, p.sEV.n)
            i = g.dma_start(out=q_dram[c * 128:(c + 1) * 128, :],
                            in_=p.B[:, bo:bo + n_out])
            p.dW.inc(i, 16)

    @p.section("vector")
    def _(v):
        return v.drain()


def emit_l1_ffn(p):
    W1T = p.din("L1_ffn_W1T", [D, 4096])
    b1 = p.din("L1_ffn_b1", [4096])
    W2T = p.din("L1_ffn_W2T", [4096, D])
    b2 = p.din("L1_ffn_b2", [D])
    onescol = p.din("c_ones_col", [128, 1])
    w1q = p.dint("l1_w1q", [D, 4096], BF16)
    w2q = p.dint("l1_w2q", [4096, D], BF16)
    oc = p.sb("onescol_sb", [128, 1], F32)

    @p.raw_section("gpsimd")
    def _(g):
        g.wait_ge(p.T.h, p.T.n)
        i = g.dma_start(out=oc[:], in_=onescol[:])
        p.dW.inc(i, 16)

    @p.section("vector")
    def _(v):
        v.wait_ge(p.dW.h, p.dW.n)
        return v.drain()

    emit_quantize(p, W1T, D, 4096, w1q, oc)
    emit_quantize(p, W2T, 4096, D, w2q, oc)

    XN = 16384
    emit_ln(p, dst=p.A, dst_off=0)
    emit_transpose(p, dst=p.A, dst_off=XN, src=p.A, src_off=0)
    emit_moe_expert(p, None, w1q, b1, w2q, b2, xn_off=XN, cw_col=None,
                    act=AF.Relu, add_xf=False)


def emit_l2_attn(p):
    Wq = p.din("L2_attn_Wq", [D, D]);  bq = p.din("L2_attn_bq", [D])
    Wk = p.din("L2_attn_Wk", [D, D]);  bk = p.din("L2_attn_bk", [D])
    Wv = p.din("L2_attn_Wv", [D, D]);  bv = p.din("L2_attn_bv", [D])
    Wo = p.din("L2_attn_Wo", [D, D]);  bo = p.din("L2_attn_bo", [D])
    ds2 = p.din("L2_attn_dist_scale2d", [1, 1])
    W1 = p.din("L2_attn_W1", [D, 4096]); b1 = p.din("L2_attn_b1", [4096])
    W2 = p.din("L2_attn_W2", [4096, D]); b2 = p.din("L2_attn_b2", [D])
    g1 = p.din("L2_attn_ln1_g", [D]); be1 = p.din("L2_attn_ln1_b", [D])
    g2 = p.din("L2_attn_ln2_g", [D]); be2 = p.din("L2_attn_ln2_b", [D])
    cdist = p.din("c_dist", [28, 128, 512])
    pen_d = p.dint("l2_pen", [28, 128, 512], BF16)
    dsb = p.ST[:, 419:420]

    XN = 16384
    @p.raw_section("gpsimd")
    def _(g):
        g.wait_ge(p.T.h, p.T.n)
        i = g.dma_start(out=dsb, in_=ds2.broadcast_to([128, 1]))
        p.dW.inc(i, 16)

    @p.section("vector")
    def _(v):
        v.wait_ge(p.dW.h, p.dW.n)
        v.tensor_scalar_mul(dsb, dsb, -0.125)
        return v.drain()

    for r in range(28):
        @p.raw_section("gpsimd")
        def _(g, r=r):
            if r == 0:
                g.wait_ge(p.T.h, p.T.n)
            else:
                g.wait_ge(p.sEV.h, p.sEV.n)
            i = g.dma_start(out=p.E[:, 0:512], in_=cdist[r])
            p.dW.inc(i, 16)

        @p.raw_section("vector")
        def _(v, r=r):
            v.wait_ge(p.dW.h, p.dW.n)
            v.tensor_scalar_mul(p.B[:, 0:512], p.E[:, 0:512], dsb)
            p.dinc(v, p.sEV)

        @p.raw_section("gpsimd")
        def _(g, r=r):
            g.wait_ge(p.sEV.h, p.sEV.n)
            i = g.dma_start(out=pen_d[r], in_=p.B[:, 0:512])
            p.dW.inc(i, 16)

    emit_ln(p, dst=p.A, dst_off=0)
    emit_transpose(p, dst=p.A, dst_off=XN, src=p.A, src_off=0)

    for e8 in range(8):
        dsl = e8 * 128
        emit_weight_load(p, Wq, p.W, 0, D, 128, out_off=dsl)
        emit_weight_load(p, Wk, p.W, 1024, D, 128, out_off=dsl)
        emit_weight_load(p, Wv, p.W, 2048, D, 128, out_off=dsl)
        emit_row_load(p, bq, 0, 128, src_off=dsl)
        emit_row_load(p, bk, 256, 128, src_off=dsl)
        emit_row_load(p, bv, 512, 128, src_off=dsl)
        dw = p.dW.n
        emit_proj_tm(p, dst=p.B, dst_off=0, src_fm=p.A, src_off=XN, w_sb=p.W, w_off=0,
                     n_in=D, n_out=128, bias_off=0, scale=0.125, wait_dw=dw,
                     dst_stride=128)
        emit_transpose(p, dst=p.B, dst_off=2048, src=p.B, src_off=0, width=128)
        emit_proj_tm(p, dst=p.B, dst_off=4096, src_fm=p.A, src_off=XN, w_sb=p.W,
                     w_off=1024, n_in=D, n_out=128, bias_off=256, wait_dw=dw,
                     dst_stride=128)
        emit_transpose(p, dst=p.B, dst_off=6144, src=p.B, src_off=4096, width=128)
        emit_proj_tm(p, dst=p.B, dst_off=8192, src_fm=p.A, src_off=XN, w_sb=p.W,
                     w_off=2048, n_in=D, n_out=128, bias_off=512, wait_dw=dw,
                     dst_stride=128)
        emit_l2_attn_8(p, e8, pen_d)
        emit_transpose(p, dst=p.B, dst_off=6144, src=p.B, src_off=10240, width=128)
        emit_weight_load(p, Wo, p.W, 3072, 128, D, in_off=dsl)
        if e8 == 7:
            emit_row_load(p, bo, 1024, D)
        emit_proj_residual(p, src_fm=(p.B, 6144), w=(p.W, 3072), n_in=128, n_out=D,
                           wait_dw=p.dW.n, dst=(p.A, 0, D),
                           bias_off=(1024 if e8 == 7 else None))

    @p.raw_section("gpsimd")
    def _(g):
        g.wait_ge(p.T.h, p.T.n)
        i = g.dma_start(out=p.W[:, 0:1024], in_=g1.unsqueeze(0).broadcast_to([128, D]))
        p.dW.inc(i, 16)
        i = g.dma_start(out=p.W[:, 1024:2048], in_=be1.unsqueeze(0).broadcast_to([128, D]))
        p.dW.inc(i, 16)

    @p.section("vector")
    def _(v):
        v.wait_ge(p.dW.h, p.dW.n)
        return v.drain()

    emit_ln(p, dst=p.B, dst_off=0, src=p.A, src_off=0,
            affine=(p.W[:, 0:1024], p.W[:, 1024:2048]))
    emit_transpose(p, dst=p.A, dst_off=XN, src=p.B, src_off=0)
    emit_moe_expert(p, None, W1, b1, W2, b2, xn_off=XN, cw_col=None, act=AF.Relu,
                    add_xf=False, xn_buf=p.A, h_buf=p.A, h_off=0, out_dst=(p.B, 0))
    @p.raw_section("gpsimd")
    def _(g):
        g.wait_ge(p.T.h, p.T.n)
        i = g.dma_start(out=p.W[:, 0:1024], in_=g2.unsqueeze(0).broadcast_to([128, D]))
        p.dW.inc(i, 16)
        i = g.dma_start(out=p.W[:, 1024:2048], in_=be2.unsqueeze(0).broadcast_to([128, D]))
        p.dW.inc(i, 16)

    @p.section("vector")
    def _(v):
        v.wait_ge(p.dW.h, p.dW.n)
        return v.drain()

    emit_ln(p, dst=p.A, dst_off=0, src=p.B, src_off=0,
            affine=(p.W[:, 0:1024], p.W[:, 1024:2048]))

    @p.section("vector")
    def _(v):
        last = None
        for ti in range(NT):
            last = v.tensor_add(p.xres[:, ti * D:(ti + 1) * D],
                                p.xres[:, ti * D:(ti + 1) * D],
                                p.A[:, ti * 1024:(ti + 1) * 1024])
        return last


def emit_l2_attn_8(p, e8, pen_d):
    QF, KF, VT, CO, P_off, PT_off, PEN = 2048, 6144, 8192, 10240, 12288, 14336, 4096
    nmax = p.ST[:, 368:369]
    sums = p.ST[:, 369:370]
    rinv = p.ST[:, 373:374]
    NIT = 32
    mm0 = p.sMM.n
    ev0 = p.sEV.n
    ac0 = p.sAC.n
    dw0 = p.dW.n

    @p.raw_section("gpsimd")
    def _(g):
        for qi in range(NT):
            if qi == 0:
                p.wait_all(g)
            else:
                g.wait_ge(p.sAC.h, ac0 + 2 * qi)
            for kc in range(4):
                r = (qi * 128 - kc * 512 + 1536) // 128
                i = g.dma_start(out=p.W[:, PEN + kc * 512:PEN + (kc + 1) * 512],
                                in_=pen_d[r])
                i.then_inc(p.dW.h, 16)
    p.dW.n += 16 * 4 * NT

    @p.raw_section("tensor")
    def _(t):
        p.wait_all(t)
        for k in range(NIT):
            qi, h = k // 2, k % 2
            hglob = e8 * 2 + h
            hin = h * 64
            t.wait_ge(p.dW.h, dw0 + 64 * (qi + 1))
            if k > 0:
                t.wait_ge(p.sAC.h, ac0 + k)
            for b in range(4):
                t.matmul(p.psS[:, b * 512:(b + 1) * 512],
                         p.B[hin:hin + 64, QF + qi * 128:QF + qi * 128 + 128],
                         p.B[hin:hin + 64, KF + b * 512:KF + (b + 1) * 512],
                         start=True, stop=False)
            for b in range(4):
                i = t.matmul(p.psS[:, b * 512:(b + 1) * 512], p.ident[:],
                             p.W[:, PEN + b * 512:PEN + (b + 1) * 512],
                             start=False, stop=True)
            i.then_inc(p.sMM.h, 1)
            t.wait_ge(p.sAC.h, ac0 + k + 1)
            if k > 0:
                t.wait_ge(p.sEV.h, ev0 + 5 * k - 1)
            for kb in range(8):
                i = t.transpose(p.psT[:, kb * 128:(kb + 1) * 128],
                                p.B[:, P_off + kb * 128:P_off + kb * 128 + 128],
                                p.ident[:])
            i.then_inc(p.sMM.h, 1)
            t.wait_ge(p.sEV.h, ev0 + 5 * k + 3)
            for kb in range(8):
                i = t.transpose(p.psT[:, kb * 128:(kb + 1) * 128],
                                p.B[:, P_off + (8 + kb) * 128:P_off + (8 + kb) * 128 + 128],
                                p.ident[:])
            i.then_inc(p.sMM.h, 1)
            t.wait_ge(p.sEV.h, ev0 + 5 * k + 4)
            for kb in range(16):
                i = t.matmul(p.psA[:, 0:64],
                             p.B[:, PT_off + kb * 128:PT_off + kb * 128 + 128],
                             p.B[:, VT + kb * 128 + hin:VT + kb * 128 + hin + 64],
                             start=(kb == 0), stop=(kb == 15))
            i.then_inc(p.sMM.h, 1)
    p.sMM.n += 4 * NIT

    @p.raw_section("vector")
    def _(v):
        for k in range(NIT):
            qi, h = k // 2, k % 2
            hglob = e8 * 2 + h
            hin = h * 64
            v.wait_ge(p.sMM.h, mm0 + 4 * k + 1)
            if hglob > 0:
                v.tensor_scalar_add(p.psS[:, 0:hglob], p.psS[:, 0:hglob], NEG)
                v.drain()
            v.reduce_max(out=nmax, in_=p.psS[:, 0:2048], axis=X, negate=True)
            i = v.drain()
            i.then_inc(p.sEV.h, 1)
            v.wait_ge(p.sAC.h, ac0 + k + 1)
            i = v.reciprocal(rinv, sums)
            i.then_inc(p.sEV.h, 1)
            v.wait_ge(p.sMM.h, mm0 + 4 * k + 2)
            i = v.tensor_copy(p.B[:, PT_off:PT_off + 1024], p.psT[:, 0:1024])
            i.then_inc(p.sEV.h, 1)
            v.wait_ge(p.sMM.h, mm0 + 4 * k + 3)
            v.tensor_copy(p.B[:, PT_off + 1024:PT_off + 2048], p.psT[:, 0:1024])
            i = v.drain()
            i.then_inc(p.sEV.h, 1)
            v.wait_ge(p.sMM.h, mm0 + 4 * k + 4)
            i = v.tensor_scalar_mul(p.B[:, CO + qi * 128 + hin:CO + qi * 128 + hin + 64],
                                    p.psA[:, 0:64], rinv)
            i.then_inc(p.sEV.h, 1)
        i = v.drain()
        p.T.inc(i)
    p.sEV.n += 5 * NIT

    @p.raw_section("scalar")
    def _(s):
        for k in range(NIT):
            s.wait_ge(p.sEV.h, ev0 + 5 * k + 1)
            if k > 0:
                s.wait_ge(p.sMM.h, mm0 + 4 * (k - 1) + 3)
            s.activation(out=p.B[:, P_off:P_off + 2048], in_=p.psS[:, 0:2048],
                         func=AF.Exp, bias=nmax, accum_out=sums)
            i = s.drain()
            i.then_inc(p.sAC.h, 1)
    p.sAC.n += NIT


def emit_l2_ffn(p):
    Wg = p.din("L2_ffn_Wg", [D, 8])
    gP = p.din("L2_ffn_gammaP", [128, 64])
    bP = p.din("L2_ffn_betaP", [128, 64])
    W1 = p.din("L2_ffn_W1", [8, D, 4096])
    b1 = p.din("L2_ffn_b1", [8, 4096])
    W2 = p.din("L2_ffn_W2", [8, 4096, D])
    b2 = p.din("L2_ffn_b2", [8, D])

    XN = 16384
    emit_ln(p, dst=p.A, dst_off=0)
    emit_transpose(p, dst=p.A, dst_off=XN, src=p.A, src_off=0)

    @p.raw_section("gpsimd")
    def _(g):
        g.wait_ge(p.T.h, p.T.n)
        i = g.dma_start(out=p.CS[:, 160:224], in_=gP[:])
        p.dW.inc(i, 16)
        i = g.dma_start(out=p.CS[:, 224:288], in_=bP[:])
        p.dW.inc(i, 16)

    emit_weight_load(p, Wg, p.W, 0, D, 8)
    emit_proj_tm(p, dst=p.E, dst_off=0, src_fm=p.A, src_off=XN, w_sb=p.W, w_off=0,
                 n_in=D, n_out=8, wait_dw=p.dW.n, evac="vector", dst_stride=8)

    m1 = p.ST[:, 384:400]; m2 = p.ST[:, 400:416]
    e2c = p.ST[:, 420:436]; w1n = p.ST[:, 436:452]; w2n = p.ST[:, 452:468]

    @p.section("vector")
    def _(v):
        lg = p.E[:, 0:128].rearrange("p (t e) -> p t e", e=8)
        eq1 = p.E[:, 128:256].rearrange("p (t e) -> p t e", e=8)
        t2 = p.E[:, 256:384].rearrange("p (t e) -> p t e", e=8)
        v.tensor_reduce(out=m1, in_=lg, axis=X, op=OP.max)
        v.drain()
        v.tensor_tensor(out=eq1, in0=lg, in1=m1.unsqueeze(2).broadcast_to([128, 16, 8]),
                        op=OP.is_equal)
        v.drain()
        v.scalar_tensor_tensor(out=t2, in0=eq1, scalar=NEG, in1=lg, op0=OP.mult, op1=OP.add)
        v.drain()
        v.tensor_reduce(out=m2, in_=t2, axis=X, op=OP.max)
        v.drain()
        v.tensor_sub(e2c, m2, m1)
        return v.drain()

    @p.section("scalar")
    def _(s):
        return s.activation(out=e2c, in_=e2c, func=AF.Exp)

    @p.section("vector")
    def _(v):
        lg = p.E[:, 0:128].rearrange("p (t e) -> p t e", e=8)
        eq1 = p.E[:, 128:256].rearrange("p (t e) -> p t e", e=8)
        t2 = p.E[:, 256:384].rearrange("p (t e) -> p t e", e=8)
        eq2 = p.E[:, 384:512].rearrange("p (t e) -> p t e", e=8)
        cw = p.ST[:, 256:384].rearrange("p (t e) -> p t e", e=8)
        v.tensor_scalar_add(w1n, e2c, 1.0)
        v.drain()
        v.reciprocal(w1n, w1n)
        v.drain()
        v.tensor_tensor(out=w2n, in0=e2c, in1=w1n, op=OP.mult)
        v.tensor_tensor(out=eq2, in0=t2, in1=m2.unsqueeze(2).broadcast_to([128, 16, 8]),
                        op=OP.is_equal)
        v.drain()
        v.tensor_tensor(out=cw, in0=eq1, in1=w1n.unsqueeze(2).broadcast_to([128, 16, 8]),
                        op=OP.mult)
        v.tensor_tensor(out=eq2, in0=eq2, in1=w2n.unsqueeze(2).broadcast_to([128, 16, 8]),
                        op=OP.mult)
        v.drain()
        v.tensor_add(cw, cw, eq2)
        return v.drain()

    @p.section("vector")
    def _(v):
        last = None
        for ti in range(NT):
            last = v.tensor_add(p.xres[:, ti * D:(ti + 1) * D],
                                p.xres[:, ti * D:(ti + 1) * D],
                                p.A[:, ti * 1024:(ti + 1) * 1024])
        return last

    emit_ln(p, dst=p.B, dst_off=0, src=p.A, src_off=0, eps=1e-6)
    emit_transpose(p, dst=p.A, dst_off=XN, src=p.B, src_off=0)

    CW = 256
    for e in range(8):
        emit_moe_expert(p, e, W1, b1, W2, b2, xn_off=XN,
                        cw_col=lambda ti, e=e: CW + ti * 8 + e,
                        act=AF.Relu, add_xf=False,
                        premul=(lambda dc, e=e: p.CS[:, 160 + e * 8 + dc:161 + e * 8 + dc],
                                lambda dc, e=e: p.CS[:, 224 + e * 8 + dc:225 + e * 8 + dc]),
                        premul_dst=(p.A, 0))


def emit_l3_attn(p):
    WqT = p.din("L3_attn_WqT", [D, D])
    WkT = p.din("L3_attn_WkT", [D, 256])
    WvT = p.din("L3_attn_WvT", [D, 256])
    WoT = p.din("L3_attn_WoT", [D, D])
    mb = p.din("c_l3_bandmask", [2, 128, 176], BF16)
    ms = p.din("c_l3_stridemask", [NT, 128, 44], BF16)
    v_d = p.dint("l3_v", [T, 256], BF16)

    XN = 16384
    Eb = p.E[:].bitcast(BF16)
    emit_ln(p, dst=p.A, dst_off=0)
    emit_transpose(p, dst=p.A, dst_off=XN, src=p.A, src_off=0)

    emit_weight_load(p, WkT, p.W, 0, D, 256)
    emit_weight_load(p, WvT, p.W, 2048, D, 256)
    dw = p.dW.n
    emit_proj_tm(p, dst=p.B, dst_off=0, src_fm=p.A, src_off=XN, w_sb=p.W, w_off=0,
                 n_in=D, n_out=256, wait_dw=dw, dst_stride=256)
    emit_proj_tm(p, dst=Eb, dst_off=4096, src_fm=p.A, src_off=XN, w_sb=p.W,
                 w_off=2048, n_in=D, n_out=256, dst_stride=256)
    emit_transpose(p, dst=p.B, dst_off=4096, src=p.B, src_off=0, width=256)
    emit_weight_load(p, WqT, p.W, 0, D, D)
    emit_proj_tm(p, dst=p.A, dst_off=0, src_fm=p.A, src_off=XN, w_sb=p.W, w_off=0,
                 n_in=D, n_out=D, scale=0.125, wait_dw=p.dW.n)
    @p.raw_section("gpsimd")
    def _(g):
        p.wait_all(g)
        for g4 in range(4):
            ch, base = g4 // 2, (g4 % 2) * 64
            i = g.dma_start(out=p.W[0:64, g4 * 2048:(g4 + 1) * 2048],
                            in_=p.B[base:base + 64, 4096 + ch * 2048:4096 + (ch + 1) * 2048])
            p.dW.inc(i, 16)
            i = g.dma_start(out=p.W[64:128, g4 * 2048:(g4 + 1) * 2048],
                            in_=p.B[base:base + 64, 4096 + ch * 2048:4096 + (ch + 1) * 2048])
            p.dW.inc(i, 16)
        i = g.dma_start(out=v_d.rearrange("(t p) d -> p t d", p=128),
                        in_=Eb[:, 4096:8192].rearrange("p (t d) -> p t d", d=256))
        p.dW.inc(i, 16)
        g.wait_ge(p.dW.h, p.dW.n)
        i = g.dma_start(out=Eb[0:44, 896:1152],
                        in_=v_d[45:2024:46, :])
        p.dW.inc(i, 16)
        i = g.dma_start(out=Eb[:, 1280:1456], in_=mb[0])
        p.dW.inc(i, 16)
        i = g.dma_start(out=Eb[:, 1456:1632], in_=mb[1])
        p.dW.inc(i, 16)

    @p.section("vector")
    def _(v):
        v.wait_ge(p.dW.h, p.dW.n)
        v.memset(Eb[:, 640:896], 0.0)
        return v.drain()

    emit_transpose(p, dst=p.B, dst_off=0, src=p.A, src_off=0)

    nmax = p.ST[:, 368:369]
    sums = p.ST[:, 369:370]
    rinv = p.ST[:, 373:374]

    for qi in range(NT):
        @p.raw_section("gpsimd")
        def _(g, qi=qi):
            if qi == 0:
                p.wait_all(g)
            else:
                g.wait_ge(p.sAC.h, p.sAC.n)
            i = g.dma_start(out=Eb[:, 1664:1708], in_=ms[qi])
            p.dW.inc(i, 16)
            if qi > 0:
                i = g.dma_start(out=Eb[0:48, 640:896],
                                in_=Eb[80:128, 4096 + (qi - 1) * 256:4096 + qi * 256])
                p.dW.inc(i, 16)

        for h in range(16):
            g4 = h // 4
            qch, qbase = h // 2, (h % 2) * 64
            kbase = qbase

            @p.raw_section("tensor")
            def _(t, qi=qi, h=h, g4=g4, qch=qch, qbase=qbase):
                if qi == 0 and h == 0:
                    p.wait_all(t)
                t.wait_ge(p.dW.h, p.dW.n)
                if not (qi == 0 and h == 0):
                    t.wait_ge(p.sAC.h, p.sAC.n)
                qf = p.B[qbase:qbase + 64,
                         qch * 2048 + qi * 128:qch * 2048 + qi * 128 + 128]
                ka = max(0, qi * 128 - 48)
                t.matmul(p.psS[:, 0:48],
                         qf, p.W[qbase:qbase + 64, g4 * 2048 + ka:g4 * 2048 + ka + 48],
                         start=True, stop=True)
                t.matmul(p.psS[:, 48:176],
                         qf, p.W[qbase:qbase + 64, g4 * 2048 + qi * 128:g4 * 2048 + qi * 128 + 128],
                         start=True, stop=True)
                t.matmul(p.psS[:, 176:220],
                         qf, p.W[qbase:qbase + 64, g4 * 2048 + 45:g4 * 2048 + 2024:46],
                         start=True, stop=True)
                mb_off = 1280 if qi == 0 else 1456
                t.matmul(p.psS[:, 0:176], p.ident[:], Eb[:, mb_off:mb_off + 176],
                         start=False, stop=True)
                last = t.matmul(p.psS[:, 176:220], p.ident[:], Eb[:, 1664:1708],
                                start=False, stop=True)
                p.sMM.inc(last)

            @p.raw_section("vector")
            def _(v):
                v.wait_ge(p.sMM.h, p.sMM.n)
                v.reduce_max(out=nmax, in_=p.psS[:, 0:220], axis=X, negate=True)
                p.dinc(v, p.sEV)

            @p.raw_section("scalar")
            def _(s):
                s.wait_ge(p.sEV.h, p.sEV.n)
                s.activation(out=Eb[:, 0:220], in_=p.psS[:, 0:220], func=AF.Exp,
                             bias=nmax, accum_out=sums)
                p.dinc(s, p.sAC)

            @p.raw_section("vector")
            def _(v):
                v.wait_ge(p.sAC.h, p.sAC.n)
                v.reciprocal(rinv, sums)
                p.dinc(v, p.sEV)

            @p.raw_section("tensor")
            def _(t):
                t.wait_ge(p.sAC.h, p.sAC.n)
                t.transpose(p.psT[0:48, 0:128], Eb[:, 0:48], p.ident[:])
                t.transpose(p.psT[:, 128:256], Eb[:, 48:176], p.ident[:])
                i = t.transpose(p.psT[0:44, 256:384], Eb[:, 176:220], p.ident[:])
                p.sMM.inc(i)

            @p.raw_section("vector")
            def _(v):
                v.wait_ge(p.sMM.h, p.sMM.n)
                v.tensor_copy(Eb[0:48, 256:384], p.psT[0:48, 0:128])
                v.tensor_copy(Eb[:, 384:512], p.psT[:, 128:256])
                v.tensor_copy(Eb[0:44, 512:640], p.psT[0:44, 256:384])
                p.dinc(v, p.sEV)

            @p.raw_section("tensor")
            def _(t, qi=qi, g4=g4):
                t.wait_ge(p.sEV.h, p.sEV.n)
                vd = g4 * 64
                t.matmul(p.psA[:, 0:64], Eb[0:48, 256:384],
                         Eb[0:48, 640 + vd:640 + vd + 64], start=True, stop=False)
                t.matmul(p.psA[:, 0:64], Eb[:, 384:512],
                         Eb[:, 4096 + qi * 256 + vd:4096 + qi * 256 + vd + 64],
                         start=False, stop=False)
                i = t.matmul(p.psA[:, 0:64], Eb[0:44, 512:640],
                             Eb[0:44, 896 + vd:896 + vd + 64], start=False, stop=True)
                p.sMM.inc(i)

            @p.raw_section("vector")
            def _(v, qi=qi, h=h):
                v.wait_ge(p.sMM.h, p.sMM.n)
                v.tensor_scalar_mul(p.A[:, qi * 1024 + h * 64:qi * 1024 + h * 64 + 64],
                                    p.psA[:, 0:64], rinv)
                v.drain()
                p.dinc(v, p.sEV)
                p.T.inc(v.drain())

    emit_transpose(p, dst=p.A, dst_off=XN, src=p.A, src_off=0)
    emit_weight_load(p, WoT, p.W, 0, D, D)
    emit_proj_residual(p, src_fm=(p.A, XN), w=(p.W, 0), n_in=D, n_out=D,
                       wait_dw=p.dW.n)


def emit_l3_ffn(p):
    w1 = p.din("L3_ffn_w1", [8, D, 4096])
    b1 = p.din("L3_ffn_b1", [8, 4096])
    w2 = p.din("L3_ffn_w2", [8, 4096, D])
    b2 = p.din("L3_ffn_b2", [8, D])
    gw = p.din("L3_ffn_gate_w", [D, 8])
    gb = p.din("L3_ffn_gate_b", [8])

    XN = 16384
    emit_ln(p, dst=p.A, dst_off=0)
    emit_transpose(p, dst=p.A, dst_off=XN, src=p.A, src_off=0)

    emit_weight_load(p, gw, p.W, 0, D, 8)
    emit_row_load(p, gb, 0, 8)
    emit_proj_tm(p, dst=p.E, dst_off=0, src_fm=p.A, src_off=XN, w_sb=p.W, w_off=0,
                 n_in=D, n_out=8, bias_off=0, wait_dw=p.dW.n, evac="vector",
                 dst_stride=8)

    m1 = p.ST[:, 384:400]
    CW = 256

    @p.section("vector")
    def _(v):
        lg = p.E[:, 0:128].rearrange("p (t e) -> p t e", e=8)
        cw = p.ST[:, 256:384].rearrange("p (t e) -> p t e", e=8)
        v.tensor_reduce(out=m1, in_=lg, axis=X, op=OP.max)
        v.drain()
        v.tensor_tensor(out=cw, in0=lg, in1=m1.unsqueeze(2).broadcast_to([128, 16, 8]),
                        op=OP.is_equal)
        return v.drain()

    for e in range(8):
        emit_moe_expert(p, e, w1, b1, w2, b2, xn_off=XN,
                        cw_col=lambda ti, e=e: CW + ti * 8 + e,
                        act=AF.Relu, add_xf=False)


def stage_host(inputs):
    H = dict(host_constants())
    x = np.ascontiguousarray(np.asarray(inputs["L0_attn_x"], dtype=np.float32))
    H["xin"] = [np.ascontiguousarray(x[c]) for c in range(NCORES)]
    for k, v in inputs.items():
        if k != "L0_attn_x":
            H[k] = np.ascontiguousarray(np.asarray(v, dtype=np.float32))
    H["L0_ffn_router_wT"] = np.ascontiguousarray(H["L0_ffn_router_w"].T)
    H["c_ident_f32"] = np.eye(128, dtype=np.float32)
    H["c_ones_col"] = np.ones((128, 1), dtype=np.float32)
    H["L1_attn_in_proj_weightT"] = np.ascontiguousarray(H["L1_attn_in_proj_weight"].T)
    H["L1_attn_out_wT"] = np.ascontiguousarray(H["L1_attn_out_w"].T)
    H["L1_ffn_W1T"] = np.ascontiguousarray(H["L1_ffn_W1"].T)
    H["L1_ffn_W2T"] = np.ascontiguousarray(H["L1_ffn_W2"].T)
    rpk = H["L1_attn_rel_pos_keys"]
    H["rho_c"] = [np.ascontiguousarray(np.vstack([rpk[np.arange(8) - c + 16, :64].T]*2)) for c in range(NCORES)]
    H["L2_attn_dist_scale2d"] = np.asarray(H["L2_attn_dist_scale"], dtype=np.float32).reshape(1, 1)
    a128 = np.arange(128); b512 = np.arange(512)
    dist = np.empty((28, 128, 512), dtype=np.float32)
    for r in range(28):
        dist[r] = np.abs((r * 128 - 1536) + a128[:, None] - b512[None, :]).astype(np.float32)
    H["c_dist"] = dist
    H["L2_ffn_gammaP"] = np.ascontiguousarray(
        H["L2_ffn_gamma"].reshape(8, 8, 128).transpose(2, 0, 1).reshape(128, 64))
    H["L2_ffn_betaP"] = np.ascontiguousarray(
        H["L2_ffn_beta"].reshape(8, 8, 128).transpose(2, 0, 1).reshape(128, 64))
    H["L3_attn_WqT"] = np.ascontiguousarray(H["L3_attn_Wq"].T)
    H["L3_attn_WkT"] = np.ascontiguousarray(H["L3_attn_Wk"].T)
    H["L3_attn_WvT"] = np.ascontiguousarray(H["L3_attn_Wv"].T)
    H["L3_attn_WoT"] = np.ascontiguousarray(H["L3_attn_Wo"].T)
    a = np.arange(128)[:, None]
    j = np.arange(176)[None, :]
    rel = j - 48 - a
    base_valid = (rel >= -45) & (rel <= 0)
    m1v = np.where(base_valid, 0.0, NEG).astype(np.float32)
    m0v = np.where(base_valid & (j >= 48), 0.0, NEG).astype(np.float32)
    H["c_l3_bandmask"] = np.stack([m0v, m1v]).astype(BF)
    sidx = np.arange(45, 2048, 46)
    sm = np.empty((NT, 128, 44), dtype=np.float32)
    for qi in range(NT):
        q = qi * 128 + np.arange(128)[:, None]
        sm[qi] = np.where(sidx[None, :] <= q - 46, 0.0, NEG)
    H["c_l3_stridemask"] = sm.astype(BF)
    return H


def run_launch(p, H):
    in_maps = []
    for c in range(NCORES):
        m = {}
        for name in p.inputs:
            v = H[name]
            m[name] = v[c] if isinstance(v, list) else v
        in_maps.append(m)
    res = run_bass_kernel_spmd(p.nc, in_maps, core_ids=list(range(NCORES)))
    return res.results


LAYER_EMITTERS = [emit_l0_attn, emit_l0_ffn, emit_l1_attn, emit_l1_ffn,
                  emit_l2_attn, emit_l2_ffn, emit_l3_attn, emit_l3_ffn]


def build_layer_program(emitters):
    p = Prog()
    p.setup()
    emit_load_consts(p)
    emit_load_x(p)
    for em in emitters:
        em(p)
    emit_store_x(p)
    p.ctx.close()
    return p


def kernel(**inputs):
    H = stage_host(inputs)
    p = build_layer_program(LAYER_EMITTERS)
    results = run_launch(p, H)
    return np.stack([results[c]["xout"] for c in range(NCORES)], axis=0).astype(np.float32)
